# version 07
# baseline (speedup 1.0000x reference)
import numpy as np
import ml_dtypes
import concourse.bass as bass
import concourse.mybir as mybir
from concourse.bass_utils import run_bass_kernel_spmd

F32 = mybir.dt.float32
BF16 = mybir.dt.bfloat16
AF = mybir.ActivationFunctionType
ALU = mybir.AluOpType

NT = 16
D = 4096
KC = 32
NOC = 8
RT = 8
NH = 8
HD = 128
EPS_LN = 1e-5
EPS_ATT = 1e-6

STREAMS = ("t", "v", "s", "g", "y")
DMA_SEMS = ("xb0", "xb1", "wb0", "wb1", "xf0", "xf1", "wo0", "wo1",
            "gsm", "ysm", "as0", "as1", "as2", "wbs", "wf0", "wf1")
ENGS = ("t", "v", "s") + DMA_SEMS
STREAM_OF = {"t": "t", "v": "v", "s": "s", "g": "g", "y": "y", "a": "s"}


class Op:
    __slots__ = ("fn", "incs", "is_wait")
    def __init__(self, fn, is_wait=False):
        self.fn = fn
        self.incs = []
        self.is_wait = is_wait


class Em:
    def __init__(self):
        self.q = {e: [] for e in STREAMS}
        self.cnt = {e: 0 for e in ENGS}
        self.lastwait = {e: {s: 0 for s in ENGS} for e in STREAMS}

    def op(self, eng, fn, inc=False):
        o = Op(fn)
        self.q[STREAM_OF[eng]].append(o)
        if inc:
            self._attach(eng, o)
        return self.cnt[eng]

    def _attach(self, sem, o):
        n = 16 if sem in DMA_SEMS else 1
        o.incs.append((sem, n))
        self.cnt[sem] += n

    def dma(self, queue, sem, fn):
        o = Op(fn)
        self.q[STREAM_OF[queue]].append(o)
        self._attach(sem, o)
        return self.cnt[sem]

    def wait(self, eng, sem, val=None):
        st = STREAM_OF[eng]
        if val is None:
            val = self.cnt[sem]
        if val <= 0 or val <= self.lastwait[st][sem]:
            return
        self.lastwait[st][sem] = val
        def f(e, _s=sem, _v=val):
            return e.wait_ge(e._sems[_s], _v)
        self.q[st].append(Op(f, is_wait=True))

    def mark(self, eng):
        st = STREAM_OF[eng]
        for o in reversed(self.q[st]):
            if o.is_wait:
                continue
            if not o.incs:
                self._attach(eng, o)
            return self.cnt[eng]
        return self.cnt[eng]

    def barrier(self, engs=("t", "v", "s")):
        counts = {e: self.mark(e) for e in engs}
        for e in engs:
            for other in engs:
                if other != e:
                    self.wait(e, other, counts[other])


class Bank:
    __slots__ = ("fill", "cons")
    def __init__(self):
        self.fill = 0
        self.cons = []


def build_program(phases=3):
    nc = bass.Bass("TRN2", target_bir_lowering=False, detect_race_conditions=False)

    xd = nc.dram_tensor("x0", [NT, 128, D], F32, kind="ExternalInput").ap()
    wbT = nc.dram_tensor("wbT", [KC, 128, D], F32, kind="ExternalInput").ap()
    wqT = nc.dram_tensor("wqT", [KC, 128, NH * HD], F32, kind="ExternalInput").ap()
    woT = nc.dram_tensor("woT", [KC, 128, D], F32, kind="ExternalInput").ap()
    aT = nc.dram_tensor("aT", [KC, 128, 128], F32, kind="ExternalInput").ap()
    bfT = nc.dram_tensor("bfT", [128, D], F32, kind="ExternalInput").ap()
    wrT = nc.dram_tensor("wrT", [KC, 128, NH], F32, kind="ExternalInput").ap()
    bbd = nc.dram_tensor("bb", [1, D], F32, kind="ExternalInput").ap()
    bqd = nc.dram_tensor("bq", [128, NH], F32, kind="ExternalInput").ap()
    Md = nc.dram_tensor("M", [NH, 128, HD], F32, kind="ExternalInput").ap()
    zTd = nc.dram_tensor("zT", [128, NH], F32, kind="ExternalInput").ap()
    identd = nc.dram_tensor("ident", [128, 128], BF16, kind="ExternalInput").ap()
    ident2d = nc.dram_tensor("ident2", [128, 128], BF16, kind="ExternalInput").ap()
    onesd = nc.dram_tensor("ones_row", [1, 128], BF16, kind="ExternalInput").ap()

    outd = nc.dram_tensor("out", [NT, 128, D], F32, kind="ExternalOutput").ap()
    x1d = nc.dram_tensor("x1s", [NT, 128, D], F32, kind="Internal").ap()
    x2d = nc.dram_tensor("x2s", [NT, 128, D], F32, kind="Internal").ap()
    wofd = nc.dram_tensor("wofd", [NH, 128, D], F32, kind="Internal").ap()
    wbbf = nc.dram_tensor("wbbf", [KC, 128, D], BF16, kind="Internal").ap()

    em = Em()
    banks = [Bank() for _ in range(8)]

    from contextlib import ExitStack
    with ExitStack() as _st:
        def SB(name, shape, dt):
            return _st.enter_context(nc.sbuf_tensor(name, shape, dt))
        ident_sb = SB("ident_sb", [128, 128], BF16)
        ident2_sb = SB("ident2_sb", [128, 128], BF16)
        ones_sb = SB("ones_sb", [1, 128], BF16)
        ones16 = SB("ones16", [128, 16], F32)
        epsln = SB("epsln", [128, 1], F32)
        epsatt = SB("epsatt", [128, 1], F32)
        awrT = SB("awrT", [128, KC, 136], BF16)
        bfT_sb = SB("bfT_sb", [128, D], BF16)
        mz = SB("mz", [128, NH, 129], BF16)
        bbrep = SB("bbrep", [128, D], BF16)
        bq_sb = SB("bq_sb", [128, NH], F32)
        yT = SB("yT", [128, KC, 1024], BF16)
        wbig = SB("wbig", [128, KC, 1024], BF16)
        xb = SB("xb", [128, 2, D], BF16)
        ybf = SB("ybf", [128, D], BF16)
        gT = SB("gT", [128, 1024], BF16)
        wexp = SB("wexp", [128, 2, 128], BF16)
        stats = SB("stats", [128, RT, 8], F32)
        ltok = SB("ltok", [128, RT, 8], F32)
        mx8 = SB("mx8", [128, RT, 8], F32)
        tp2 = SB("tp2", [128, RT, 4], F32)
        msk = SB("msk", [128, RT, 8], F32)
        e1b = SB("e1b", [128, RT, 8], F32)
        scr = SB("scr", [128, 8192], BF16)
        outb = SB("outb", [128, 3, 512], F32)
        retrst = SB("retrst", [128, NH, 128], BF16)
        PS = _st.enter_context(nc.psum_tensor("PS", [128, 8, 512], F32))
        _semnames = ("t", "v", "s") + DMA_SEMS
        sems = {nm: _st.enter_context(nc.semaphore("sem_" + nm))
                for nm in _semnames}
        block = _st.enter_context(nc.Block())

        sigq = scr[:, :].rearrange("p (h t) -> p h t", h=NH)
        xbfc = [scr[:, i * 4096:(i + 1) * 4096].rearrange("p (a b) -> p a b", a=8)
                for i in range(2)]
        dumpA = scr[:, 0:4096]
        ybf_odd = scr[:, 4096:8192]
        hT = scr[:, 4096:5120]
        wexpT = scr[:, 5120:6144]
        ltT = scr[0:8, 6144:7168]
        retrT = yT[:, 0:NH, :]
        wof_ch = [xb[:, i, :].rearrange("p (a b) -> p a b", a=8) for i in range(2)]
        wb_ch = [wbig[:, :, i * 512:(i + 1) * 512] for i in range(2)]

        def t_op(fn, inc=False):
            return em.op("t", fn, inc)

        def v_op(fn, inc=False):
            return em.op("v", fn, inc)

        def s_op(fn, inc=False):
            return em.op("s", fn, inc)

        def bank_fill_prep(bs):
            for b in bs:
                for (eng, val) in banks[b].cons:
                    em.wait("t", eng, val)

        def bank_filled(bs):
            c = em.mark("t")
            for b in bs:
                banks[b].fill = c
                banks[b].cons = []

        def consume_bank(eng, b):
            em.wait(eng, "t", banks[b].fill)

        def bank_consumed(bs, eng):
            c = em.mark(eng)
            for b in bs:
                banks[b].cons.append((eng, c))

        slot_state = {"i": 0, "free": [0, 0, 0]}

        def next_slot():
            sl = slot_state["i"] % 3
            slot_state["i"] += 1
            return sl

        em.dma("g", "gsm", lambda e: e.dma_start(out=awrT[:, :, 0:128],
                                          in_=aT.rearrange("k p o -> p k o")))
        em.dma("g", "gsm", lambda e: e.dma_start(out=awrT[:, :, 128:136],
                                          in_=wrT.rearrange("k p o -> p k o")))
        em.dma("g", "gsm", lambda e: e.dma_start(out=bfT_sb[:, :], in_=bfT[:, :]))
        em.dma("g", "gsm", lambda e: e.dma_start(out=mz[:, :, 0:128],
                                          in_=Md.rearrange("h p o -> p h o"),
                                          allow_slow_non_contiguous=True))
        em.dma("g", "gsm", lambda e: e.dma_start(out=mz[:, :, 128:129],
                                          in_=zTd.rearrange("p (h o) -> p h o", o=1),
                                          allow_slow_non_contiguous=True))
        em.dma("y", "ysm", lambda e: e.dma_start(out=ident_sb[:, :], in_=identd[:, :]))
        em.dma("y", "ysm", lambda e: e.dma_start(out=ident2_sb[:, :], in_=ident2d[:, :]))
        em.dma("y", "ysm", lambda e: e.dma_start(out=ones_sb[:, :], in_=onesd[:, :]))
        em.dma("y", "ysm", lambda e: e.dma_start(out=bq_sb[:, :], in_=bqd[:, :]))
        em.dma("g", "gsm", lambda e: e.dma_start(out=bbrep[0:1, :], in_=bbd[:, :]))
        gc0, yc0 = em.cnt["gsm"], em.cnt["ysm"]

        v_op(lambda e: e.memset(ones16[:, :], 1.0))
        v_op(lambda e: e.memset(epsln[:, :], EPS_LN))
        v_op(lambda e: e.memset(epsatt[:, :], EPS_ATT), inc=True)

        em.wait("t", "ysm", yc0)
        em.wait("t", "gsm", gc0)
        for oc in range(NOC):
            b = oc % 2
            bank_fill_prep([b])
            def f(e, _b=b, _oc=oc):
                return e.matmul(PS[:, _b, :], ones_sb[:, :],
                         bbrep[0:1, _oc * 512:(_oc + 1) * 512],
                         start=True, stop=True)
            t_op(f, inc=True)
            bank_filled([b])
            consume_bank("v", b)
            def fv(e, _b=b, _oc=oc):
                return e.tensor_scalar(bbrep[:, _oc * 512:(_oc + 1) * 512], PS[:, _b, :],
                                0.5, None, ALU.mult)
            v_op(fv, inc=True)
            bank_consumed([b], "v")

        fold_state = {"v": 0, "wf0": 0, "wf1": 0}
        fold_done_v = {}

        def wo_stage(k4):
            return wbig[:, 4 * k4:4 * k4 + 4, :].bitcast(F32)

        def _mk_fold_unit(u):
            h, half = u % NH, u // NH
            p = u % 2
            base = 0 if p == 0 else 4
            srcs = [wo_stage(base + g) for g in range(4)]
            dsem = "wb%d" % p
            fsem = "wf%d" % p
            def emit():
                if u >= 2:
                    em.wait("y", "v", fold_done_v[u - 2])
                    em.wait("y", fsem, fold_state[fsem])
                last = 0
                for g4 in range(4):
                    def fdl(e, _h=h, _g=g4, _half=half, _dst=srcs[g4]):
                        return e.dma_start(
                            out=_dst,
                            in_=woT[4 * _h + _g, :,
                                    _half * 2048:(_half + 1) * 2048]
                            .rearrange("p (a b) -> p a b", a=4))
                    last = em.dma("y", dsem, fdl)
                em.wait("v", dsem, last)
                v_op(lambda e, _a=srcs[0], _b=srcs[1], _o=srcs[0]:
                     e.tensor_add(_o, _a, _b))
                v_op(lambda e, _a=srcs[2], _b=srcs[3], _o=srcs[2]:
                     e.tensor_add(_o, _a, _b))
                v_op(lambda e, _a=srcs[0], _b=srcs[2], _o=srcs[0]:
                     e.tensor_add(_o, _a, _b), inc=True)
                fold_done_v[u] = em.cnt["v"]
                em.wait("a", "v", fold_done_v[u])
                def fst(e, _h=h, _half=half, _src=srcs[0]):
                    return e.dma_start(
                        out=wofd[_h, :, _half * 2048:(_half + 1) * 2048]
                        .rearrange("p (a b) -> p a b", a=4), in_=_src)
                em.dma("a", fsem, fst)
                fold_state[fsem] = em.cnt[fsem]
            return emit

        fold_units = [_mk_fold_unit(u) for u in range(2 * NH)]
        fold_units.reverse()

        def store_totals():
            return {k: em.cnt[k] for k in ("as0", "as1", "as2")}

        def wait_stores(eng, tot):
            for k, v in tot.items():
                em.wait(eng, k, v)

        def stage_A(rnd, src, ffn, fold_hook=False):
            entry_t = em.cnt["t"]
            entry_st = store_totals()
            norm_cnt = {}
            tr_done = {}
            dma_cnt = {}
            sq_done = {}

            def emit_dma(i):
                tt = rnd * RT + i
                em.wait("g", "t", entry_t)
                wait_stores("g", entry_st)
                if i - 2 >= 0:
                    em.wait("g", "s", sq_done[i - 2])
                    em.wait("g", "v", norm_cnt[i - 2])
                sem = "xb%d" % (i % 2)
                def fx(e, _tt=tt, _bi=i % 2):
                    return e.dma_start(out=xb[:, _bi, :], in_=src[_tt, :, :])
                dma_cnt[i] = (sem, em.dma("g", sem, fx))

            emit_dma(0)
            emit_dma(1)
            for i in range(RT):
                em.wait("s", dma_cnt[i][0], dma_cnt[i][1])
                s_op(lambda e, _bi=i % 2, _i=i: e.activation(
                    dumpA[:, :], xb[:, _bi, :], AF.Copy,
                    accum_out=stats[:, _i, 0:1]))
                s_op(lambda e, _bi=i % 2, _i=i: e.activation(
                    dumpA[:, :], xb[:, _bi, :], AF.Square,
                    accum_out=stats[:, _i, 1:2]), inc=True)
                sq_done[i] = em.cnt["s"]
                s_op(lambda e, _i=i: e.activation(stats[:, _i, 2:3],
                                                  stats[:, _i, 0:1],
                                                  AF.Copy, scale=1.0 / D),
                     inc=True)
                em.wait("v", "s", em.cnt["s"])
                v_op(lambda e, _i=i: e.tensor_scalar(
                    stats[:, _i, 3:4], stats[:, _i, 2:3], stats[:, _i, 2:3],
                    -1.0, ALU.mult, ALU.mult), inc=True)
                em.wait("s", "v", em.cnt["v"])
                s_op(lambda e, _i=i: e.activation(stats[:, _i, 4:5],
                                                  stats[:, _i, 1:2], AF.Identity,
                                                  bias=stats[:, _i, 3:4],
                                                  scale=1.0 / D))
                s_op(lambda e: e.drain())
                s_op(lambda e, _i=i: e.activation(stats[:, _i, 5:6],
                                                  stats[:, _i, 4:5], AF.Ln,
                                                  bias=epsln[:, :]))
                s_op(lambda e: e.drain())
                s_op(lambda e, _i=i: e.activation(stats[:, _i, 6:7],
                                                  stats[:, _i, 5:6], AF.Exp,
                                                  scale=-0.5), inc=True)
                chain_done_i = em.cnt["s"]

                em.wait("v", "s", chain_done_i)
                if i >= 2:
                    em.wait("v", "t", tr_done[i - 2])
                yb = ybf if i % 2 == 0 else ybf_odd
                def fn(e, _bi=i % 2, _i=i, _yb=yb):
                    return e.tensor_scalar(_yb[:, :], xb[:, _bi, :],
                                    stats[:, _i, 2:3], stats[:, _i, 6:7],
                                    ALU.subtract, ALU.mult)
                v_op(fn, inc=True)
                norm_cnt[i] = em.cnt["v"]

                em.wait("t", "v", norm_cnt[i])
                for half in range(2):
                    bs = [4 * half + k for k in range(4)]
                    bank_fill_prep(bs)
                    for k in range(16):
                        kc = 16 * half + k
                        b = bs[k // 4]
                        off = (k % 4) * 128
                        def fm(e, _kc=kc, _b=b, _off=off, _yb=yb):
                            return e.matmul(PS[:, _b, _off:_off + 128],
                                     _yb[:, _kc * 128:(_kc + 1) * 128],
                                     ident_sb[:, :], start=True, stop=True)
                        t_op(fm, inc=(k == 15))
                    bank_filled(bs)
                    for b in bs:
                        consume_bank("v", b)
                    kc0 = 16 * half
                    def fe(e, _kc0=kc0, _i=i, _h=half):
                        return e.tensor_copy(
                            yT[:, _kc0:_kc0 + 16, _i * 128:(_i + 1) * 128],
                            PS[:, _h * 4:_h * 4 + 4, :]
                            .rearrange("p a (c d) -> p (a c) d", d=128))
                    v_op(fe, inc=True)
                    bank_consumed(bs, "v")
                tr_done[i] = em.cnt["t"]

                if i + 2 < RT:
                    emit_dma(i + 2)
                if fold_hook:
                    em.wait("y", "t", entry_t)
                    for _ in range(2):
                        if fold_units:
                            fold_units.pop()()

            if not ffn:
                return

            bank_fill_prep([0, 1, 2, 3])
            for kc in range(KC):
                for tc in range(2):
                    def fh(e, _kc=kc, _tc=tc):
                        return e.matmul(PS[:, _tc, :], awrT[:, _kc, 0:128],
                                 yT[:, _kc, _tc * 512:(_tc + 1) * 512],
                                 start=(_kc == 0), stop=(_kc == KC - 1))
                    t_op(fh)
                    def fr(e, _kc=kc, _tc=tc):
                        return e.matmul(PS[0:8, 2 + _tc, :], awrT[:, _kc, 128:136],
                                 yT[:, _kc, _tc * 512:(_tc + 1) * 512],
                                 start=(_kc == 0), stop=(_kc == KC - 1))
                    t_op(fr, inc=(kc == KC - 1 and tc == 1))
            bank_filled([0, 1, 2, 3])
            for b in (0, 1, 2, 3):
                consume_bank("v", b)
            v_op(lambda e: e.tensor_copy(
                hT[:, :], PS[:, 0:2, :].rearrange("p a b -> p (a b)")))
            v_op(lambda e: e.tensor_copy(
                ltT[:, :], PS[0:8, 2:4, :].rearrange("p a b -> p (a b)")),
                inc=True)
            bank_consumed([0, 1, 2, 3], "v")

            em.wait("t", "v", em.cnt["v"])
            bank_fill_prep([4])
            for j in range(RT):
                def fl(e, _j=j):
                    return e.matmul(PS[:, 4, _j * 8:(_j + 1) * 8],
                             ltT[:, _j * 128:(_j + 1) * 128],
                             ident_sb[0:8, 0:8], start=True, stop=True)
                t_op(fl, inc=(j == RT - 1))
            bank_filled([4])
            consume_bank("v", 4)
            v_op(lambda e: e.tensor_copy(
                ltok[:, :, :],
                PS[:, 4, 0:64].rearrange("p (a b) -> p a b", a=RT)), inc=True)
            bank_consumed([4], "v")

            for j in range(RT):
                v_op(lambda e, _j=j: e.max(out=mx8[:, _j, :], in_=ltok[:, _j, :]))
            for j in range(RT):
                v_op(lambda e, _j=j: e.tensor_scalar(
                    e1b[:, _j, :], ltok[:, _j, :], mx8[:, _j, 0:1], None,
                    ALU.subtract))
            for j in range(RT):
                v_op(lambda e, _j=j: e.tensor_scalar(
                    msk[:, _j, :], ltok[:, _j, :], mx8[:, _j, 1:2], None,
                    ALU.is_ge))
            for j in range(RT):
                v_op(lambda e, _j=j: e.tensor_scalar(
                    tp2[:, _j, 0:1], mx8[:, _j, 1:2], mx8[:, _j, 0:1], None,
                    ALU.subtract), inc=(j == RT - 1))
            em.wait("s", "v", em.cnt["v"])
            for j in range(RT):
                s_op(lambda e, _j=j: e.activation(e1b[:, _j, :], e1b[:, _j, :],
                                                  AF.Exp))
            for j in range(RT):
                s_op(lambda e, _j=j: e.activation(tp2[:, _j, 1:2],
                                                  tp2[:, _j, 0:1], AF.Exp),
                     inc=(j == RT - 1))
            em.wait("v", "s", em.cnt["s"])
            for j in range(RT):
                v_op(lambda e, _j=j: e.tensor_scalar(
                    tp2[:, _j, 2:3], tp2[:, _j, 1:2], 1.0, None, ALU.add))
            for j in range(RT):
                v_op(lambda e, _j=j: e.reciprocal(tp2[:, _j, 3:4], tp2[:, _j, 2:3]))
            for j in range(RT):
                v_op(lambda e, _j=j: e.tensor_mul(e1b[:, _j, :], e1b[:, _j, :],
                                                  msk[:, _j, :]))
            for j in range(RT):
                v_op(lambda e, _j=j: e.tensor_scalar(
                    e1b[:, _j, :], e1b[:, _j, :], tp2[:, _j, 3:4], 2.0,
                    ALU.mult, ALU.mult))
            for j in range(RT):
                wb_ = j % 2
                for ex in range(8):
                    v_op(lambda e, _j=j, _w=wb_, _ex=ex: e.tensor_scalar(
                        wexp[:, _w, _ex * 16:(_ex + 1) * 16], ones16[:, :],
                        e1b[:, _j, _ex:_ex + 1], None, ALU.mult),
                        inc=(ex == 7))
                em.wait("t", "v", em.cnt["v"])
                b = 5 + (j % 2)
                bank_fill_prep([b])
                def ft(e, _w=wb_, _b=b):
                    return e.matmul(PS[:, _b, 0:128], wexp[:, _w, :], ident_sb[:, :],
                             start=True, stop=True)
                t_op(ft, inc=True)
                bank_filled([b])
                consume_bank("v", b)
                v_op(lambda e, _j=j, _b=b: e.tensor_copy(
                    wexpT[:, _j * 128:(_j + 1) * 128], PS[:, _b, 0:128]),
                    inc=True)
                bank_consumed([b], "v")
            v_op(lambda e: e.tensor_mul(gT[:, :], hT[:, :], wexpT[:, :]),
                 inc=True)

        def ffn_B(rnd, src, dst, wsrc="bf16", save=False):
            gT_ready_v = em.cnt["v"]
            entry_t = em.cnt["t"]
            entry_s = em.cnt["s"]
            entry_st = store_totals()
            wb_g = {}
            wb_free_t = {}
            xb_y = {}
            xbf_free_t = {}

            def issue_w(oc):
                buf = oc % 2
                q = "g" if wsrc == "f32" else "y"
                em.wait(q, "t", wb_free_t.get(oc - 2, entry_t))
                em.wait(q, "wf0", fold_state["wf0"])
                em.wait(q, "wf1", fold_state["wf1"])
                em.wait(q, "v", fold_done_v.get(2 * NH - 1, 0))

                if wsrc == "bf16":
                    em.wait(q, "wbs", wbbf_saved[0])
                sem = "wb%d" % buf
                srcw = wbT if wsrc == "f32" else wbbf
                last = 0
                for part in range(4):
                    k0 = part * 8
                    def fw(e, _oc=oc, _buf=buf, _k0=k0, _srcw=srcw):
                        return e.dma_start(
                            out=wb_ch[_buf][:, _k0:_k0 + 8, :],
                            in_=_srcw.rearrange("k p o -> p k o")
                            [:, _k0:_k0 + 8, _oc * 512:(_oc + 1) * 512])
                    last = em.dma(q, sem, fw)
                wb_g[oc] = (sem, last)
                if save:
                    em.wait("a", sem, last)
                    for part in range(2):
                        k0 = part * 16
                        def fsv(e, _oc=oc, _buf=buf, _k0=k0):
                            return e.dma_start(
                                out=wbbf.rearrange("k p o -> p k o")
                                [:, _k0:_k0 + 16, _oc * 512:(_oc + 1) * 512],
                                in_=wb_ch[_buf][:, _k0:_k0 + 16, :])
                        em.dma("a", "wbs", fsv)

            def issue_x(oc):
                buf = oc % 2
                em.wait("g", "t", xbf_free_t.get(oc - 2, entry_t))
                em.wait("g", "s", entry_s)
                em.wait("g", "v", gT_ready_v)
                wait_stores("g", entry_st)
                sem = "xf%d" % buf
                last = 0
                for part in range(2):
                    a0 = part * 4
                    def fx(e, _oc=oc, _buf=buf, _a0=a0):
                        return e.dma_start(
                            out=xbfc[_buf][:, _a0:_a0 + 4, :],
                            in_=src[rnd * 8 + _a0:rnd * 8 + _a0 + 4, :,
                                    _oc * 512:(_oc + 1) * 512]
                            .rearrange("a p b -> p a b"))
                    last = em.dma("g", sem, fx)
                xb_y[oc] = (sem, last)

            issue_w(0)
            issue_x(0)
            issue_w(1)
            issue_x(1)

            for oc in range(NOC):
                buf = oc % 2
                em.wait("t", wb_g[oc][0], wb_g[oc][1])
                em.wait("t", xb_y[oc][0], xb_y[oc][1])
                em.wait("t", "v", gT_ready_v)
                for th in range(2):
                    bs = [4 * th + k for k in range(4)]
                    bank_fill_prep(bs)
                    for k in range(4):
                        def fi(e, _b=bs[k], _buf=buf, _tt=4 * th + k):
                            return e.matmul(PS[:, _b, :], ident2_sb[:, :],
                                     xbfc[_buf][:, _tt, :], start=True,
                                     stop=False)
                        t_op(fi)
                    for kc in range(KC):
                        for k in range(4):
                            tt = 4 * th + k
                            def fm(e, _kc=kc, _b=bs[k], _buf=buf, _tt=tt):
                                return e.matmul(PS[:, _b, :],
                                         yT[:, _kc, _tt * 128:(_tt + 1) * 128],
                                         wb_ch[_buf][:, _kc, :],
                                         start=False, stop=False)
                            t_op(fm)
                    for k in range(4):
                        tt = 4 * th + k
                        def fl(e, _b=bs[k], _tt=tt, _oc=oc):
                            return e.matmul(PS[:, _b, :],
                                     gT[:, _tt * 128:(_tt + 1) * 128],
                                     bfT_sb[:, _oc * 512:(_oc + 1) * 512],
                                     start=False, stop=True)
                        t_op(fl, inc=(k == 3))
                    bank_filled(bs)
                    for k in range(4):
                        tt = 4 * th + k
                        b = bs[k]
                        sl = next_slot()
                        em.wait("v", "as%d" % sl, slot_state["free"][sl])
                        consume_bank("v", b)
                        def fe(e, _b=b, _sl=sl, _oc=oc):
                            return e.scalar_tensor_tensor(
                                outb[:, _sl, :], PS[:, _b, :], 0.5,
                                bbrep[:, _oc * 512:(_oc + 1) * 512],
                                ALU.mult, ALU.add)
                        v_op(fe, inc=True)
                        bank_consumed([b], "v")
                        em.wait("a", "v", em.cnt["v"])
                        def fst(e, _sl=sl, _tt=rnd * 8 + tt, _oc=oc):
                            return e.dma_start(out=dst[_tt, :, _oc * 512:(_oc + 1) * 512],
                                        in_=outb[:, _sl, :])
                        slot_state["free"][sl] = em.dma("a", "as%d" % sl, fst)
                wb_free_t[oc] = em.mark("t")
                xbf_free_t[oc] = wb_free_t[oc]
                if oc + 2 < NOC:
                    issue_w(oc + 2)
                    issue_x(oc + 2)


        def attn_B(rnd, src, dst):
            entry_t = em.cnt["t"]
            entry_s = em.cnt["s"]
            entry_v = em.cnt["v"]
            entry_st = store_totals()
            for sl in range(3):
                em.wait("v", "as%d" % sl, slot_state["free"][sl])
            for ocq in range(8):
                bs = [2 * (ocq % 2), 2 * (ocq % 2) + 1]
                bank_fill_prep(bs)
                for kc in range(KC):
                    for tc in range(2):
                        def fq(e, _kc=kc, _ocq=ocq, _tc=tc, _b=bs[tc]):
                            return e.matmul(PS[:, _b, :],
                                     wbig[:, _kc, _ocq * 128:(_ocq + 1) * 128],
                                     yT[:, _kc, _tc * 512:(_tc + 1) * 512],
                                     start=(_kc == 0), stop=(_kc == KC - 1))
                        t_op(fq, inc=(kc == KC - 1 and tc == 1))
                bank_filled(bs)
                for tc in range(2):
                    b = bs[tc]
                    consume_bank("v", b)
                    def f1(e, _b=b, _ocq=ocq):
                        return e.tensor_scalar(outb[:, 0, :], PS[:, _b, :],
                                        bq_sb[:, _ocq:_ocq + 1], 0.0,
                                        ALU.add, ALU.max)
                    v_op(f1)
                    def f2(e, _b=b, _ocq=ocq):
                        return e.tensor_scalar(outb[:, 1, :], PS[:, _b, :],
                                        bq_sb[:, _ocq:_ocq + 1], 0.0,
                                        ALU.add, ALU.min)
                    v_op(f2, inc=True)
                    bank_consumed([b], "v")
                    em.wait("s", "v", em.cnt["v"])
                    s_op(lambda e: e.activation(outb[:, 2, :], outb[:, 1, :],
                                                AF.Exp), inc=True)
                    em.wait("v", "s", em.cnt["s"])
                    def f3(e, _ocq=ocq, _tc=tc):
                        return e.tensor_add(sigq[:, _ocq, _tc * 512:(_tc + 1) * 512],
                                     outb[:, 0, :], outb[:, 2, :])
                    v_op(f3, inc=True)
            em.barrier()

            for h in range(NH):
                bank_fill_prep([0, 1, 2, 3])
                for j in range(RT):
                    def fr(e, _h=h, _j=j):
                        return e.matmul(PS[:, _j // 2,
                                    (_j % 2) * 256:(_j % 2) * 256 + 129],
                                 sigq[:, _h, _j * 128:(_j + 1) * 128],
                                 mz[:, _h, :], start=True, stop=True)
                    t_op(fr, inc=(j == RT - 1))
                bank_filled([0, 1, 2, 3])
                for b in (0, 1, 2, 3):
                    consume_bank("v", b)
                for j in range(RT):
                    def fd(e, _j=j):
                        return e.tensor_scalar(
                            tp2[:, _j, 0:1],
                            PS[:, _j // 2,
                               (_j % 2) * 256 + 128:(_j % 2) * 256 + 129],
                            epsatt[:, :], None, ALU.add)
                    v_op(fd)
                for j in range(RT):
                    v_op(lambda e, _j=j: e.reciprocal(tp2[:, _j, 1:2],
                                                      tp2[:, _j, 0:1]))
                for j in range(RT):
                    def fsc(e, _j=j):
                        return e.tensor_scalar(
                            retrst[:, _j, :],
                            PS[:, _j // 2, (_j % 2) * 256:(_j % 2) * 256 + 128],
                            tp2[:, _j, 1:2], None, ALU.mult)
                    v_op(fsc, inc=(j == RT - 1))
                bank_consumed([0, 1, 2, 3], "v")
                em.wait("t", "v", em.cnt["v"])
                bank_fill_prep([4, 5])
                for j in range(RT):
                    b = 4 + j // 4
                    off = (j % 4) * 128
                    def ftr(e, _j=j, _b=b, _off=off):
                        return e.matmul(PS[:, _b, _off:_off + 128], retrst[:, _j, :],
                                 ident_sb[:, :], start=True, stop=True)
                    t_op(ftr, inc=(j == RT - 1))
                bank_filled([4, 5])
                consume_bank("v", 4)
                consume_bank("v", 5)
                def fev(e, _h=h):
                    return e.tensor_copy(retrT[:, _h, :],
                                  PS[:, 4:6, :].rearrange("p a b -> p (a b)"))
                v_op(fev, inc=True)
                bank_consumed([4, 5], "v")
            em.barrier()

            bo_entry_t = em.cnt["t"]
            wo_y = {}
            wo_free_t = {}
            xb_y = {}
            xbf_free_t = {}

            def issue_wo(oc):
                buf = oc % 2
                em.wait("g", "t", wo_free_t.get(oc - 2, bo_entry_t))
                em.wait("g", "wf0", fold_state["wf0"])
                em.wait("g", "wf1", fold_state["wf1"])
                em.wait("g", "v", fold_state["v"])
                sem = "wo%d" % buf
                def fw(e, _oc=oc, _buf=buf):
                    return e.dma_start(out=wof_ch[_buf][:, :, :],
                                in_=wofd.rearrange("h p o -> p h o")
                                [:, :, _oc * 512:(_oc + 1) * 512])
                wo_y[oc] = (sem, em.dma("g", sem, fw))

            def issue_x(oc):
                buf = oc % 2
                em.wait("g", "t", xbf_free_t.get(oc - 2, bo_entry_t))
                wait_stores("g", entry_st)
                sem = "xf%d" % buf
                last = 0
                for part in range(2):
                    a0 = part * 4
                    def fx(e, _oc=oc, _buf=buf, _a0=a0):
                        return e.dma_start(
                            out=xbfc[_buf][:, _a0:_a0 + 4, :],
                            in_=src[rnd * 8 + _a0:rnd * 8 + _a0 + 4, :,
                                    _oc * 512:(_oc + 1) * 512]
                            .rearrange("a p b -> p a b"))
                    last = em.dma("g", sem, fx)
                xb_y[oc] = (sem, last)

            issue_wo(0)
            issue_x(0)
            issue_wo(1)
            issue_x(1)

            for oc in range(NOC):
                buf = oc % 2
                em.wait("t", wo_y[oc][0], wo_y[oc][1])
                em.wait("t", xb_y[oc][0], xb_y[oc][1])
                for th in range(2):
                    bs = [4 * th + k for k in range(4)]
                    bank_fill_prep(bs)
                    for k in range(4):
                        def fi(e, _b=bs[k], _buf=buf, _tt=4 * th + k):
                            return e.matmul(PS[:, _b, :], ident_sb[:, :],
                                     xbfc[_buf][:, _tt, :], start=True,
                                     stop=False)
                        t_op(fi)
                    for kc in range(NH):
                        for k in range(4):
                            tt = 4 * th + k
                            def fm(e, _kc=kc, _b=bs[k], _buf=buf, _tt=tt):
                                return e.matmul(PS[:, _b, :],
                                         retrT[:, _kc, _tt * 128:(_tt + 1) * 128],
                                         wof_ch[_buf][:, _kc, :],
                                         start=False, stop=(_kc == NH - 1))
                            t_op(fm, inc=(kc == NH - 1 and k == 3))
                    bank_filled(bs)
                    for k in range(4):
                        tt = 4 * th + k
                        b = bs[k]
                        sl = next_slot()
                        em.wait("s", "as%d" % sl, slot_state["free"][sl])
                        consume_bank("s", b)
                        def fe(e, _b=b, _sl=sl):
                            return e.activation(outb[:, _sl, :], PS[:, _b, :], AF.Copy)
                        s_op(fe, inc=True)
                        bank_consumed([b], "s")
                        em.wait("a", "s", em.cnt["s"])
                        def fst(e, _sl=sl, _tt=rnd * 8 + tt, _oc=oc):
                            return e.dma_start(out=dst[_tt, :, _oc * 512:(_oc + 1) * 512],
                                        in_=outb[:, _sl, :])
                        slot_state["free"][sl] = em.dma("a", "as%d" % sl, fst)
                wo_free_t[oc] = em.mark("t")
                xbf_free_t[oc] = wo_free_t[oc]
                if oc + 2 < NOC:
                    issue_wo(oc + 2)
                    issue_x(oc + 2)

        p1_dst = x1d if phases >= 2 else outd
        wbbf_saved = [0]
        for rnd in range(2):
            stage_A(rnd, xd, ffn=True, fold_hook=(rnd == 1))
            em.barrier()
            ffn_B(rnd, xd, p1_dst, wsrc=("f32" if rnd == 0 else "bf16"),
                  save=(rnd == 0))
            if rnd == 0:
                wbbf_saved[0] = em.cnt["wbs"]
            em.barrier()

        if phases >= 2:
            p2_dst = x2d if phases >= 3 else outd
            p1_end_t = em.cnt["t"]
            wq_last = 0
            for rnd in range(2):
                stage_A(rnd, x1d, ffn=False)
                if rnd == 0:
                    em.wait("g", "t", p1_end_t)
                    for part in range(8):
                        k0 = part * 4
                        def fwq(e, _k0=k0):
                            return e.dma_start(out=wbig[:, _k0:_k0 + 4, :],
                                        in_=wqT.rearrange("k p o -> p k o")
                                        [:, _k0:_k0 + 4, :])
                        wq_last = em.dma("g", "gsm", fwq)
                em.barrier()
                em.wait("t", "gsm", wq_last)
                attn_B(rnd, x1d, p2_dst)
                em.barrier()

        if phases >= 3:
            for rnd in range(2):
                stage_A(rnd, x2d, ffn=True)
                em.barrier()
                ffn_B(rnd, x2d, outd)
                em.barrier()

        for k in ("as0", "as1", "as2"):
            em.wait("y", k, em.cnt[k])

        def replay(eng_obj, key):
            eng_obj._sems = sems
            for o in em.q[key]:
                ins = o.fn(eng_obj)
                for (s, n) in o.incs:
                    ins.then_inc(sems[s], n)

        @block.tensor
        def _(e):
            replay(e, "t")

        @block.vector
        def _(e):
            replay(e, "v")

        @block.scalar
        def _(e):
            replay(e, "s")

        @block.gpsimd
        def _(e):
            replay(e, "g")

        @block.sync
        def _(e):
            replay(e, "y")

    return nc


def _prep_inputs(inputs):
    f = np.ascontiguousarray
    x = inputs["L0_attn_hidden_states"]
    Wq = inputs["L0_attn_Wq"]
    bq = inputs["L0_attn_bq"]
    Wo = inputs["L0_attn_Wo"]
    M = inputs["L0_attn_M"]
    z = inputs["L0_attn_z"]
    Wb = inputs["L0_ffn_W_base"]
    bb = inputs["L0_ffn_b_base"]
    Wr = inputs["L0_ffn_W_router"]
    A = inputs["L0_ffn_A"]
    B = inputs["L0_ffn_B"]

    xs = np.asarray(x, np.float32).reshape(16384, D)
    wq_sel = np.asarray(Wq, np.float32).reshape(32, 128, D)[::4].reshape(1024, D)
    bq_sel = np.asarray(bq, np.float32).reshape(32, 128)[::4].reshape(8, 128)
    common = {
        "wbT": f(np.asarray(Wb, np.float32).T).reshape(KC, 128, D),
        "wqT": f(wq_sel.T).reshape(KC, 128, 1024),
        "woT": f(np.asarray(Wo, np.float32).T).reshape(KC, 128, D),
        "aT": f(np.asarray(A, np.float32).reshape(128, D).T).reshape(KC, 128, 128),
        "bfT": f(np.asarray(B, np.float32).transpose(0, 2, 1).reshape(128, D)),
        "wrT": f(np.asarray(Wr, np.float32).T).reshape(KC, 128, 8),
        "bb": f(np.asarray(bb, np.float32).reshape(1, D)),
        "bq": f(bq_sel.T),
        "M": f(np.asarray(M, np.float32)),
        "zT": f(np.asarray(z, np.float32).T),
        "ident": np.eye(128, dtype=ml_dtypes.bfloat16),
        "ident2": (2 * np.eye(128)).astype(ml_dtypes.bfloat16),
        "ones_row": np.ones((1, 128), ml_dtypes.bfloat16),
    }
    shards = []
    for c in range(8):
        m = dict(common)
        m["x0"] = f(xs[c * 2048:(c + 1) * 2048].reshape(NT, 128, D))
        shards.append(m)
    return shards


def kernel(**inputs):
    shards = _prep_inputs(inputs)
    nc = build_program(phases=3)
    res = run_bass_kernel_spmd(nc, shards, core_ids=list(range(8)))
    out = np.concatenate(
        [np.asarray(r["out"]).reshape(2048, D) for r in res.results], axis=0)
    return np.ascontiguousarray(out.reshape(4, 4096, 4096)).astype(np.float32)


# version 08
# speedup vs baseline: 1.0548x; 1.0548x over previous
import numpy as np
import ml_dtypes
import concourse.bass as bass
import concourse.mybir as mybir
from concourse.bass_utils import run_bass_kernel_spmd

F32 = mybir.dt.float32
BF16 = mybir.dt.bfloat16
AF = mybir.ActivationFunctionType
ALU = mybir.AluOpType

NT = 16
D = 4096
KC = 32
NOC = 8
RT = 8
NH = 8
HD = 128
EPS_LN = 1e-5
EPS_ATT = 1e-6

STREAMS = ("t", "v", "s", "g", "y")
DMA_SEMS = ("xb0", "xb1", "wb0", "wb1", "xf0", "xf1", "wo0", "wo1",
            "gsm", "ysm", "as0", "as1", "as2", "wbs", "wf0", "wf1")
ENGS = ("t", "v", "s") + DMA_SEMS
STREAM_OF = {"t": "t", "v": "v", "s": "s", "g": "g", "y": "y", "a": "s"}


class Op:
    __slots__ = ("fn", "incs", "is_wait")
    def __init__(self, fn, is_wait=False):
        self.fn = fn
        self.incs = []
        self.is_wait = is_wait


class Em:
    def __init__(self):
        self.q = {e: [] for e in STREAMS}
        self.cnt = {e: 0 for e in ENGS}
        self.lastwait = {e: {s: 0 for s in ENGS} for e in STREAMS}

    def op(self, eng, fn, inc=False):
        o = Op(fn)
        self.q[STREAM_OF[eng]].append(o)
        if inc:
            self._attach(eng, o)
        return self.cnt[eng]

    def _attach(self, sem, o):
        n = 16 if sem in DMA_SEMS else 1
        o.incs.append((sem, n))
        self.cnt[sem] += n

    def dma(self, queue, sem, fn):
        o = Op(fn)
        self.q[STREAM_OF[queue]].append(o)
        self._attach(sem, o)
        return self.cnt[sem]

    def wait(self, eng, sem, val=None):
        st = STREAM_OF[eng]
        if val is None:
            val = self.cnt[sem]
        if val <= 0 or val <= self.lastwait[st][sem]:
            return
        self.lastwait[st][sem] = val
        def f(e, _s=sem, _v=val):
            return e.wait_ge(e._sems[_s], _v)
        self.q[st].append(Op(f, is_wait=True))

    def mark(self, eng):
        st = STREAM_OF[eng]
        for o in reversed(self.q[st]):
            if o.is_wait:
                continue
            if not o.incs:
                self._attach(eng, o)
            return self.cnt[eng]
        return self.cnt[eng]

    def barrier(self, engs=("t", "v", "s")):
        counts = {e: self.mark(e) for e in engs}
        for e in engs:
            for other in engs:
                if other != e:
                    self.wait(e, other, counts[other])


class Bank:
    __slots__ = ("fill", "cons")
    def __init__(self):
        self.fill = 0
        self.cons = []


def build_program(phases=3):
    nc = bass.Bass("TRN2", target_bir_lowering=False, detect_race_conditions=False)

    xd = nc.dram_tensor("x0", [NT, 128, D], F32, kind="ExternalInput").ap()
    wbT = nc.dram_tensor("wbT", [KC, 128, D], F32, kind="ExternalInput").ap()
    wqT = nc.dram_tensor("wqT", [KC, 128, NH * HD], F32, kind="ExternalInput").ap()
    woT = nc.dram_tensor("woT", [KC, 128, D], F32, kind="ExternalInput").ap()
    aT = nc.dram_tensor("aT", [KC, 128, 128], F32, kind="ExternalInput").ap()
    bfT = nc.dram_tensor("bfT", [128, D], F32, kind="ExternalInput").ap()
    wrT = nc.dram_tensor("wrT", [KC, 128, NH], F32, kind="ExternalInput").ap()
    bbd = nc.dram_tensor("bb", [1, D], F32, kind="ExternalInput").ap()
    bqd = nc.dram_tensor("bq", [128, NH], F32, kind="ExternalInput").ap()
    Md = nc.dram_tensor("M", [NH, 128, HD], F32, kind="ExternalInput").ap()
    zTd = nc.dram_tensor("zT", [128, NH], F32, kind="ExternalInput").ap()
    identd = nc.dram_tensor("ident", [128, 128], BF16, kind="ExternalInput").ap()
    ident2d = nc.dram_tensor("ident2", [128, 128], BF16, kind="ExternalInput").ap()
    onesd = nc.dram_tensor("ones_row", [1, 128], BF16, kind="ExternalInput").ap()

    outd = nc.dram_tensor("out", [NT, 128, D], F32, kind="ExternalOutput").ap()
    x1d = nc.dram_tensor("x1s", [NT, 128, D], BF16, kind="Internal").ap()
    x2d = nc.dram_tensor("x2s", [NT, 128, D], BF16, kind="Internal").ap()
    wofd = nc.dram_tensor("wofd", [NH, 128, D], BF16, kind="Internal").ap()
    wbbf = nc.dram_tensor("wbbf", [KC, 128, D], BF16, kind="Internal").ap()

    em = Em()
    banks = [Bank() for _ in range(8)]

    from contextlib import ExitStack
    with ExitStack() as _st:
        def SB(name, shape, dt):
            return _st.enter_context(nc.sbuf_tensor(name, shape, dt))
        ident_sb = SB("ident_sb", [128, 128], BF16)
        ident2_sb = SB("ident2_sb", [128, 128], BF16)
        ones_sb = SB("ones_sb", [1, 128], BF16)
        ones16 = SB("ones16", [128, 16], F32)
        epsln = SB("epsln", [128, 1], F32)
        epsatt = SB("epsatt", [128, 1], F32)
        awrT = SB("awrT", [128, KC, 136], BF16)
        bfT_sb = SB("bfT_sb", [128, D], BF16)
        mz = SB("mz", [128, NH, 129], BF16)
        bbrep = SB("bbrep", [128, D], BF16)
        bq_sb = SB("bq_sb", [128, NH], F32)
        yT = SB("yT", [128, KC, 1024], BF16)
        wbig = SB("wbig", [128, KC, 1024], BF16)
        xb = SB("xb", [128, 2, D], BF16)
        ybf = SB("ybf", [128, D], BF16)
        gT = SB("gT", [128, 1024], BF16)
        wexp = SB("wexp", [128, 2, 128], BF16)
        stats = SB("stats", [128, RT, 8], F32)
        ltok = SB("ltok", [128, RT, 8], F32)
        mx8 = SB("mx8", [128, RT, 8], F32)
        tp2 = SB("tp2", [128, RT, 4], F32)
        msk = SB("msk", [128, RT, 8], F32)
        e1b = SB("e1b", [128, RT, 8], F32)
        scr = SB("scr", [128, 8192], BF16)
        outb = SB("outb", [128, 3, 512], F32)
        retrst = SB("retrst", [128, NH, 128], BF16)
        PS = _st.enter_context(nc.psum_tensor("PS", [128, 8, 512], F32))
        _semnames = ("t", "v", "s") + DMA_SEMS
        sems = {nm: _st.enter_context(nc.semaphore("sem_" + nm))
                for nm in _semnames}
        block = _st.enter_context(nc.Block())

        outb_bf = [outb[:, i, :].bitcast(BF16)[:, 0:512] for i in range(3)]
        sigq = scr[:, :].rearrange("p (h t) -> p h t", h=NH)
        xbfc = [scr[:, i * 4096:(i + 1) * 4096].rearrange("p (a b) -> p a b", a=8)
                for i in range(2)]
        dumpA = scr[:, 0:4096]
        ybf_odd = scr[:, 4096:8192]
        hT = scr[:, 4096:5120]
        wexpT = scr[:, 5120:6144]
        ltT = scr[0:8, 6144:7168]
        retrT = yT[:, 0:NH, :]
        wof_ch = [xb[:, i, :].rearrange("p (a b) -> p a b", a=8) for i in range(2)]
        wb_ch = [wbig[:, :, i * 512:(i + 1) * 512] for i in range(2)]

        def t_op(fn, inc=False):
            return em.op("t", fn, inc)

        def v_op(fn, inc=False):
            return em.op("v", fn, inc)

        def s_op(fn, inc=False):
            return em.op("s", fn, inc)

        def bank_fill_prep(bs):
            for b in bs:
                for (eng, val) in banks[b].cons:
                    em.wait("t", eng, val)

        def bank_filled(bs):
            c = em.mark("t")
            for b in bs:
                banks[b].fill = c
                banks[b].cons = []

        def consume_bank(eng, b):
            em.wait(eng, "t", banks[b].fill)

        def bank_consumed(bs, eng):
            c = em.mark(eng)
            for b in bs:
                banks[b].cons.append((eng, c))

        slot_state = {"i": 0, "free": [0, 0, 0]}

        def next_slot():
            sl = slot_state["i"] % 3
            slot_state["i"] += 1
            return sl

        em.dma("g", "gsm", lambda e: e.dma_start(out=awrT[:, :, 0:128],
                                          in_=aT.rearrange("k p o -> p k o")))
        em.dma("g", "gsm", lambda e: e.dma_start(out=awrT[:, :, 128:136],
                                          in_=wrT.rearrange("k p o -> p k o")))
        em.dma("g", "gsm", lambda e: e.dma_start(out=bfT_sb[:, :], in_=bfT[:, :]))
        em.dma("g", "gsm", lambda e: e.dma_start(out=mz[:, :, 0:128],
                                          in_=Md.rearrange("h p o -> p h o"),
                                          allow_slow_non_contiguous=True))
        em.dma("g", "gsm", lambda e: e.dma_start(out=mz[:, :, 128:129],
                                          in_=zTd.rearrange("p (h o) -> p h o", o=1),
                                          allow_slow_non_contiguous=True))
        em.dma("y", "ysm", lambda e: e.dma_start(out=ident_sb[:, :], in_=identd[:, :]))
        em.dma("y", "ysm", lambda e: e.dma_start(out=ident2_sb[:, :], in_=ident2d[:, :]))
        em.dma("y", "ysm", lambda e: e.dma_start(out=ones_sb[:, :], in_=onesd[:, :]))
        em.dma("y", "ysm", lambda e: e.dma_start(out=bq_sb[:, :], in_=bqd[:, :]))
        em.dma("g", "gsm", lambda e: e.dma_start(out=bbrep[0:1, :], in_=bbd[:, :]))
        gc0, yc0 = em.cnt["gsm"], em.cnt["ysm"]

        v_op(lambda e: e.memset(ones16[:, :], 1.0))
        v_op(lambda e: e.memset(epsln[:, :], EPS_LN))
        v_op(lambda e: e.memset(epsatt[:, :], EPS_ATT), inc=True)

        em.wait("t", "ysm", yc0)
        em.wait("t", "gsm", gc0)
        for oc in range(NOC):
            b = oc % 2
            bank_fill_prep([b])
            def f(e, _b=b, _oc=oc):
                return e.matmul(PS[:, _b, :], ones_sb[:, :],
                         bbrep[0:1, _oc * 512:(_oc + 1) * 512],
                         start=True, stop=True)
            t_op(f, inc=True)
            bank_filled([b])
            consume_bank("v", b)
            def fv(e, _b=b, _oc=oc):
                return e.tensor_scalar(bbrep[:, _oc * 512:(_oc + 1) * 512], PS[:, _b, :],
                                0.5, None, ALU.mult)
            v_op(fv, inc=True)
            bank_consumed([b], "v")

        fold_state = {"v": 0, "wf0": 0, "wf1": 0}
        fold_done_v = {}

        def wo_stage(k4):
            return wbig[:, 4 * k4:4 * k4 + 4, :].bitcast(F32)

        def _mk_fold_unit(u):
            h, half = u % NH, u // NH
            p = u % 2
            base = 0 if p == 0 else 4
            srcs = [wo_stage(base + g) for g in range(4)]
            dsem = "wb%d" % p
            fsem = "wf%d" % p
            def emit():
                if u >= 2:
                    em.wait("y", "v", fold_done_v[u - 2])
                    em.wait("y", fsem, fold_state[fsem])
                last = 0
                for g4 in range(4):
                    def fdl(e, _h=h, _g=g4, _half=half, _dst=srcs[g4]):
                        return e.dma_start(
                            out=_dst,
                            in_=woT[4 * _h + _g, :,
                                    _half * 2048:(_half + 1) * 2048]
                            .rearrange("p (a b) -> p a b", a=4))
                    last = em.dma("y", dsem, fdl)
                em.wait("v", dsem, last)
                v_op(lambda e, _a=srcs[0], _b=srcs[1], _o=srcs[0]:
                     e.tensor_add(_o, _a, _b))
                v_op(lambda e, _a=srcs[2], _b=srcs[3], _o=srcs[2]:
                     e.tensor_add(_o, _a, _b))
                v_op(lambda e, _a=srcs[0], _b=srcs[2], _o=srcs[0]:
                     e.tensor_add(_o, _a, _b), inc=True)
                fold_done_v[u] = em.cnt["v"]
                em.wait("g", "v", fold_done_v[u])
                def fst(e, _h=h, _half=half, _src=srcs[0]):
                    return e.dma_start(
                        out=wofd[_h, :, _half * 2048:(_half + 1) * 2048]
                        .rearrange("p (a b) -> p a b", a=4), in_=_src)
                em.dma("g", fsem, fst)
                fold_state[fsem] = em.cnt[fsem]
            return emit

        fold_units = [_mk_fold_unit(u) for u in range(2 * NH)]
        fold_units.reverse()

        def store_totals():
            return {k: em.cnt[k] for k in ("as0", "as1", "as2")}

        def wait_stores(eng, tot):
            for k, v in tot.items():
                em.wait(eng, k, v)

        def stage_A(rnd, src, ffn, fold_hook=False):
            entry_t = em.cnt["t"]
            entry_st = store_totals()
            norm_cnt = {}
            tr_done = {}
            dma_cnt = {}
            sq_done = {}

            def emit_dma(i):
                tt = rnd * RT + i
                em.wait("g", "t", entry_t)
                wait_stores("g", entry_st)
                if i - 2 >= 0:
                    em.wait("g", "s", sq_done[i - 2])
                    em.wait("g", "v", norm_cnt[i - 2])
                sem = "xb%d" % (i % 2)
                def fx(e, _tt=tt, _bi=i % 2):
                    return e.dma_start(out=xb[:, _bi, :], in_=src[_tt, :, :])
                dma_cnt[i] = (sem, em.dma("g", sem, fx))

            emit_dma(0)
            emit_dma(1)
            for i in range(RT):
                em.wait("s", dma_cnt[i][0], dma_cnt[i][1])
                s_op(lambda e, _bi=i % 2, _i=i: e.activation(
                    dumpA[:, :], xb[:, _bi, :], AF.Copy,
                    accum_out=stats[:, _i, 0:1]))
                s_op(lambda e, _bi=i % 2, _i=i: e.activation(
                    dumpA[:, :], xb[:, _bi, :], AF.Square,
                    accum_out=stats[:, _i, 1:2]), inc=True)
                sq_done[i] = em.cnt["s"]
                s_op(lambda e, _i=i: e.activation(stats[:, _i, 2:3],
                                                  stats[:, _i, 0:1],
                                                  AF.Copy, scale=1.0 / D),
                     inc=True)
                em.wait("v", "s", em.cnt["s"])
                v_op(lambda e, _i=i: e.tensor_scalar(
                    stats[:, _i, 3:4], stats[:, _i, 2:3], stats[:, _i, 2:3],
                    -1.0, ALU.mult, ALU.mult), inc=True)
                em.wait("s", "v", em.cnt["v"])
                s_op(lambda e, _i=i: e.activation(stats[:, _i, 4:5],
                                                  stats[:, _i, 1:2], AF.Identity,
                                                  bias=stats[:, _i, 3:4],
                                                  scale=1.0 / D))
                s_op(lambda e: e.drain())
                s_op(lambda e, _i=i: e.activation(stats[:, _i, 5:6],
                                                  stats[:, _i, 4:5], AF.Ln,
                                                  bias=epsln[:, :]))
                s_op(lambda e: e.drain())
                s_op(lambda e, _i=i: e.activation(stats[:, _i, 6:7],
                                                  stats[:, _i, 5:6], AF.Exp,
                                                  scale=-0.5), inc=True)
                chain_done_i = em.cnt["s"]

                em.wait("v", "s", chain_done_i)
                if i >= 2:
                    em.wait("v", "t", tr_done[i - 2])
                yb = ybf if i % 2 == 0 else ybf_odd
                def fn(e, _bi=i % 2, _i=i, _yb=yb):
                    return e.tensor_scalar(_yb[:, :], xb[:, _bi, :],
                                    stats[:, _i, 2:3], stats[:, _i, 6:7],
                                    ALU.subtract, ALU.mult)
                v_op(fn, inc=True)
                norm_cnt[i] = em.cnt["v"]

                em.wait("t", "v", norm_cnt[i])
                for half in range(2):
                    bs = [4 * half + k for k in range(4)]
                    bank_fill_prep(bs)
                    for k in range(16):
                        kc = 16 * half + k
                        b = bs[k // 4]
                        off = (k % 4) * 128
                        def fm(e, _kc=kc, _b=b, _off=off, _yb=yb):
                            return e.matmul(PS[:, _b, _off:_off + 128],
                                     _yb[:, _kc * 128:(_kc + 1) * 128],
                                     ident_sb[:, :], start=True, stop=True)
                        t_op(fm, inc=(k == 15))
                    bank_filled(bs)
                    for b in bs:
                        consume_bank("v", b)
                    kc0 = 16 * half
                    def fe(e, _kc0=kc0, _i=i, _h=half):
                        return e.tensor_copy(
                            yT[:, _kc0:_kc0 + 16, _i * 128:(_i + 1) * 128],
                            PS[:, _h * 4:_h * 4 + 4, :]
                            .rearrange("p a (c d) -> p (a c) d", d=128))
                    v_op(fe, inc=True)
                    bank_consumed(bs, "v")
                tr_done[i] = em.cnt["t"]

                if i + 2 < RT:
                    emit_dma(i + 2)
                if fold_hook:
                    em.wait("y", "t", entry_t)
                    for _ in range(2):
                        if fold_units:
                            fold_units.pop()()

            if not ffn:
                return

            bank_fill_prep([0, 1, 2, 3])
            for kc in range(KC):
                for tc in range(2):
                    def fh(e, _kc=kc, _tc=tc):
                        return e.matmul(PS[:, _tc, :], awrT[:, _kc, 0:128],
                                 yT[:, _kc, _tc * 512:(_tc + 1) * 512],
                                 start=(_kc == 0), stop=(_kc == KC - 1))
                    t_op(fh)
                    def fr(e, _kc=kc, _tc=tc):
                        return e.matmul(PS[0:8, 2 + _tc, :], awrT[:, _kc, 128:136],
                                 yT[:, _kc, _tc * 512:(_tc + 1) * 512],
                                 start=(_kc == 0), stop=(_kc == KC - 1))
                    t_op(fr, inc=(kc == KC - 1 and tc == 1))
            bank_filled([0, 1, 2, 3])
            for b in (0, 1, 2, 3):
                consume_bank("v", b)
            v_op(lambda e: e.tensor_copy(
                hT[:, :], PS[:, 0:2, :].rearrange("p a b -> p (a b)")))
            v_op(lambda e: e.tensor_copy(
                ltT[:, :], PS[0:8, 2:4, :].rearrange("p a b -> p (a b)")),
                inc=True)
            bank_consumed([0, 1, 2, 3], "v")

            em.wait("t", "v", em.cnt["v"])
            bank_fill_prep([4])
            for j in range(RT):
                def fl(e, _j=j):
                    return e.matmul(PS[:, 4, _j * 8:(_j + 1) * 8],
                             ltT[:, _j * 128:(_j + 1) * 128],
                             ident_sb[0:8, 0:8], start=True, stop=True)
                t_op(fl, inc=(j == RT - 1))
            bank_filled([4])
            consume_bank("v", 4)
            v_op(lambda e: e.tensor_copy(
                ltok[:, :, :],
                PS[:, 4, 0:64].rearrange("p (a b) -> p a b", a=RT)), inc=True)
            bank_consumed([4], "v")

            for j in range(RT):
                v_op(lambda e, _j=j: e.max(out=mx8[:, _j, :], in_=ltok[:, _j, :]))
            for j in range(RT):
                v_op(lambda e, _j=j: e.tensor_scalar(
                    e1b[:, _j, :], ltok[:, _j, :], mx8[:, _j, 0:1], None,
                    ALU.subtract))
            for j in range(RT):
                v_op(lambda e, _j=j: e.tensor_scalar(
                    msk[:, _j, :], ltok[:, _j, :], mx8[:, _j, 1:2], None,
                    ALU.is_ge))
            for j in range(RT):
                v_op(lambda e, _j=j: e.tensor_scalar(
                    tp2[:, _j, 0:1], mx8[:, _j, 1:2], mx8[:, _j, 0:1], None,
                    ALU.subtract), inc=(j == RT - 1))
            em.wait("s", "v", em.cnt["v"])
            for j in range(RT):
                s_op(lambda e, _j=j: e.activation(e1b[:, _j, :], e1b[:, _j, :],
                                                  AF.Exp))
            for j in range(RT):
                s_op(lambda e, _j=j: e.activation(tp2[:, _j, 1:2],
                                                  tp2[:, _j, 0:1], AF.Exp),
                     inc=(j == RT - 1))
            em.wait("v", "s", em.cnt["s"])
            for j in range(RT):
                v_op(lambda e, _j=j: e.tensor_scalar(
                    tp2[:, _j, 2:3], tp2[:, _j, 1:2], 1.0, None, ALU.add))
            for j in range(RT):
                v_op(lambda e, _j=j: e.reciprocal(tp2[:, _j, 3:4], tp2[:, _j, 2:3]))
            for j in range(RT):
                v_op(lambda e, _j=j: e.tensor_mul(e1b[:, _j, :], e1b[:, _j, :],
                                                  msk[:, _j, :]))
            for j in range(RT):
                v_op(lambda e, _j=j: e.tensor_scalar(
                    e1b[:, _j, :], e1b[:, _j, :], tp2[:, _j, 3:4], 2.0,
                    ALU.mult, ALU.mult))
            for j in range(RT):
                wb_ = j % 2
                for ex in range(8):
                    v_op(lambda e, _j=j, _w=wb_, _ex=ex: e.tensor_scalar(
                        wexp[:, _w, _ex * 16:(_ex + 1) * 16], ones16[:, :],
                        e1b[:, _j, _ex:_ex + 1], None, ALU.mult),
                        inc=(ex == 7))
                em.wait("t", "v", em.cnt["v"])
                b = 5 + (j % 2)
                bank_fill_prep([b])
                def ft(e, _w=wb_, _b=b):
                    return e.matmul(PS[:, _b, 0:128], wexp[:, _w, :], ident_sb[:, :],
                             start=True, stop=True)
                t_op(ft, inc=True)
                bank_filled([b])
                consume_bank("v", b)
                v_op(lambda e, _j=j, _b=b: e.tensor_copy(
                    wexpT[:, _j * 128:(_j + 1) * 128], PS[:, _b, 0:128]),
                    inc=True)
                bank_consumed([b], "v")
            v_op(lambda e: e.tensor_mul(gT[:, :], hT[:, :], wexpT[:, :]),
                 inc=True)

        def ffn_B(rnd, src, dst, wsrc="bf16", save=False):
            gT_ready_v = em.cnt["v"]
            entry_t = em.cnt["t"]
            entry_s = em.cnt["s"]
            entry_st = store_totals()
            wb_g = {}
            wb_free_t = {}
            xb_y = {}
            xbf_free_t = {}

            def issue_w(oc):
                buf = oc % 2
                q = "g" if wsrc == "f32" else "y"
                em.wait(q, "t", wb_free_t.get(oc - 2, entry_t))
                em.wait(q, "wf0", fold_state["wf0"])
                em.wait(q, "wf1", fold_state["wf1"])
                em.wait(q, "v", fold_done_v.get(2 * NH - 1, 0))

                if wsrc == "bf16":
                    em.wait(q, "wbs", wbbf_saved[0])
                sem = "wb%d" % buf
                srcw = wbT if wsrc == "f32" else wbbf
                last = 0
                for part in range(4):
                    k0 = part * 8
                    def fw(e, _oc=oc, _buf=buf, _k0=k0, _srcw=srcw):
                        return e.dma_start(
                            out=wb_ch[_buf][:, _k0:_k0 + 8, :],
                            in_=_srcw.rearrange("k p o -> p k o")
                            [:, _k0:_k0 + 8, _oc * 512:(_oc + 1) * 512])
                    last = em.dma(q, sem, fw)
                wb_g[oc] = (sem, last)
                if save:
                    em.wait("a", sem, last)
                    for part in range(2):
                        k0 = part * 16
                        def fsv(e, _oc=oc, _buf=buf, _k0=k0):
                            return e.dma_start(
                                out=wbbf.rearrange("k p o -> p k o")
                                [:, _k0:_k0 + 16, _oc * 512:(_oc + 1) * 512],
                                in_=wb_ch[_buf][:, _k0:_k0 + 16, :])
                        em.dma("a", "wbs", fsv)

            def issue_x(oc):
                buf = oc % 2
                em.wait("g", "t", xbf_free_t.get(oc - 2, entry_t))
                em.wait("g", "s", entry_s)
                em.wait("g", "v", gT_ready_v)
                wait_stores("g", entry_st)
                sem = "xf%d" % buf
                last = 0
                for part in range(2):
                    a0 = part * 4
                    def fx(e, _oc=oc, _buf=buf, _a0=a0):
                        return e.dma_start(
                            out=xbfc[_buf][:, _a0:_a0 + 4, :],
                            in_=src[rnd * 8 + _a0:rnd * 8 + _a0 + 4, :,
                                    _oc * 512:(_oc + 1) * 512]
                            .rearrange("a p b -> p a b"))
                    last = em.dma("g", sem, fx)
                xb_y[oc] = (sem, last)

            issue_w(0)
            issue_x(0)
            issue_w(1)
            issue_x(1)

            for oc in range(NOC):
                buf = oc % 2
                em.wait("t", wb_g[oc][0], wb_g[oc][1])
                em.wait("t", xb_y[oc][0], xb_y[oc][1])
                em.wait("t", "v", gT_ready_v)
                for th in range(2):
                    bs = [4 * th + k for k in range(4)]
                    bank_fill_prep(bs)
                    for k in range(4):
                        def fi(e, _b=bs[k], _buf=buf, _tt=4 * th + k):
                            return e.matmul(PS[:, _b, :], ident2_sb[:, :],
                                     xbfc[_buf][:, _tt, :], start=True,
                                     stop=False)
                        t_op(fi)
                    for kc in range(KC):
                        for k in range(4):
                            tt = 4 * th + k
                            def fm(e, _kc=kc, _b=bs[k], _buf=buf, _tt=tt):
                                return e.matmul(PS[:, _b, :],
                                         yT[:, _kc, _tt * 128:(_tt + 1) * 128],
                                         wb_ch[_buf][:, _kc, :],
                                         start=False, stop=False)
                            t_op(fm)
                    for k in range(4):
                        tt = 4 * th + k
                        def fl(e, _b=bs[k], _tt=tt, _oc=oc):
                            return e.matmul(PS[:, _b, :],
                                     gT[:, _tt * 128:(_tt + 1) * 128],
                                     bfT_sb[:, _oc * 512:(_oc + 1) * 512],
                                     start=False, stop=True)
                        t_op(fl, inc=(k == 3))
                    bank_filled(bs)
                    for k in range(4):
                        tt = 4 * th + k
                        b = bs[k]
                        sl = next_slot()
                        bf = (dst.dtype == BF16)
                        tgt = outb_bf[sl] if bf else outb[:, sl, :]
                        em.wait("v", "as%d" % sl, slot_state["free"][sl])
                        consume_bank("v", b)
                        def fe(e, _b=b, _tgt=tgt, _oc=oc):
                            return e.scalar_tensor_tensor(
                                _tgt, PS[:, _b, :], 0.5,
                                bbrep[:, _oc * 512:(_oc + 1) * 512],
                                ALU.mult, ALU.add)
                        v_op(fe, inc=True)
                        bank_consumed([b], "v")
                        em.wait("a", "v", em.cnt["v"])
                        def fst(e, _tgt=tgt, _tt=rnd * 8 + tt, _oc=oc):
                            return e.dma_start(out=dst[_tt, :, _oc * 512:(_oc + 1) * 512],
                                        in_=_tgt)
                        slot_state["free"][sl] = em.dma("a", "as%d" % sl, fst)
                wb_free_t[oc] = em.mark("t")
                xbf_free_t[oc] = wb_free_t[oc]
                if oc + 2 < NOC:
                    issue_w(oc + 2)
                    issue_x(oc + 2)


        def attn_B(rnd, src, dst):
            entry_t = em.cnt["t"]
            entry_s = em.cnt["s"]
            entry_v = em.cnt["v"]
            entry_st = store_totals()
            for sl in range(3):
                em.wait("v", "as%d" % sl, slot_state["free"][sl])
            for ocq in range(8):
                bs = [2 * (ocq % 2), 2 * (ocq % 2) + 1]
                bank_fill_prep(bs)
                for kc in range(KC):
                    for tc in range(2):
                        def fq(e, _kc=kc, _ocq=ocq, _tc=tc, _b=bs[tc]):
                            return e.matmul(PS[:, _b, :],
                                     wbig[:, _kc, _ocq * 128:(_ocq + 1) * 128],
                                     yT[:, _kc, _tc * 512:(_tc + 1) * 512],
                                     start=(_kc == 0), stop=(_kc == KC - 1))
                        t_op(fq, inc=(kc == KC - 1 and tc == 1))
                bank_filled(bs)
                for tc in range(2):
                    b = bs[tc]
                    consume_bank("v", b)
                    def f1(e, _b=b, _ocq=ocq):
                        return e.tensor_scalar(outb[:, 0, :], PS[:, _b, :],
                                        bq_sb[:, _ocq:_ocq + 1], 0.0,
                                        ALU.add, ALU.max)
                    v_op(f1)
                    def f2(e, _b=b, _ocq=ocq):
                        return e.tensor_scalar(outb[:, 1, :], PS[:, _b, :],
                                        bq_sb[:, _ocq:_ocq + 1], 0.0,
                                        ALU.add, ALU.min)
                    v_op(f2, inc=True)
                    bank_consumed([b], "v")
                    em.wait("s", "v", em.cnt["v"])
                    s_op(lambda e: e.activation(outb[:, 2, :], outb[:, 1, :],
                                                AF.Exp), inc=True)
                    em.wait("v", "s", em.cnt["s"])
                    def f3(e, _ocq=ocq, _tc=tc):
                        return e.tensor_add(sigq[:, _ocq, _tc * 512:(_tc + 1) * 512],
                                     outb[:, 0, :], outb[:, 2, :])
                    v_op(f3, inc=True)
            em.barrier()

            for h in range(NH):
                bank_fill_prep([0, 1, 2, 3])
                for j in range(RT):
                    def fr(e, _h=h, _j=j):
                        return e.matmul(PS[:, _j // 2,
                                    (_j % 2) * 256:(_j % 2) * 256 + 129],
                                 sigq[:, _h, _j * 128:(_j + 1) * 128],
                                 mz[:, _h, :], start=True, stop=True)
                    t_op(fr, inc=(j == RT - 1))
                bank_filled([0, 1, 2, 3])
                for b in (0, 1, 2, 3):
                    consume_bank("v", b)
                for j in range(RT):
                    def fd(e, _j=j):
                        return e.tensor_scalar(
                            tp2[:, _j, 0:1],
                            PS[:, _j // 2,
                               (_j % 2) * 256 + 128:(_j % 2) * 256 + 129],
                            epsatt[:, :], None, ALU.add)
                    v_op(fd)
                for j in range(RT):
                    v_op(lambda e, _j=j: e.reciprocal(tp2[:, _j, 1:2],
                                                      tp2[:, _j, 0:1]))
                for j in range(RT):
                    def fsc(e, _j=j):
                        return e.tensor_scalar(
                            retrst[:, _j, :],
                            PS[:, _j // 2, (_j % 2) * 256:(_j % 2) * 256 + 128],
                            tp2[:, _j, 1:2], None, ALU.mult)
                    v_op(fsc, inc=(j == RT - 1))
                bank_consumed([0, 1, 2, 3], "v")
                em.wait("t", "v", em.cnt["v"])
                bank_fill_prep([4, 5])
                for j in range(RT):
                    b = 4 + j // 4
                    off = (j % 4) * 128
                    def ftr(e, _j=j, _b=b, _off=off):
                        return e.matmul(PS[:, _b, _off:_off + 128], retrst[:, _j, :],
                                 ident_sb[:, :], start=True, stop=True)
                    t_op(ftr, inc=(j == RT - 1))
                bank_filled([4, 5])
                consume_bank("v", 4)
                consume_bank("v", 5)
                def fev(e, _h=h):
                    return e.tensor_copy(retrT[:, _h, :],
                                  PS[:, 4:6, :].rearrange("p a b -> p (a b)"))
                v_op(fev, inc=True)
                bank_consumed([4, 5], "v")
            em.barrier()

            bo_entry_t = em.cnt["t"]
            wo_y = {}
            wo_free_t = {}
            xb_y = {}
            xbf_free_t = {}

            def issue_wo(oc):
                buf = oc % 2
                em.wait("g", "t", wo_free_t.get(oc - 2, bo_entry_t))
                em.wait("g", "wf0", fold_state["wf0"])
                em.wait("g", "wf1", fold_state["wf1"])
                em.wait("g", "v", fold_state["v"])
                sem = "wo%d" % buf
                def fw(e, _oc=oc, _buf=buf):
                    return e.dma_start(out=wof_ch[_buf][:, :, :],
                                in_=wofd.rearrange("h p o -> p h o")
                                [:, :, _oc * 512:(_oc + 1) * 512])
                wo_y[oc] = (sem, em.dma("g", sem, fw))

            def issue_x(oc):
                buf = oc % 2
                em.wait("g", "t", xbf_free_t.get(oc - 2, bo_entry_t))
                wait_stores("g", entry_st)
                sem = "xf%d" % buf
                last = 0
                for part in range(2):
                    a0 = part * 4
                    def fx(e, _oc=oc, _buf=buf, _a0=a0):
                        return e.dma_start(
                            out=xbfc[_buf][:, _a0:_a0 + 4, :],
                            in_=src[rnd * 8 + _a0:rnd * 8 + _a0 + 4, :,
                                    _oc * 512:(_oc + 1) * 512]
                            .rearrange("a p b -> p a b"))
                    last = em.dma("g", sem, fx)
                xb_y[oc] = (sem, last)

            issue_wo(0)
            issue_x(0)
            issue_wo(1)
            issue_x(1)

            for oc in range(NOC):
                buf = oc % 2
                em.wait("t", wo_y[oc][0], wo_y[oc][1])
                em.wait("t", xb_y[oc][0], xb_y[oc][1])
                for th in range(2):
                    bs = [4 * th + k for k in range(4)]
                    bank_fill_prep(bs)
                    for k in range(4):
                        def fi(e, _b=bs[k], _buf=buf, _tt=4 * th + k):
                            return e.matmul(PS[:, _b, :], ident_sb[:, :],
                                     xbfc[_buf][:, _tt, :], start=True,
                                     stop=False)
                        t_op(fi)
                    for kc in range(NH):
                        for k in range(4):
                            tt = 4 * th + k
                            def fm(e, _kc=kc, _b=bs[k], _buf=buf, _tt=tt):
                                return e.matmul(PS[:, _b, :],
                                         retrT[:, _kc, _tt * 128:(_tt + 1) * 128],
                                         wof_ch[_buf][:, _kc, :],
                                         start=False, stop=(_kc == NH - 1))
                            t_op(fm, inc=(kc == NH - 1 and k == 3))
                    bank_filled(bs)
                    for k in range(4):
                        tt = 4 * th + k
                        b = bs[k]
                        sl = next_slot()
                        bf = (dst.dtype == BF16)
                        tgt = outb_bf[sl] if bf else outb[:, sl, :]
                        em.wait("s", "as%d" % sl, slot_state["free"][sl])
                        consume_bank("s", b)
                        def fe(e, _b=b, _tgt=tgt):
                            return e.activation(_tgt, PS[:, _b, :], AF.Copy)
                        s_op(fe, inc=True)
                        bank_consumed([b], "s")
                        em.wait("a", "s", em.cnt["s"])
                        def fst(e, _tgt=tgt, _tt=rnd * 8 + tt, _oc=oc):
                            return e.dma_start(out=dst[_tt, :, _oc * 512:(_oc + 1) * 512],
                                        in_=_tgt)
                        slot_state["free"][sl] = em.dma("a", "as%d" % sl, fst)
                wo_free_t[oc] = em.mark("t")
                xbf_free_t[oc] = wo_free_t[oc]
                if oc + 2 < NOC:
                    issue_wo(oc + 2)
                    issue_x(oc + 2)

        p1_dst = x1d if phases >= 2 else outd
        wbbf_saved = [0]
        for rnd in range(2):
            stage_A(rnd, xd, ffn=True, fold_hook=(rnd == 0))
            em.barrier()
            ffn_B(rnd, xd, p1_dst, wsrc=("f32" if rnd == 0 else "bf16"),
                  save=(rnd == 0))
            if rnd == 0:
                wbbf_saved[0] = em.cnt["wbs"]
            em.barrier()

        if phases >= 2:
            p2_dst = x2d if phases >= 3 else outd
            p1_end_t = em.cnt["t"]
            wq_last = 0
            for rnd in range(2):
                stage_A(rnd, x1d, ffn=False)
                if rnd == 0:
                    em.wait("g", "t", p1_end_t)
                    for part in range(8):
                        k0 = part * 4
                        def fwq(e, _k0=k0):
                            return e.dma_start(out=wbig[:, _k0:_k0 + 4, :],
                                        in_=wqT.rearrange("k p o -> p k o")
                                        [:, _k0:_k0 + 4, :])
                        wq_last = em.dma("g", "gsm", fwq)
                em.barrier()
                em.wait("t", "gsm", wq_last)
                attn_B(rnd, x1d, p2_dst)
                em.barrier()

        if phases >= 3:
            for rnd in range(2):
                stage_A(rnd, x2d, ffn=True)
                em.barrier()
                ffn_B(rnd, x2d, outd)
                em.barrier()

        for k in ("as0", "as1", "as2"):
            em.wait("y", k, em.cnt[k])

        def replay(eng_obj, key):
            eng_obj._sems = sems
            for o in em.q[key]:
                ins = o.fn(eng_obj)
                for (s, n) in o.incs:
                    ins.then_inc(sems[s], n)

        @block.tensor
        def _(e):
            replay(e, "t")

        @block.vector
        def _(e):
            replay(e, "v")

        @block.scalar
        def _(e):
            replay(e, "s")

        @block.gpsimd
        def _(e):
            replay(e, "g")

        @block.sync
        def _(e):
            replay(e, "y")

    return nc


def _prep_inputs(inputs):
    f = np.ascontiguousarray
    x = inputs["L0_attn_hidden_states"]
    Wq = inputs["L0_attn_Wq"]
    bq = inputs["L0_attn_bq"]
    Wo = inputs["L0_attn_Wo"]
    M = inputs["L0_attn_M"]
    z = inputs["L0_attn_z"]
    Wb = inputs["L0_ffn_W_base"]
    bb = inputs["L0_ffn_b_base"]
    Wr = inputs["L0_ffn_W_router"]
    A = inputs["L0_ffn_A"]
    B = inputs["L0_ffn_B"]

    xs = np.asarray(x, np.float32).reshape(16384, D)
    wq_sel = np.asarray(Wq, np.float32).reshape(32, 128, D)[::4].reshape(1024, D)
    bq_sel = np.asarray(bq, np.float32).reshape(32, 128)[::4].reshape(8, 128)
    common = {
        "wbT": f(np.asarray(Wb, np.float32).T).reshape(KC, 128, D),
        "wqT": f(wq_sel.T).reshape(KC, 128, 1024),
        "woT": f(np.asarray(Wo, np.float32).T).reshape(KC, 128, D),
        "aT": f(np.asarray(A, np.float32).reshape(128, D).T).reshape(KC, 128, 128),
        "bfT": f(np.asarray(B, np.float32).transpose(0, 2, 1).reshape(128, D)),
        "wrT": f(np.asarray(Wr, np.float32).T).reshape(KC, 128, 8),
        "bb": f(np.asarray(bb, np.float32).reshape(1, D)),
        "bq": f(bq_sel.T),
        "M": f(np.asarray(M, np.float32)),
        "zT": f(np.asarray(z, np.float32).T),
        "ident": np.eye(128, dtype=ml_dtypes.bfloat16),
        "ident2": (2 * np.eye(128)).astype(ml_dtypes.bfloat16),
        "ones_row": np.ones((1, 128), ml_dtypes.bfloat16),
    }
    shards = []
    for c in range(8):
        m = dict(common)
        m["x0"] = f(xs[c * 2048:(c + 1) * 2048].reshape(NT, 128, D))
        shards.append(m)
    return shards


def kernel(**inputs):
    shards = _prep_inputs(inputs)
    nc = build_program(phases=3)
    res = run_bass_kernel_spmd(nc, shards, core_ids=list(range(8)))
    out = np.concatenate(
        [np.asarray(r["out"]).reshape(2048, D) for r in res.results], axis=0)
    return np.ascontiguousarray(out.reshape(4, 4096, 4096)).astype(np.float32)


# version 09
# speedup vs baseline: 1.0679x; 1.0124x over previous
import numpy as np
import ml_dtypes
import concourse.bass as bass
import concourse.mybir as mybir
from concourse.bass_utils import run_bass_kernel_spmd

F32 = mybir.dt.float32
BF16 = mybir.dt.bfloat16
AF = mybir.ActivationFunctionType
ALU = mybir.AluOpType

NT = 16
D = 4096
KC = 32
NOC = 8
RT = 8
NH = 8
HD = 128
EPS_LN = 1e-5
EPS_ATT = 1e-6

STREAMS = ("t", "v", "s", "g", "y")
DMA_SEMS = ("xb0", "xb1", "wb0", "wb1", "xf0", "xf1", "wo0", "wo1",
            "gsm", "ysm", "as0", "as1", "as2", "wbs", "wf0", "wf1")
ENGS = ("t", "v", "s") + DMA_SEMS
STREAM_OF = {"t": "t", "v": "v", "s": "s", "g": "g", "y": "y", "a": "s"}


class Op:
    __slots__ = ("fn", "incs", "is_wait")
    def __init__(self, fn, is_wait=False):
        self.fn = fn
        self.incs = []
        self.is_wait = is_wait


class Em:
    def __init__(self):
        self.q = {e: [] for e in STREAMS}
        self.cnt = {e: 0 for e in ENGS}
        self.lastwait = {e: {s: 0 for s in ENGS} for e in STREAMS}

    def op(self, eng, fn, inc=False):
        o = Op(fn)
        self.q[STREAM_OF[eng]].append(o)
        if inc:
            self._attach(eng, o)
        return self.cnt[eng]

    def _attach(self, sem, o):
        n = 16 if sem in DMA_SEMS else 1
        o.incs.append((sem, n))
        self.cnt[sem] += n

    def dma(self, queue, sem, fn):
        o = Op(fn)
        self.q[STREAM_OF[queue]].append(o)
        self._attach(sem, o)
        return self.cnt[sem]

    def wait(self, eng, sem, val=None):
        st = STREAM_OF[eng]
        if val is None:
            val = self.cnt[sem]
        if val <= 0 or val <= self.lastwait[st][sem]:
            return
        self.lastwait[st][sem] = val
        def f(e, _s=sem, _v=val):
            return e.wait_ge(e._sems[_s], _v)
        self.q[st].append(Op(f, is_wait=True))

    def mark(self, eng):
        st = STREAM_OF[eng]
        for o in reversed(self.q[st]):
            if o.is_wait:
                continue
            if not o.incs:
                self._attach(eng, o)
            return self.cnt[eng]
        return self.cnt[eng]

    def barrier(self, engs=("t", "v", "s")):
        counts = {e: self.mark(e) for e in engs}
        for e in engs:
            for other in engs:
                if other != e:
                    self.wait(e, other, counts[other])


class Bank:
    __slots__ = ("fill", "cons")
    def __init__(self):
        self.fill = 0
        self.cons = []


def build_program(phases=3):
    nc = bass.Bass("TRN2", target_bir_lowering=False, detect_race_conditions=False)

    xd = nc.dram_tensor("x0", [NT, 128, D], F32, kind="ExternalInput").ap()
    wbT = nc.dram_tensor("wbT", [KC, 128, D], F32, kind="ExternalInput").ap()
    wqT = nc.dram_tensor("wqT", [KC, 128, NH * HD], F32, kind="ExternalInput").ap()
    woT = nc.dram_tensor("woT", [KC, 128, D], F32, kind="ExternalInput").ap()
    aT = nc.dram_tensor("aT", [KC, 128, 128], F32, kind="ExternalInput").ap()
    bfT = nc.dram_tensor("bfT", [128, D], F32, kind="ExternalInput").ap()
    wrT = nc.dram_tensor("wrT", [KC, 128, NH], F32, kind="ExternalInput").ap()
    bbd = nc.dram_tensor("bb", [1, D], F32, kind="ExternalInput").ap()
    bqd = nc.dram_tensor("bq", [128, NH], F32, kind="ExternalInput").ap()
    Md = nc.dram_tensor("M", [NH, 128, HD], F32, kind="ExternalInput").ap()
    zTd = nc.dram_tensor("zT", [128, NH], F32, kind="ExternalInput").ap()
    identd = nc.dram_tensor("ident", [128, 128], BF16, kind="ExternalInput").ap()
    ident2d = nc.dram_tensor("ident2", [128, 128], BF16, kind="ExternalInput").ap()
    onesd = nc.dram_tensor("ones_row", [1, 128], BF16, kind="ExternalInput").ap()

    outd = nc.dram_tensor("out", [NT, 128, D], F32, kind="ExternalOutput").ap()
    x1d = nc.dram_tensor("x1s", [NT, 128, D], BF16, kind="Internal").ap()
    x2d = nc.dram_tensor("x2s", [NT, 128, D], BF16, kind="Internal").ap()
    wofd = nc.dram_tensor("wofd", [NH, 128, D], BF16, kind="Internal").ap()
    wbbf = nc.dram_tensor("wbbf", [KC, 128, D], BF16, kind="Internal").ap()

    em = Em()
    banks = [Bank() for _ in range(8)]

    from contextlib import ExitStack
    with ExitStack() as _st:
        def SB(name, shape, dt):
            return _st.enter_context(nc.sbuf_tensor(name, shape, dt))
        ident_sb = SB("ident_sb", [128, 128], BF16)
        ident2_sb = SB("ident2_sb", [128, 128], BF16)
        ones_sb = SB("ones_sb", [1, 128], BF16)
        ones16 = SB("ones16", [128, 16], F32)
        epsln = SB("epsln", [128, 1], F32)
        epsatt = SB("epsatt", [128, 1], F32)
        awrT = SB("awrT", [128, KC, 136], BF16)
        bfT_sb = SB("bfT_sb", [128, D], BF16)
        mz = SB("mz", [128, NH, 129], BF16)
        bbrep = SB("bbrep", [128, D], BF16)
        bq_sb = SB("bq_sb", [128, NH], F32)
        yT = SB("yT", [128, KC, 1024], BF16)
        wbig = SB("wbig", [128, KC, 1024], BF16)
        xb = SB("xb", [128, 2, D], BF16)
        ybf = SB("ybf", [128, D], BF16)
        gT = SB("gT", [128, 1024], BF16)
        wexp = SB("wexp", [128, 2, 128], BF16)
        stats = SB("stats", [128, RT, 8], F32)
        ltok = SB("ltok", [128, RT, 8], F32)
        mx8 = SB("mx8", [128, RT, 8], F32)
        tp2 = SB("tp2", [128, RT, 4], F32)
        msk = SB("msk", [128, RT, 8], F32)
        e1b = SB("e1b", [128, RT, 8], F32)
        scr = SB("scr", [128, 8192], BF16)
        outb = SB("outb", [128, 3, 512], F32)
        retrst = SB("retrst", [128, NH, 128], BF16)
        PS = _st.enter_context(nc.psum_tensor("PS", [128, 8, 512], F32))
        _semnames = ("t", "v", "s") + DMA_SEMS
        sems = {nm: _st.enter_context(nc.semaphore("sem_" + nm))
                for nm in _semnames}
        block = _st.enter_context(nc.Block())

        outb_bf = [outb[:, i, :].bitcast(BF16)[:, 0:512] for i in range(3)]
        sigq = scr[:, :].rearrange("p (h t) -> p h t", h=NH)
        xbfc = [scr[:, i * 4096:(i + 1) * 4096].rearrange("p (a b) -> p a b", a=8)
                for i in range(2)]
        dumpA = scr[:, 0:4096]
        ybf_odd = scr[:, 4096:8192]
        sqdump = ybf[:, 0:2048]
        spart = ybf[:, 2048:4096].bitcast(F32).rearrange(
            "p (r t c) -> p (r t) c", r=2, t=RT)
        hT = scr[:, 4096:5120]
        wexpT = scr[:, 5120:6144]
        ltT = scr[0:8, 6144:7168]
        retrT = yT[:, 0:NH, :]
        wof_ch = [xb[:, i, :].rearrange("p (a b) -> p a b", a=8) for i in range(2)]
        wb_ch = [wbig[:, :, i * 512:(i + 1) * 512] for i in range(2)]

        def t_op(fn, inc=False):
            return em.op("t", fn, inc)

        def v_op(fn, inc=False):
            return em.op("v", fn, inc)

        def s_op(fn, inc=False):
            return em.op("s", fn, inc)

        def bank_fill_prep(bs):
            for b in bs:
                for (eng, val) in banks[b].cons:
                    em.wait("t", eng, val)

        def bank_filled(bs):
            c = em.mark("t")
            for b in bs:
                banks[b].fill = c
                banks[b].cons = []

        def consume_bank(eng, b):
            em.wait(eng, "t", banks[b].fill)

        def bank_consumed(bs, eng):
            c = em.mark(eng)
            for b in bs:
                banks[b].cons.append((eng, c))

        slot_state = {"i": 0, "free": [0, 0, 0]}
        slot_free_s = [0, 0, 0]

        def next_slot():
            sl = slot_state["i"] % 3
            slot_state["i"] += 1
            return sl

        em.dma("g", "gsm", lambda e: e.dma_start(out=awrT[:, :, 0:128],
                                          in_=aT.rearrange("k p o -> p k o")))
        em.dma("g", "gsm", lambda e: e.dma_start(out=awrT[:, :, 128:136],
                                          in_=wrT.rearrange("k p o -> p k o")))
        em.dma("g", "gsm", lambda e: e.dma_start(out=bfT_sb[:, :], in_=bfT[:, :]))
        em.dma("g", "gsm", lambda e: e.dma_start(out=mz[:, :, 0:128],
                                          in_=Md.rearrange("h p o -> p h o"),
                                          allow_slow_non_contiguous=True))
        em.dma("g", "gsm", lambda e: e.dma_start(out=mz[:, :, 128:129],
                                          in_=zTd.rearrange("p (h o) -> p h o", o=1),
                                          allow_slow_non_contiguous=True))
        em.dma("y", "ysm", lambda e: e.dma_start(out=ident_sb[:, :], in_=identd[:, :]))
        em.dma("y", "ysm", lambda e: e.dma_start(out=ident2_sb[:, :], in_=ident2d[:, :]))
        em.dma("y", "ysm", lambda e: e.dma_start(out=ones_sb[:, :], in_=onesd[:, :]))
        em.dma("y", "ysm", lambda e: e.dma_start(out=bq_sb[:, :], in_=bqd[:, :]))
        em.dma("g", "gsm", lambda e: e.dma_start(out=bbrep[0:1, :], in_=bbd[:, :]))
        gc0, yc0 = em.cnt["gsm"], em.cnt["ysm"]

        v_op(lambda e: e.memset(ones16[:, :], 1.0))
        v_op(lambda e: e.memset(epsln[:, :], EPS_LN))
        v_op(lambda e: e.memset(epsatt[:, :], EPS_ATT), inc=True)

        em.wait("t", "ysm", yc0)
        em.wait("t", "gsm", gc0)
        for oc in range(NOC):
            b = oc % 2
            bank_fill_prep([b])
            def f(e, _b=b, _oc=oc):
                return e.matmul(PS[:, _b, :], ones_sb[:, :],
                         bbrep[0:1, _oc * 512:(_oc + 1) * 512],
                         start=True, stop=True)
            t_op(f, inc=True)
            bank_filled([b])
            consume_bank("v", b)
            def fv(e, _b=b, _oc=oc):
                return e.tensor_scalar(bbrep[:, _oc * 512:(_oc + 1) * 512], PS[:, _b, :],
                                0.5, None, ALU.mult)
            v_op(fv, inc=True)
            bank_consumed([b], "v")

        fold_state = {"v": 0, "wf0": 0, "wf1": 0}
        fold_done_v = {}

        def wo_stage(k4):
            return wbig[:, 4 * k4:4 * k4 + 4, :].bitcast(F32)

        def _mk_fold_unit(u):
            h, half = u % NH, u // NH
            p = u % 2
            base = 0 if p == 0 else 4
            srcs = [wo_stage(base + g) for g in range(4)]
            dsem = "wb%d" % p
            fsem = "wf%d" % p
            def emit():
                if u >= 2:
                    em.wait("y", "v", fold_done_v[u - 2])
                    em.wait("y", fsem, fold_state[fsem])
                last = 0
                for g4 in range(4):
                    def fdl(e, _h=h, _g=g4, _half=half, _dst=srcs[g4]):
                        return e.dma_start(
                            out=_dst,
                            in_=woT[4 * _h + _g, :,
                                    _half * 2048:(_half + 1) * 2048]
                            .rearrange("p (a b) -> p a b", a=4))
                    last = em.dma("y", dsem, fdl)
                em.wait("v", dsem, last)
                v_op(lambda e, _a=srcs[0], _b=srcs[1], _o=srcs[0]:
                     e.tensor_add(_o, _a, _b))
                v_op(lambda e, _a=srcs[2], _b=srcs[3], _o=srcs[2]:
                     e.tensor_add(_o, _a, _b))
                v_op(lambda e, _a=srcs[0], _b=srcs[2], _o=srcs[0]:
                     e.tensor_add(_o, _a, _b), inc=True)
                fold_done_v[u] = em.cnt["v"]
                em.wait("g", "v", fold_done_v[u])
                def fst(e, _h=h, _half=half, _src=srcs[0]):
                    return e.dma_start(
                        out=wofd[_h, :, _half * 2048:(_half + 1) * 2048]
                        .rearrange("p (a b) -> p a b", a=4), in_=_src)
                em.dma("g", fsem, fst)
                fold_state[fsem] = em.cnt[fsem]
            return emit

        fold_units = [_mk_fold_unit(u) for u in range(2 * NH)]
        fold_units.reverse()

        def store_totals():
            return {k: em.cnt[k] for k in ("as0", "as1", "as2")}

        def wait_stores(eng, tot):
            for k, v in tot.items():
                em.wait(eng, k, v)

        g_lastnorm = {0: 0, 1: 0}

        def emit_ffn_tail():
            bank_fill_prep([0, 1, 2, 3])
            for kc in range(KC):
                for tc in range(2):
                    def fh(e, _kc=kc, _tc=tc):
                        return e.matmul(PS[:, _tc, :], awrT[:, _kc, 0:128],
                                 yT[:, _kc, _tc * 512:(_tc + 1) * 512],
                                 start=(_kc == 0), stop=(_kc == KC - 1))
                    t_op(fh)
                    def fr(e, _kc=kc, _tc=tc):
                        return e.matmul(PS[0:8, 2 + _tc, :], awrT[:, _kc, 128:136],
                                 yT[:, _kc, _tc * 512:(_tc + 1) * 512],
                                 start=(_kc == 0), stop=(_kc == KC - 1))
                    t_op(fr, inc=(kc == KC - 1 and tc == 1))
            bank_filled([0, 1, 2, 3])
            for b in (0, 1, 2, 3):
                consume_bank("v", b)
            v_op(lambda e: e.tensor_copy(
                hT[:, :], PS[:, 0:2, :].rearrange("p a b -> p (a b)")))
            v_op(lambda e: e.tensor_copy(
                ltT[:, :], PS[0:8, 2:4, :].rearrange("p a b -> p (a b)")),
                inc=True)
            bank_consumed([0, 1, 2, 3], "v")

            em.wait("t", "v", em.cnt["v"])
            bank_fill_prep([4])
            for j in range(RT):
                def fl(e, _j=j):
                    return e.matmul(PS[:, 4, _j * 8:(_j + 1) * 8],
                             ltT[:, _j * 128:(_j + 1) * 128],
                             ident_sb[0:8, 0:8], start=True, stop=True)
                t_op(fl, inc=(j == RT - 1))
            bank_filled([4])
            consume_bank("v", 4)
            v_op(lambda e: e.tensor_copy(
                ltok[:, :, :],
                PS[:, 4, 0:64].rearrange("p (a b) -> p a b", a=RT)), inc=True)
            bank_consumed([4], "v")

            for j in range(RT):
                v_op(lambda e, _j=j: e.max(out=mx8[:, _j, :], in_=ltok[:, _j, :]))
            for j in range(RT):
                v_op(lambda e, _j=j: e.tensor_scalar(
                    e1b[:, _j, :], ltok[:, _j, :], mx8[:, _j, 0:1], None,
                    ALU.subtract))
            for j in range(RT):
                v_op(lambda e, _j=j: e.tensor_scalar(
                    msk[:, _j, :], ltok[:, _j, :], mx8[:, _j, 1:2], None,
                    ALU.is_ge))
            for j in range(RT):
                v_op(lambda e, _j=j: e.tensor_scalar(
                    tp2[:, _j, 0:1], mx8[:, _j, 1:2], mx8[:, _j, 0:1], None,
                    ALU.subtract), inc=(j == RT - 1))
            em.wait("s", "v", em.cnt["v"])
            for j in range(RT):
                s_op(lambda e, _j=j: e.activation(e1b[:, _j, :], e1b[:, _j, :],
                                                  AF.Exp))
            for j in range(RT):
                s_op(lambda e, _j=j: e.activation(tp2[:, _j, 1:2],
                                                  tp2[:, _j, 0:1], AF.Exp),
                     inc=(j == RT - 1))
            em.wait("v", "s", em.cnt["s"])
            for j in range(RT):
                v_op(lambda e, _j=j: e.tensor_scalar(
                    tp2[:, _j, 2:3], tp2[:, _j, 1:2], 1.0, None, ALU.add))
            for j in range(RT):
                v_op(lambda e, _j=j: e.reciprocal(tp2[:, _j, 3:4], tp2[:, _j, 2:3]))
            for j in range(RT):
                v_op(lambda e, _j=j: e.tensor_mul(e1b[:, _j, :], e1b[:, _j, :],
                                                  msk[:, _j, :]))
            for j in range(RT):
                v_op(lambda e, _j=j: e.tensor_scalar(
                    e1b[:, _j, :], e1b[:, _j, :], tp2[:, _j, 3:4], 2.0,
                    ALU.mult, ALU.mult))
            for j in range(RT):
                wb_ = j % 2
                for ex in range(8):
                    v_op(lambda e, _j=j, _w=wb_, _ex=ex: e.tensor_scalar(
                        wexp[:, _w, _ex * 16:(_ex + 1) * 16], ones16[:, :],
                        e1b[:, _j, _ex:_ex + 1], None, ALU.mult),
                        inc=(ex == 7))
                em.wait("t", "v", em.cnt["v"])
                b = 5 + (j % 2)
                bank_fill_prep([b])
                def ft(e, _w=wb_, _b=b):
                    return e.matmul(PS[:, _b, 0:128], wexp[:, _w, :], ident_sb[:, :],
                             start=True, stop=True)
                t_op(ft, inc=True)
                bank_filled([b])
                consume_bank("v", b)
                v_op(lambda e, _j=j, _b=b: e.tensor_copy(
                    wexpT[:, _j * 128:(_j + 1) * 128], PS[:, _b, 0:128]),
                    inc=True)
                bank_consumed([b], "v")
            v_op(lambda e: e.tensor_mul(gT[:, :], hT[:, :], wexpT[:, :]),
                 inc=True)

        def stage_A(rnd, src, ffn, fold_hook=False):
            entry_t = em.cnt["t"]
            entry_st = store_totals()
            norm_cnt = {}
            tr_done = {}
            dma_cnt = {}
            sq_done = {}

            def emit_dma(i):
                tt = rnd * RT + i
                em.wait("g", "t", entry_t)
                wait_stores("g", entry_st)
                if i - 2 >= 0:
                    em.wait("g", "s", sq_done[i - 2])
                    em.wait("g", "v", norm_cnt[i - 2])
                sem = "xb%d" % (i % 2)
                def fx(e, _tt=tt, _bi=i % 2):
                    return e.dma_start(out=xb[:, _bi, :], in_=src[_tt, :, :])
                dma_cnt[i] = (sem, em.dma("g", sem, fx))

            emit_dma(0)
            emit_dma(1)
            for i in range(RT):
                em.wait("s", dma_cnt[i][0], dma_cnt[i][1])
                s_op(lambda e, _bi=i % 2, _i=i: e.activation(
                    dumpA[:, :], xb[:, _bi, :], AF.Copy,
                    accum_out=stats[:, _i, 0:1]))
                s_op(lambda e, _bi=i % 2, _i=i: e.activation(
                    dumpA[:, :], xb[:, _bi, :], AF.Square,
                    accum_out=stats[:, _i, 1:2]), inc=True)
                sq_done[i] = em.cnt["s"]
                s_op(lambda e, _i=i: e.activation(stats[:, _i, 2:3],
                                                  stats[:, _i, 0:1],
                                                  AF.Copy, scale=1.0 / D),
                     inc=True)
                em.wait("v", "s", em.cnt["s"])
                v_op(lambda e, _i=i: e.tensor_scalar(
                    stats[:, _i, 3:4], stats[:, _i, 2:3], stats[:, _i, 2:3],
                    -1.0, ALU.mult, ALU.mult), inc=True)
                em.wait("s", "v", em.cnt["v"])
                s_op(lambda e, _i=i: e.activation(stats[:, _i, 4:5],
                                                  stats[:, _i, 1:2], AF.Identity,
                                                  bias=stats[:, _i, 3:4],
                                                  scale=1.0 / D))
                s_op(lambda e: e.drain())
                s_op(lambda e, _i=i: e.activation(stats[:, _i, 5:6],
                                                  stats[:, _i, 4:5], AF.Ln,
                                                  bias=epsln[:, :]))
                s_op(lambda e: e.drain())
                s_op(lambda e, _i=i: e.activation(stats[:, _i, 6:7],
                                                  stats[:, _i, 5:6], AF.Exp,
                                                  scale=-0.5), inc=True)
                chain_done_i = em.cnt["s"]

                em.wait("v", "s", chain_done_i)
                if i >= 1:
                    em.wait("v", "t", tr_done[i - 1])
                yb = ybf_odd
                def fn(e, _bi=i % 2, _i=i, _yb=yb):
                    return e.tensor_scalar(_yb[:, :], xb[:, _bi, :],
                                    stats[:, _i, 2:3], stats[:, _i, 6:7],
                                    ALU.subtract, ALU.mult)
                v_op(fn, inc=True)
                norm_cnt[i] = em.cnt["v"]
                g_lastnorm[i % 2] = em.cnt["v"]

                em.wait("t", "v", norm_cnt[i])
                for half in range(2):
                    bs = [4 * half + k for k in range(4)]
                    bank_fill_prep(bs)
                    for k in range(16):
                        kc = 16 * half + k
                        b = bs[k // 4]
                        off = (k % 4) * 128
                        def fm(e, _kc=kc, _b=b, _off=off, _yb=yb):
                            return e.matmul(PS[:, _b, _off:_off + 128],
                                     _yb[:, _kc * 128:(_kc + 1) * 128],
                                     ident_sb[:, :], start=True, stop=True)
                        t_op(fm, inc=(k == 15))
                    bank_filled(bs)
                    for b in bs:
                        consume_bank("v", b)
                    kc0 = 16 * half
                    def fe(e, _kc0=kc0, _i=i, _h=half):
                        return e.tensor_copy(
                            yT[:, _kc0:_kc0 + 16, _i * 128:(_i + 1) * 128],
                            PS[:, _h * 4:_h * 4 + 4, :]
                            .rearrange("p a (c d) -> p (a c) d", d=128))
                    v_op(fe, inc=True)
                    bank_consumed(bs, "v")
                tr_done[i] = em.cnt["t"]

                if i + 2 < RT:
                    emit_dma(i + 2)
                if fold_hook:
                    em.wait("y", "t", entry_t)
                    for _ in range(2):
                        if fold_units:
                            fold_units.pop()()

            if not ffn:
                return

            emit_ffn_tail()

        def stage_A_fast(rnd, src, ffn, sq_prod_s):
            entry_t = em.cnt["t"]
            entry_st = store_totals()
            em.wait("v", "s", sq_prod_s)
            for i in range(RT):
                ps_ = (rnd % 2) * RT + i
                v_op(lambda e, _p=ps_, _i=i: e.tensor_scalar(
                    mx8[:, _i, :], spart[:, _p, 0:8], 1.0, None,
                    ALU.mult, ALU.add, accum_out=stats[:, _i, 0:1]))
            for i in range(RT):
                ps_ = (rnd % 2) * RT + i
                v_op(lambda e, _p=ps_, _i=i: e.tensor_scalar(
                    mx8[:, _i, :], spart[:, _p, 8:16], 1.0, None,
                    ALU.mult, ALU.add, accum_out=stats[:, _i, 1:2]),
                    inc=(i == RT - 1))
            em.wait("s", "v", em.cnt["v"])
            for i in range(RT):
                s_op(lambda e, _i=i: e.activation(stats[:, _i, 2:3],
                                                  stats[:, _i, 0:1],
                                                  AF.Copy, scale=1.0 / D),
                     inc=(i == RT - 1))
            em.wait("v", "s", em.cnt["s"])
            for i in range(RT):
                v_op(lambda e, _i=i: e.tensor_scalar(
                    stats[:, _i, 3:4], stats[:, _i, 2:3], stats[:, _i, 2:3],
                    -1.0, ALU.mult, ALU.mult), inc=(i == RT - 1))
            em.wait("s", "v", em.cnt["v"])
            for i in range(RT):
                s_op(lambda e, _i=i: e.activation(stats[:, _i, 4:5],
                                                  stats[:, _i, 1:2], AF.Identity,
                                                  bias=stats[:, _i, 3:4],
                                                  scale=1.0 / D))
            for i in range(RT):
                s_op(lambda e, _i=i: e.activation(stats[:, _i, 5:6],
                                                  stats[:, _i, 4:5], AF.Ln,
                                                  bias=epsln[:, :]))
            for i in range(RT):
                s_op(lambda e, _i=i: e.activation(stats[:, _i, 6:7],
                                                  stats[:, _i, 5:6], AF.Exp,
                                                  scale=-0.5), inc=(i == RT - 1))
            chain_all = em.cnt["s"]

            dma_cnt = {}
            tr_done = {}

            def emit_dma(i):
                tt = rnd * RT + i
                em.wait("g", "t", entry_t)
                wait_stores("g", entry_st)
                em.wait("g", "v", g_lastnorm[i % 2])
                sem = "xb%d" % (i % 2)
                def fx(e, _tt=tt, _bi=i % 2):
                    return e.dma_start(out=xb[:, _bi, :], in_=src[_tt, :, :])
                dma_cnt[i] = (sem, em.dma("g", sem, fx))

            emit_dma(0)
            emit_dma(1)
            for i in range(RT):
                em.wait("v", "s", chain_all)
                em.wait("v", dma_cnt[i][0], dma_cnt[i][1])
                if i >= 1:
                    em.wait("v", "t", tr_done[i - 1])
                def fn(e, _bi=i % 2, _i=i):
                    return e.tensor_scalar(ybf_odd[:, :], xb[:, _bi, :],
                                    stats[:, _i, 2:3], stats[:, _i, 6:7],
                                    ALU.subtract, ALU.mult)
                v_op(fn, inc=True)
                g_lastnorm[i % 2] = em.cnt["v"]
                em.wait("t", "v", em.cnt["v"])
                for half in range(2):
                    bs = [4 * half + k for k in range(4)]
                    bank_fill_prep(bs)
                    for k in range(16):
                        kc = 16 * half + k
                        b = bs[k // 4]
                        off = (k % 4) * 128
                        def fm(e, _kc=kc, _b=b, _off=off):
                            return e.matmul(PS[:, _b, _off:_off + 128],
                                     ybf_odd[:, _kc * 128:(_kc + 1) * 128],
                                     ident_sb[:, :], start=True, stop=True)
                        t_op(fm, inc=(k == 15))
                    bank_filled(bs)
                    for b in bs:
                        consume_bank("v", b)
                    kc0 = 16 * half
                    def fe(e, _kc0=kc0, _i=i, _h=half):
                        return e.tensor_copy(
                            yT[:, _kc0:_kc0 + 16, _i * 128:(_i + 1) * 128],
                            PS[:, _h * 4:_h * 4 + 4, :]
                            .rearrange("p a (c d) -> p (a c) d", d=128))
                    v_op(fe, inc=True)
                    bank_consumed(bs, "v")
                tr_done[i] = em.cnt["t"]
                if i + 2 < RT:
                    emit_dma(i + 2)

            if not ffn:
                return
            emit_ffn_tail()

        def ffn_B(rnd, src, dst, wsrc="bf16", save=False, feed_stats=False):
            gT_ready_v = em.cnt["v"]
            entry_t = em.cnt["t"]
            entry_s = em.cnt["s"]
            entry_st = store_totals()
            wb_g = {}
            wb_free_t = {}
            xb_y = {}
            xbf_free_t = {}

            def issue_w(oc):
                buf = oc % 2
                q = "g" if wsrc == "f32" else "y"
                em.wait(q, "t", wb_free_t.get(oc - 2, entry_t))
                em.wait(q, "wf0", fold_state["wf0"])
                em.wait(q, "wf1", fold_state["wf1"])
                em.wait(q, "v", fold_done_v.get(2 * NH - 1, 0))

                if wsrc == "bf16":
                    em.wait(q, "wbs", wbbf_saved[0])
                sem = "wb%d" % buf
                srcw = wbT if wsrc == "f32" else wbbf
                last = 0
                for part in range(4):
                    k0 = part * 8
                    def fw(e, _oc=oc, _buf=buf, _k0=k0, _srcw=srcw):
                        return e.dma_start(
                            out=wb_ch[_buf][:, _k0:_k0 + 8, :],
                            in_=_srcw.rearrange("k p o -> p k o")
                            [:, _k0:_k0 + 8, _oc * 512:(_oc + 1) * 512])
                    last = em.dma(q, sem, fw)
                wb_g[oc] = (sem, last)
                if save:
                    em.wait("a", sem, last)
                    for part in range(2):
                        k0 = part * 16
                        def fsv(e, _oc=oc, _buf=buf, _k0=k0):
                            return e.dma_start(
                                out=wbbf.rearrange("k p o -> p k o")
                                [:, _k0:_k0 + 16, _oc * 512:(_oc + 1) * 512],
                                in_=wb_ch[_buf][:, _k0:_k0 + 16, :])
                        em.dma("a", "wbs", fsv)

            def issue_x(oc):
                buf = oc % 2
                em.wait("g", "t", xbf_free_t.get(oc - 2, entry_t))
                em.wait("g", "s", entry_s)
                em.wait("g", "v", gT_ready_v)
                wait_stores("g", entry_st)
                sem = "xf%d" % buf
                last = 0
                for part in range(2):
                    a0 = part * 4
                    def fx(e, _oc=oc, _buf=buf, _a0=a0):
                        return e.dma_start(
                            out=xbfc[_buf][:, _a0:_a0 + 4, :],
                            in_=src[rnd * 8 + _a0:rnd * 8 + _a0 + 4, :,
                                    _oc * 512:(_oc + 1) * 512]
                            .rearrange("a p b -> p a b"))
                    last = em.dma("g", sem, fx)
                xb_y[oc] = (sem, last)

            issue_w(0)
            issue_x(0)
            issue_w(1)
            issue_x(1)

            for oc in range(NOC):
                buf = oc % 2
                em.wait("t", wb_g[oc][0], wb_g[oc][1])
                em.wait("t", xb_y[oc][0], xb_y[oc][1])
                em.wait("t", "v", gT_ready_v)
                for th in range(2):
                    bs = [4 * th + k for k in range(4)]
                    bank_fill_prep(bs)
                    for k in range(4):
                        def fi(e, _b=bs[k], _buf=buf, _tt=4 * th + k):
                            return e.matmul(PS[:, _b, :], ident2_sb[:, :],
                                     xbfc[_buf][:, _tt, :], start=True,
                                     stop=False)
                        t_op(fi)
                    for kc in range(KC):
                        for k in range(4):
                            tt = 4 * th + k
                            def fm(e, _kc=kc, _b=bs[k], _buf=buf, _tt=tt):
                                return e.matmul(PS[:, _b, :],
                                         yT[:, _kc, _tt * 128:(_tt + 1) * 128],
                                         wb_ch[_buf][:, _kc, :],
                                         start=False, stop=False)
                            t_op(fm)
                    for k in range(4):
                        tt = 4 * th + k
                        def fl(e, _b=bs[k], _tt=tt, _oc=oc):
                            return e.matmul(PS[:, _b, :],
                                     gT[:, _tt * 128:(_tt + 1) * 128],
                                     bfT_sb[:, _oc * 512:(_oc + 1) * 512],
                                     start=False, stop=True)
                        t_op(fl, inc=(k == 3))
                    bank_filled(bs)
                    for k in range(4):
                        tt = 4 * th + k
                        b = bs[k]
                        sl = next_slot()
                        bf = (dst.dtype == BF16)
                        tgt = outb_bf[sl] if bf else outb[:, sl, :]
                        em.wait("v", "as%d" % sl, slot_state["free"][sl])
                        em.wait("v", "s", slot_free_s[sl])
                        consume_bank("v", b)
                        pslot = (rnd % 2) * RT + tt
                        acc = (spart[:, pslot, oc:oc + 1] if feed_stats else None)
                        def fe(e, _b=b, _tgt=tgt, _oc=oc, _acc=acc):
                            return e.scalar_tensor_tensor(
                                _tgt, PS[:, _b, :], 0.5,
                                bbrep[:, _oc * 512:(_oc + 1) * 512],
                                ALU.mult, ALU.add, accum_out=_acc)
                        v_op(fe, inc=True)
                        bank_consumed([b], "v")
                        if feed_stats:
                            em.wait("s", "v", em.cnt["v"])
                            def fsq(e, _tgt=tgt, _pslot=pslot, _oc=oc):
                                return e.activation(
                                    sqdump[:, 0:512], _tgt, AF.Square,
                                    accum_out=spart[:, _pslot, 8 + _oc:9 + _oc])
                            s_op(fsq, inc=True)
                            slot_free_s[sl] = em.cnt["s"]
                        em.wait("a", "v", em.cnt["v"])
                        def fst(e, _tgt=tgt, _tt=rnd * 8 + tt, _oc=oc):
                            return e.dma_start(out=dst[_tt, :, _oc * 512:(_oc + 1) * 512],
                                        in_=_tgt)
                        slot_state["free"][sl] = em.dma("a", "as%d" % sl, fst)
                wb_free_t[oc] = em.mark("t")
                xbf_free_t[oc] = wb_free_t[oc]
                if oc + 2 < NOC:
                    issue_w(oc + 2)
                    issue_x(oc + 2)


        def attn_B(rnd, src, dst, feed_stats=False):
            entry_t = em.cnt["t"]
            entry_s = em.cnt["s"]
            entry_v = em.cnt["v"]
            entry_st = store_totals()
            for sl in range(3):
                em.wait("v", "as%d" % sl, slot_state["free"][sl])
            for ocq in range(8):
                bs = [2 * (ocq % 2), 2 * (ocq % 2) + 1]
                bank_fill_prep(bs)
                for kc in range(KC):
                    for tc in range(2):
                        def fq(e, _kc=kc, _ocq=ocq, _tc=tc, _b=bs[tc]):
                            return e.matmul(PS[:, _b, :],
                                     wbig[:, _kc, _ocq * 128:(_ocq + 1) * 128],
                                     yT[:, _kc, _tc * 512:(_tc + 1) * 512],
                                     start=(_kc == 0), stop=(_kc == KC - 1))
                        t_op(fq, inc=(kc == KC - 1 and tc == 1))
                bank_filled(bs)
                for tc in range(2):
                    b = bs[tc]
                    consume_bank("v", b)
                    def f1(e, _b=b, _ocq=ocq):
                        return e.tensor_scalar(outb[:, 0, :], PS[:, _b, :],
                                        bq_sb[:, _ocq:_ocq + 1], 0.0,
                                        ALU.add, ALU.max)
                    v_op(f1)
                    def f2(e, _b=b, _ocq=ocq):
                        return e.tensor_scalar(outb[:, 1, :], PS[:, _b, :],
                                        bq_sb[:, _ocq:_ocq + 1], 0.0,
                                        ALU.add, ALU.min)
                    v_op(f2, inc=True)
                    bank_consumed([b], "v")
                    em.wait("s", "v", em.cnt["v"])
                    s_op(lambda e: e.activation(outb[:, 2, :], outb[:, 1, :],
                                                AF.Exp), inc=True)
                    em.wait("v", "s", em.cnt["s"])
                    def f3(e, _ocq=ocq, _tc=tc):
                        return e.tensor_add(sigq[:, _ocq, _tc * 512:(_tc + 1) * 512],
                                     outb[:, 0, :], outb[:, 2, :])
                    v_op(f3, inc=True)
            em.barrier()

            for h in range(NH):
                bank_fill_prep([0, 1, 2, 3])
                for j in range(RT):
                    def fr(e, _h=h, _j=j):
                        return e.matmul(PS[:, _j // 2,
                                    (_j % 2) * 256:(_j % 2) * 256 + 129],
                                 sigq[:, _h, _j * 128:(_j + 1) * 128],
                                 mz[:, _h, :], start=True, stop=True)
                    t_op(fr, inc=(j == RT - 1))
                bank_filled([0, 1, 2, 3])
                for b in (0, 1, 2, 3):
                    consume_bank("v", b)
                for j in range(RT):
                    def fd(e, _j=j):
                        return e.tensor_scalar(
                            tp2[:, _j, 0:1],
                            PS[:, _j // 2,
                               (_j % 2) * 256 + 128:(_j % 2) * 256 + 129],
                            epsatt[:, :], None, ALU.add)
                    v_op(fd)
                for j in range(RT):
                    v_op(lambda e, _j=j: e.reciprocal(tp2[:, _j, 1:2],
                                                      tp2[:, _j, 0:1]))
                for j in range(RT):
                    def fsc(e, _j=j):
                        return e.tensor_scalar(
                            retrst[:, _j, :],
                            PS[:, _j // 2, (_j % 2) * 256:(_j % 2) * 256 + 128],
                            tp2[:, _j, 1:2], None, ALU.mult)
                    v_op(fsc, inc=(j == RT - 1))
                bank_consumed([0, 1, 2, 3], "v")
                em.wait("t", "v", em.cnt["v"])
                bank_fill_prep([4, 5])
                for j in range(RT):
                    b = 4 + j // 4
                    off = (j % 4) * 128
                    def ftr(e, _j=j, _b=b, _off=off):
                        return e.matmul(PS[:, _b, _off:_off + 128], retrst[:, _j, :],
                                 ident_sb[:, :], start=True, stop=True)
                    t_op(ftr, inc=(j == RT - 1))
                bank_filled([4, 5])
                consume_bank("v", 4)
                consume_bank("v", 5)
                def fev(e, _h=h):
                    return e.tensor_copy(retrT[:, _h, :],
                                  PS[:, 4:6, :].rearrange("p a b -> p (a b)"))
                v_op(fev, inc=True)
                bank_consumed([4, 5], "v")
            em.barrier()

            bo_entry_t = em.cnt["t"]
            wo_y = {}
            wo_free_t = {}
            xb_y = {}
            xbf_free_t = {}

            def issue_wo(oc):
                buf = oc % 2
                em.wait("g", "t", wo_free_t.get(oc - 2, bo_entry_t))
                em.wait("g", "wf0", fold_state["wf0"])
                em.wait("g", "wf1", fold_state["wf1"])
                em.wait("g", "v", fold_state["v"])
                sem = "wo%d" % buf
                def fw(e, _oc=oc, _buf=buf):
                    return e.dma_start(out=wof_ch[_buf][:, :, :],
                                in_=wofd.rearrange("h p o -> p h o")
                                [:, :, _oc * 512:(_oc + 1) * 512])
                wo_y[oc] = (sem, em.dma("g", sem, fw))

            def issue_x(oc):
                buf = oc % 2
                em.wait("g", "t", xbf_free_t.get(oc - 2, bo_entry_t))
                wait_stores("g", entry_st)
                sem = "xf%d" % buf
                last = 0
                for part in range(2):
                    a0 = part * 4
                    def fx(e, _oc=oc, _buf=buf, _a0=a0):
                        return e.dma_start(
                            out=xbfc[_buf][:, _a0:_a0 + 4, :],
                            in_=src[rnd * 8 + _a0:rnd * 8 + _a0 + 4, :,
                                    _oc * 512:(_oc + 1) * 512]
                            .rearrange("a p b -> p a b"))
                    last = em.dma("g", sem, fx)
                xb_y[oc] = (sem, last)

            issue_wo(0)
            issue_x(0)
            issue_wo(1)
            issue_x(1)

            for oc in range(NOC):
                buf = oc % 2
                em.wait("t", wo_y[oc][0], wo_y[oc][1])
                em.wait("t", xb_y[oc][0], xb_y[oc][1])
                for th in range(2):
                    bs = [4 * th + k for k in range(4)]
                    bank_fill_prep(bs)
                    for k in range(4):
                        def fi(e, _b=bs[k], _buf=buf, _tt=4 * th + k):
                            return e.matmul(PS[:, _b, :], ident_sb[:, :],
                                     xbfc[_buf][:, _tt, :], start=True,
                                     stop=False)
                        t_op(fi)
                    for kc in range(NH):
                        for k in range(4):
                            tt = 4 * th + k
                            def fm(e, _kc=kc, _b=bs[k], _buf=buf, _tt=tt):
                                return e.matmul(PS[:, _b, :],
                                         retrT[:, _kc, _tt * 128:(_tt + 1) * 128],
                                         wof_ch[_buf][:, _kc, :],
                                         start=False, stop=(_kc == NH - 1))
                            t_op(fm, inc=(kc == NH - 1 and k == 3))
                    bank_filled(bs)
                    for k in range(4):
                        tt = 4 * th + k
                        b = bs[k]
                        sl = next_slot()
                        bf = (dst.dtype == BF16)
                        tgt = outb_bf[sl] if bf else outb[:, sl, :]
                        em.wait("s", "as%d" % sl, slot_state["free"][sl])
                        consume_bank("s", b)
                        pslot = (rnd % 2) * RT + tt
                        acc = spart[:, pslot, oc:oc + 1] if feed_stats else None
                        def fe(e, _b=b, _tgt=tgt, _acc=acc):
                            return e.activation(_tgt, PS[:, _b, :], AF.Copy,
                                                accum_out=_acc)
                        s_op(fe)
                        bank_consumed([b], "s")
                        if feed_stats:
                            def fsq(e, _tgt=tgt, _pslot=pslot, _oc=oc):
                                return e.activation(
                                    sqdump[:, 0:512], _tgt, AF.Square,
                                    accum_out=spart[:, _pslot, 8 + _oc:9 + _oc])
                            s_op(fsq)
                        em.mark("s")
                        em.wait("a", "s", em.cnt["s"])
                        def fst(e, _tgt=tgt, _tt=rnd * 8 + tt, _oc=oc):
                            return e.dma_start(out=dst[_tt, :, _oc * 512:(_oc + 1) * 512],
                                        in_=_tgt)
                        slot_state["free"][sl] = em.dma("a", "as%d" % sl, fst)
                wo_free_t[oc] = em.mark("t")
                xbf_free_t[oc] = wo_free_t[oc]
                if oc + 2 < NOC:
                    issue_wo(oc + 2)
                    issue_x(oc + 2)

        p1_dst = x1d if phases >= 2 else outd
        wbbf_saved = [0]
        for rnd in range(2):
            stage_A(rnd, xd, ffn=True, fold_hook=(rnd == 0))
            em.barrier()
            ffn_B(rnd, xd, p1_dst, wsrc=("f32" if rnd == 0 else "bf16"),
                  save=(rnd == 0), feed_stats=(phases >= 2))
            if rnd == 0:
                wbbf_saved[0] = em.cnt["wbs"]
            em.barrier()

        if phases >= 2:
            p2_dst = x2d if phases >= 3 else outd
            p1_end_t = em.cnt["t"]
            wq_last = 0
            for rnd in range(2):
                stage_A_fast(rnd, x1d, ffn=False, sq_prod_s=em.cnt["s"])
                if rnd == 0:
                    em.wait("g", "t", p1_end_t)
                    for part in range(8):
                        k0 = part * 4
                        def fwq(e, _k0=k0):
                            return e.dma_start(out=wbig[:, _k0:_k0 + 4, :],
                                        in_=wqT.rearrange("k p o -> p k o")
                                        [:, _k0:_k0 + 4, :])
                        wq_last = em.dma("g", "gsm", fwq)
                em.barrier()
                em.wait("t", "gsm", wq_last)
                attn_B(rnd, x1d, p2_dst, feed_stats=(phases >= 3))
                em.barrier()

        if phases >= 3:
            for rnd in range(2):
                stage_A_fast(rnd, x2d, ffn=True, sq_prod_s=em.cnt["s"])
                em.barrier()
                ffn_B(rnd, x2d, outd)
                em.barrier()

        for k in ("as0", "as1", "as2"):
            em.wait("y", k, em.cnt[k])

        def replay(eng_obj, key):
            eng_obj._sems = sems
            for o in em.q[key]:
                ins = o.fn(eng_obj)
                for (s, n) in o.incs:
                    ins.then_inc(sems[s], n)

        @block.tensor
        def _(e):
            replay(e, "t")

        @block.vector
        def _(e):
            replay(e, "v")

        @block.scalar
        def _(e):
            replay(e, "s")

        @block.gpsimd
        def _(e):
            replay(e, "g")

        @block.sync
        def _(e):
            replay(e, "y")

    return nc


def _prep_inputs(inputs):
    f = np.ascontiguousarray
    x = inputs["L0_attn_hidden_states"]
    Wq = inputs["L0_attn_Wq"]
    bq = inputs["L0_attn_bq"]
    Wo = inputs["L0_attn_Wo"]
    M = inputs["L0_attn_M"]
    z = inputs["L0_attn_z"]
    Wb = inputs["L0_ffn_W_base"]
    bb = inputs["L0_ffn_b_base"]
    Wr = inputs["L0_ffn_W_router"]
    A = inputs["L0_ffn_A"]
    B = inputs["L0_ffn_B"]

    xs = np.asarray(x, np.float32).reshape(16384, D)
    wq_sel = np.asarray(Wq, np.float32).reshape(32, 128, D)[::4].reshape(1024, D)
    bq_sel = np.asarray(bq, np.float32).reshape(32, 128)[::4].reshape(8, 128)
    common = {
        "wbT": f(np.asarray(Wb, np.float32).T).reshape(KC, 128, D),
        "wqT": f(wq_sel.T).reshape(KC, 128, 1024),
        "woT": f(np.asarray(Wo, np.float32).T).reshape(KC, 128, D),
        "aT": f(np.asarray(A, np.float32).reshape(128, D).T).reshape(KC, 128, 128),
        "bfT": f(np.asarray(B, np.float32).transpose(0, 2, 1).reshape(128, D)),
        "wrT": f(np.asarray(Wr, np.float32).T).reshape(KC, 128, 8),
        "bb": f(np.asarray(bb, np.float32).reshape(1, D)),
        "bq": f(bq_sel.T),
        "M": f(np.asarray(M, np.float32)),
        "zT": f(np.asarray(z, np.float32).T),
        "ident": np.eye(128, dtype=ml_dtypes.bfloat16),
        "ident2": (2 * np.eye(128)).astype(ml_dtypes.bfloat16),
        "ones_row": np.ones((1, 128), ml_dtypes.bfloat16),
    }
    shards = []
    for c in range(8):
        m = dict(common)
        m["x0"] = f(xs[c * 2048:(c + 1) * 2048].reshape(NT, 128, D))
        shards.append(m)
    return shards


def kernel(**inputs):
    shards = _prep_inputs(inputs)
    nc = build_program(phases=3)
    res = run_bass_kernel_spmd(nc, shards, core_ids=list(range(8)))
    out = np.concatenate(
        [np.asarray(r["out"]).reshape(2048, D) for r in res.results], axis=0)
    return np.ascontiguousarray(out.reshape(4, 4096, 4096)).astype(np.float32)


# version 10
# speedup vs baseline: 1.0776x; 1.0091x over previous
import numpy as np
import ml_dtypes
import concourse.bass as bass
import concourse.mybir as mybir
from concourse.bass_utils import run_bass_kernel_spmd

F32 = mybir.dt.float32
BF16 = mybir.dt.bfloat16
AF = mybir.ActivationFunctionType
ALU = mybir.AluOpType

NT = 16
D = 4096
KC = 32
NOC = 8
RT = 8
NH = 8
HD = 128
EPS_LN = 1e-5
EPS_ATT = 1e-6

STREAMS = ("t", "v", "s", "g", "y")
DMA_SEMS = ("xb0", "xb1", "wb0", "wb1", "xf0", "xf1", "wo0", "wo1",
            "gsm", "ysm", "as0", "as1", "as2", "wbs", "wf0", "wf1")
ENGS = ("t", "v", "s") + DMA_SEMS
STREAM_OF = {"t": "t", "v": "v", "s": "s", "g": "g", "y": "y", "a": "s"}


class Op:
    __slots__ = ("fn", "incs", "is_wait")
    def __init__(self, fn, is_wait=False):
        self.fn = fn
        self.incs = []
        self.is_wait = is_wait


class Em:
    def __init__(self):
        self.q = {e: [] for e in STREAMS}
        self.cnt = {e: 0 for e in ENGS}
        self.lastwait = {e: {s: 0 for s in ENGS} for e in STREAMS}

    def op(self, eng, fn, inc=False):
        o = Op(fn)
        self.q[STREAM_OF[eng]].append(o)
        if inc:
            self._attach(eng, o)
        return self.cnt[eng]

    def _attach(self, sem, o):
        n = 16 if sem in DMA_SEMS else 1
        o.incs.append((sem, n))
        self.cnt[sem] += n

    def dma(self, queue, sem, fn):
        o = Op(fn)
        self.q[STREAM_OF[queue]].append(o)
        self._attach(sem, o)
        return self.cnt[sem]

    def wait(self, eng, sem, val=None):
        st = STREAM_OF[eng]
        if val is None:
            val = self.cnt[sem]
        if val <= 0 or val <= self.lastwait[st][sem]:
            return
        self.lastwait[st][sem] = val
        def f(e, _s=sem, _v=val):
            return e.wait_ge(e._sems[_s], _v)
        self.q[st].append(Op(f, is_wait=True))

    def mark(self, eng):
        st = STREAM_OF[eng]
        for o in reversed(self.q[st]):
            if o.is_wait:
                continue
            if not o.incs:
                self._attach(eng, o)
            return self.cnt[eng]
        return self.cnt[eng]

    def barrier(self, engs=("t", "v", "s")):
        counts = {e: self.mark(e) for e in engs}
        for e in engs:
            for other in engs:
                if other != e:
                    self.wait(e, other, counts[other])


class Bank:
    __slots__ = ("fill", "cons")
    def __init__(self):
        self.fill = 0
        self.cons = []


def build_program(phases=3):
    nc = bass.Bass("TRN2", target_bir_lowering=False, detect_race_conditions=False)

    xd = nc.dram_tensor("x0", [NT, 128, D], F32, kind="ExternalInput").ap()
    wbT = nc.dram_tensor("wbT", [KC, 128, D], F32, kind="ExternalInput").ap()
    wqT = nc.dram_tensor("wqT", [KC, 128, NH * HD], F32, kind="ExternalInput").ap()
    woT = nc.dram_tensor("woT", [KC, 128, D], F32, kind="ExternalInput").ap()
    aT = nc.dram_tensor("aT", [KC, 128, 128], F32, kind="ExternalInput").ap()
    bfT = nc.dram_tensor("bfT", [128, D], F32, kind="ExternalInput").ap()
    wrT = nc.dram_tensor("wrT", [KC, 128, NH], F32, kind="ExternalInput").ap()
    bbd = nc.dram_tensor("bb", [1, D], F32, kind="ExternalInput").ap()
    bqd = nc.dram_tensor("bq", [128, NH], F32, kind="ExternalInput").ap()
    Md = nc.dram_tensor("M", [NH, 128, HD], F32, kind="ExternalInput").ap()
    zTd = nc.dram_tensor("zT", [128, NH], F32, kind="ExternalInput").ap()
    identd = nc.dram_tensor("ident", [128, 128], BF16, kind="ExternalInput").ap()
    ident2d = nc.dram_tensor("ident2", [128, 128], BF16, kind="ExternalInput").ap()
    onesd = nc.dram_tensor("ones_row", [1, 128], BF16, kind="ExternalInput").ap()

    outd = nc.dram_tensor("out", [NT, 128, D], F32, kind="ExternalOutput").ap()
    x1d = nc.dram_tensor("x1s", [NT, 128, D], BF16, kind="Internal").ap()
    x2d = nc.dram_tensor("x2s", [NT, 128, D], BF16, kind="Internal").ap()
    wofd = nc.dram_tensor("wofd", [NH, 128, D], BF16, kind="Internal").ap()
    wbbf = nc.dram_tensor("wbbf", [KC, 128, D], BF16, kind="Internal").ap()

    em = Em()
    banks = [Bank() for _ in range(8)]

    from contextlib import ExitStack
    with ExitStack() as _st:
        def SB(name, shape, dt):
            return _st.enter_context(nc.sbuf_tensor(name, shape, dt))
        ident_sb = SB("ident_sb", [128, 128], BF16)
        ident2_sb = SB("ident2_sb", [128, 128], BF16)
        ones_sb = SB("ones_sb", [1, 128], BF16)
        ones16 = SB("ones16", [128, 16], F32)
        epsln = SB("epsln", [128, 1], F32)
        epsatt = SB("epsatt", [128, 1], F32)
        awrT = SB("awrT", [128, KC, 136], BF16)
        bfT_sb = SB("bfT_sb", [128, D], BF16)
        mz = SB("mz", [128, NH, 129], BF16)
        bbrep = SB("bbrep", [128, D], BF16)
        bq_sb = SB("bq_sb", [128, NH], F32)
        yT = SB("yT", [128, KC, 1024], BF16)
        wbig = SB("wbig", [128, KC, 1024], BF16)
        xb = SB("xb", [128, 2, D], BF16)
        ybf = SB("ybf", [128, D], BF16)
        gT = SB("gT", [128, 1024], BF16)
        wexp = SB("wexp", [128, 2, 128], BF16)
        stats = SB("stats", [128, RT, 8], F32)
        ltok = SB("ltok", [128, RT, 8], F32)
        mx8 = SB("mx8", [128, RT, 8], F32)
        tp2 = SB("tp2", [128, RT, 4], F32)
        msk = SB("msk", [128, RT, 8], F32)
        e1b = SB("e1b", [128, RT, 8], F32)
        scr = SB("scr", [128, 8192], BF16)
        outb = SB("outb", [128, 3, 512], F32)
        retrst = SB("retrst", [128, NH, 128], BF16)
        PS = _st.enter_context(nc.psum_tensor("PS", [128, 8, 512], F32))
        _semnames = ("t", "v", "s") + DMA_SEMS
        sems = {nm: _st.enter_context(nc.semaphore("sem_" + nm))
                for nm in _semnames}
        block = _st.enter_context(nc.Block())

        outb_bf = [outb[:, i, :].bitcast(BF16)[:, 0:512] for i in range(3)]
        sigq = scr[:, :].rearrange("p (h t) -> p h t", h=NH)
        xbfc = [scr[:, i * 4096:(i + 1) * 4096].rearrange("p (a b) -> p a b", a=8)
                for i in range(2)]
        dumpA = scr[:, 0:4096]
        ybf_odd = scr[:, 4096:8192]
        sqdump = ybf[:, 0:2048]
        spart = ybf[:, 2048:4096].bitcast(F32).rearrange(
            "p (r t c) -> p (r t) c", r=2, t=RT)
        hT = scr[:, 4096:5120]
        wexpT = scr[:, 5120:6144]
        ltT = scr[0:8, 6144:7168]
        retrT = yT[:, 0:NH, :]
        wof_ch = [xb[:, i, :].rearrange("p (a b) -> p a b", a=8) for i in range(2)]
        wb_ch = [wbig[:, :, i * 512:(i + 1) * 512] for i in range(2)]

        def t_op(fn, inc=False):
            return em.op("t", fn, inc)

        def v_op(fn, inc=False):
            return em.op("v", fn, inc)

        def s_op(fn, inc=False):
            return em.op("s", fn, inc)

        def bank_fill_prep(bs):
            for b in bs:
                for (eng, val) in banks[b].cons:
                    em.wait("t", eng, val)

        def bank_filled(bs):
            c = em.mark("t")
            for b in bs:
                banks[b].fill = c
                banks[b].cons = []

        def consume_bank(eng, b):
            em.wait(eng, "t", banks[b].fill)

        def bank_consumed(bs, eng):
            c = em.mark(eng)
            for b in bs:
                banks[b].cons.append((eng, c))

        slot_state = {"i": 0, "free": [0, 0, 0]}
        slot_free_s = [0, 0, 0]

        def next_slot():
            sl = slot_state["i"] % 3
            slot_state["i"] += 1
            return sl

        em.dma("g", "gsm", lambda e: e.dma_start(out=awrT[:, :, 0:128],
                                          in_=aT.rearrange("k p o -> p k o")))
        em.dma("g", "gsm", lambda e: e.dma_start(out=awrT[:, :, 128:136],
                                          in_=wrT.rearrange("k p o -> p k o")))
        em.dma("g", "gsm", lambda e: e.dma_start(out=bfT_sb[:, :], in_=bfT[:, :]))
        em.dma("g", "gsm", lambda e: e.dma_start(out=mz[:, :, 0:128],
                                          in_=Md.rearrange("h p o -> p h o"),
                                          allow_slow_non_contiguous=True))
        em.dma("g", "gsm", lambda e: e.dma_start(out=mz[:, :, 128:129],
                                          in_=zTd.rearrange("p (h o) -> p h o", o=1),
                                          allow_slow_non_contiguous=True))
        em.dma("y", "ysm", lambda e: e.dma_start(out=ident_sb[:, :], in_=identd[:, :]))
        em.dma("y", "ysm", lambda e: e.dma_start(out=ident2_sb[:, :], in_=ident2d[:, :]))
        em.dma("y", "ysm", lambda e: e.dma_start(out=ones_sb[:, :], in_=onesd[:, :]))
        em.dma("y", "ysm", lambda e: e.dma_start(out=bq_sb[:, :], in_=bqd[:, :]))
        em.dma("g", "gsm", lambda e: e.dma_start(out=bbrep[0:1, :], in_=bbd[:, :]))
        gc0, yc0 = em.cnt["gsm"], em.cnt["ysm"]

        v_op(lambda e: e.memset(ones16[:, :], 1.0))
        v_op(lambda e: e.memset(epsln[:, :], EPS_LN))
        v_op(lambda e: e.memset(epsatt[:, :], EPS_ATT), inc=True)

        em.wait("t", "ysm", yc0)
        em.wait("t", "gsm", gc0)
        for oc in range(NOC):
            b = oc % 2
            bank_fill_prep([b])
            def f(e, _b=b, _oc=oc):
                return e.matmul(PS[:, _b, :], ones_sb[:, :],
                         bbrep[0:1, _oc * 512:(_oc + 1) * 512],
                         start=True, stop=True)
            t_op(f, inc=True)
            bank_filled([b])
            consume_bank("v", b)
            def fv(e, _b=b, _oc=oc):
                return e.tensor_scalar(bbrep[:, _oc * 512:(_oc + 1) * 512], PS[:, _b, :],
                                0.5, None, ALU.mult)
            v_op(fv, inc=True)
            bank_consumed([b], "v")

        fold_state = {"v": 0, "wf0": 0, "wf1": 0}
        fold_done_v = {}

        def wo_stage(k4):
            return wbig[:, 4 * k4:4 * k4 + 4, :].bitcast(F32)

        def _mk_fold_unit(u):
            h, half = u % NH, u // NH
            p = u % 2
            base = 0 if p == 0 else 4
            srcs = [wo_stage(base + g) for g in range(4)]
            dsem = "wb%d" % p
            fsem = "wf%d" % p
            def emit():
                if u >= 2:
                    em.wait("y", "v", fold_done_v[u - 2])
                    em.wait("y", fsem, fold_state[fsem])
                last = 0
                for g4 in range(4):
                    def fdl(e, _h=h, _g=g4, _half=half, _dst=srcs[g4]):
                        return e.dma_start(
                            out=_dst,
                            in_=woT[4 * _h + _g, :,
                                    _half * 2048:(_half + 1) * 2048]
                            .rearrange("p (a b) -> p a b", a=4))
                    last = em.dma("y", dsem, fdl)
                em.wait("v", dsem, last)
                v_op(lambda e, _a=srcs[0], _b=srcs[1], _o=srcs[0]:
                     e.tensor_add(_o, _a, _b))
                v_op(lambda e, _a=srcs[2], _b=srcs[3], _o=srcs[2]:
                     e.tensor_add(_o, _a, _b))
                v_op(lambda e, _a=srcs[0], _b=srcs[2], _o=srcs[0]:
                     e.tensor_add(_o, _a, _b), inc=True)
                fold_done_v[u] = em.cnt["v"]
                em.wait("g", "v", fold_done_v[u])
                def fst(e, _h=h, _half=half, _src=srcs[0]):
                    return e.dma_start(
                        out=wofd[_h, :, _half * 2048:(_half + 1) * 2048]
                        .rearrange("p (a b) -> p a b", a=4), in_=_src)
                em.dma("g", fsem, fst)
                fold_state[fsem] = em.cnt[fsem]
            return emit

        fold_units = [_mk_fold_unit(u) for u in range(2 * NH)]
        fold_units.reverse()

        def store_totals():
            return {k: em.cnt[k] for k in ("as0", "as1", "as2")}

        def wait_stores(eng, tot):
            for k, v in tot.items():
                em.wait(eng, k, v)

        g_lastnorm = {0: 0, 1: 0}

        def emit_ffn_tail():
            bank_fill_prep([0, 1, 2, 3])
            for kc in range(KC):
                for tc in range(2):
                    def fh(e, _kc=kc, _tc=tc):
                        return e.matmul(PS[:, _tc, :], awrT[:, _kc, 0:128],
                                 yT[:, _kc, _tc * 512:(_tc + 1) * 512],
                                 start=(_kc == 0), stop=(_kc == KC - 1))
                    t_op(fh)
                    def fr(e, _kc=kc, _tc=tc):
                        return e.matmul(PS[0:8, 2 + _tc, :], awrT[:, _kc, 128:136],
                                 yT[:, _kc, _tc * 512:(_tc + 1) * 512],
                                 start=(_kc == 0), stop=(_kc == KC - 1))
                    t_op(fr, inc=(kc == KC - 1 and tc == 1))
            bank_filled([0, 1, 2, 3])
            for b in (0, 1, 2, 3):
                consume_bank("v", b)
            v_op(lambda e: e.tensor_copy(
                hT[:, :], PS[:, 0:2, :].rearrange("p a b -> p (a b)")))
            v_op(lambda e: e.tensor_copy(
                ltT[:, :], PS[0:8, 2:4, :].rearrange("p a b -> p (a b)")),
                inc=True)
            bank_consumed([0, 1, 2, 3], "v")

            em.wait("t", "v", em.cnt["v"])
            bank_fill_prep([4])
            for j in range(RT):
                def fl(e, _j=j):
                    return e.matmul(PS[:, 4, _j * 8:(_j + 1) * 8],
                             ltT[:, _j * 128:(_j + 1) * 128],
                             ident_sb[0:8, 0:8], start=True, stop=True)
                t_op(fl, inc=(j == RT - 1))
            bank_filled([4])
            consume_bank("v", 4)
            v_op(lambda e: e.tensor_copy(
                ltok[:, :, :],
                PS[:, 4, 0:64].rearrange("p (a b) -> p a b", a=RT)), inc=True)
            bank_consumed([4], "v")

            for j in range(RT):
                v_op(lambda e, _j=j: e.max(out=mx8[:, _j, :], in_=ltok[:, _j, :]))
            for j in range(RT):
                v_op(lambda e, _j=j: e.tensor_scalar(
                    e1b[:, _j, :], ltok[:, _j, :], mx8[:, _j, 0:1], None,
                    ALU.subtract))
            for j in range(RT):
                v_op(lambda e, _j=j: e.tensor_scalar(
                    msk[:, _j, :], ltok[:, _j, :], mx8[:, _j, 1:2], None,
                    ALU.is_ge))
            for j in range(RT):
                v_op(lambda e, _j=j: e.tensor_scalar(
                    tp2[:, _j, 0:1], mx8[:, _j, 1:2], mx8[:, _j, 0:1], None,
                    ALU.subtract), inc=(j == RT - 1))
            em.wait("s", "v", em.cnt["v"])
            for j in range(RT):
                s_op(lambda e, _j=j: e.activation(e1b[:, _j, :], e1b[:, _j, :],
                                                  AF.Exp))
            for j in range(RT):
                s_op(lambda e, _j=j: e.activation(tp2[:, _j, 1:2],
                                                  tp2[:, _j, 0:1], AF.Exp),
                     inc=(j == RT - 1))
            em.wait("v", "s", em.cnt["s"])
            for j in range(RT):
                v_op(lambda e, _j=j: e.tensor_scalar(
                    tp2[:, _j, 2:3], tp2[:, _j, 1:2], 1.0, None, ALU.add))
            for j in range(RT):
                v_op(lambda e, _j=j: e.reciprocal(tp2[:, _j, 3:4], tp2[:, _j, 2:3]))
            for j in range(RT):
                v_op(lambda e, _j=j: e.tensor_mul(e1b[:, _j, :], e1b[:, _j, :],
                                                  msk[:, _j, :]))
            for j in range(RT):
                v_op(lambda e, _j=j: e.tensor_scalar(
                    e1b[:, _j, :], e1b[:, _j, :], tp2[:, _j, 3:4], 2.0,
                    ALU.mult, ALU.mult))
            for j in range(RT):
                wb_ = j % 2
                for ex in range(8):
                    v_op(lambda e, _j=j, _w=wb_, _ex=ex: e.tensor_scalar(
                        wexp[:, _w, _ex * 16:(_ex + 1) * 16], ones16[:, :],
                        e1b[:, _j, _ex:_ex + 1], None, ALU.mult),
                        inc=(ex == 7))
                em.wait("t", "v", em.cnt["v"])
                b = 5 + (j % 2)
                bank_fill_prep([b])
                def ft(e, _w=wb_, _b=b):
                    return e.matmul(PS[:, _b, 0:128], wexp[:, _w, :], ident_sb[:, :],
                             start=True, stop=True)
                t_op(ft, inc=True)
                bank_filled([b])
                consume_bank("v", b)
                v_op(lambda e, _j=j, _b=b: e.tensor_copy(
                    wexpT[:, _j * 128:(_j + 1) * 128], PS[:, _b, 0:128]),
                    inc=True)
                bank_consumed([b], "v")
            v_op(lambda e: e.tensor_mul(gT[:, :], hT[:, :], wexpT[:, :]),
                 inc=True)

        def stage_A(rnd, src, ffn, fold_hook=False):
            entry_t = em.cnt["t"]
            entry_st = store_totals()
            norm_cnt = {}
            tr_done = {}
            dma_cnt = {}
            sq_done = {}

            def emit_dma(i):
                tt = rnd * RT + i
                em.wait("g", "t", entry_t)
                wait_stores("g", entry_st)
                if i - 2 >= 0:
                    em.wait("g", "s", sq_done[i - 2])
                    em.wait("g", "v", norm_cnt[i - 2])
                sem = "xb%d" % (i % 2)
                def fx(e, _tt=tt, _bi=i % 2):
                    return e.dma_start(out=xb[:, _bi, :], in_=src[_tt, :, :])
                dma_cnt[i] = (sem, em.dma("g", sem, fx))

            emit_dma(0)
            emit_dma(1)
            for i in range(RT):
                em.wait("s", dma_cnt[i][0], dma_cnt[i][1])
                s_op(lambda e, _bi=i % 2, _i=i: e.activation(
                    dumpA[:, :], xb[:, _bi, :], AF.Copy,
                    accum_out=stats[:, _i, 0:1]))
                s_op(lambda e, _bi=i % 2, _i=i: e.activation(
                    dumpA[:, :], xb[:, _bi, :], AF.Square,
                    accum_out=stats[:, _i, 1:2]), inc=True)
                sq_done[i] = em.cnt["s"]
                s_op(lambda e, _i=i: e.activation(stats[:, _i, 2:3],
                                                  stats[:, _i, 0:1],
                                                  AF.Copy, scale=1.0 / D),
                     inc=True)
                em.wait("v", "s", em.cnt["s"])
                v_op(lambda e, _i=i: e.tensor_scalar(
                    stats[:, _i, 3:4], stats[:, _i, 2:3], stats[:, _i, 2:3],
                    -1.0, ALU.mult, ALU.mult), inc=True)
                em.wait("s", "v", em.cnt["v"])
                s_op(lambda e, _i=i: e.activation(stats[:, _i, 4:5],
                                                  stats[:, _i, 1:2], AF.Identity,
                                                  bias=stats[:, _i, 3:4],
                                                  scale=1.0 / D))
                s_op(lambda e: e.drain())
                s_op(lambda e, _i=i: e.activation(stats[:, _i, 5:6],
                                                  stats[:, _i, 4:5], AF.Ln,
                                                  bias=epsln[:, :]))
                s_op(lambda e: e.drain())
                s_op(lambda e, _i=i: e.activation(stats[:, _i, 6:7],
                                                  stats[:, _i, 5:6], AF.Exp,
                                                  scale=-0.5), inc=True)
                chain_done_i = em.cnt["s"]

                em.wait("v", "s", chain_done_i)
                if i >= 1:
                    em.wait("v", "t", tr_done[i - 1])
                yb = ybf_odd
                def fn(e, _bi=i % 2, _i=i, _yb=yb):
                    return e.tensor_scalar(_yb[:, :], xb[:, _bi, :],
                                    stats[:, _i, 2:3], stats[:, _i, 6:7],
                                    ALU.subtract, ALU.mult)
                v_op(fn, inc=True)
                norm_cnt[i] = em.cnt["v"]
                g_lastnorm[i % 2] = em.cnt["v"]

                em.wait("t", "v", norm_cnt[i])
                for half in range(2):
                    bs = [4 * half + k for k in range(4)]
                    bank_fill_prep(bs)
                    for k in range(16):
                        kc = 16 * half + k
                        b = bs[k // 4]
                        off = (k % 4) * 128
                        def fm(e, _kc=kc, _b=b, _off=off, _yb=yb):
                            return e.matmul(PS[:, _b, _off:_off + 128],
                                     _yb[:, _kc * 128:(_kc + 1) * 128],
                                     ident_sb[:, :], start=True, stop=True)
                        t_op(fm, inc=(k == 15))
                    bank_filled(bs)
                    for b in bs:
                        consume_bank("v", b)
                    kc0 = 16 * half
                    def fe(e, _kc0=kc0, _i=i, _h=half):
                        return e.tensor_copy(
                            yT[:, _kc0:_kc0 + 16, _i * 128:(_i + 1) * 128],
                            PS[:, _h * 4:_h * 4 + 4, :]
                            .rearrange("p a (c d) -> p (a c) d", d=128))
                    v_op(fe, inc=True)
                    bank_consumed(bs, "v")
                tr_done[i] = em.cnt["t"]

                if i + 2 < RT:
                    emit_dma(i + 2)
                if fold_hook:
                    em.wait("y", "t", entry_t)
                    for _ in range(2):
                        if fold_units:
                            fold_units.pop()()

            if not ffn:
                return

            emit_ffn_tail()

        def stage_A_fast(rnd, src, ffn, sq_prod_s):
            entry_t = em.cnt["t"]
            entry_st = store_totals()
            for i in range(RT):
                ps_ = (rnd % 2) * RT + i
                v_op(lambda e, _p=ps_, _i=i: e.tensor_scalar(
                    mx8[:, _i, :], spart[:, _p, 0:8], 1.0, None,
                    ALU.mult, ALU.add, accum_out=stats[:, _i, 0:1]),
                    inc=(i == RT - 1))
            em.wait("s", "v", em.cnt["v"])
            for i in range(RT):
                s_op(lambda e, _i=i: e.activation(stats[:, _i, 2:3],
                                                  stats[:, _i, 0:1],
                                                  AF.Copy, scale=1.0 / D),
                     inc=(i == RT - 1))
            em.wait("v", "s", em.cnt["s"])
            for i in range(RT):
                v_op(lambda e, _i=i: e.tensor_scalar(
                    stats[:, _i, 3:4], stats[:, _i, 2:3], stats[:, _i, 2:3],
                    -1.0, ALU.mult, ALU.mult), inc=(i == RT - 1))
            negmsq_v = em.cnt["v"]

            dma_cnt = {}
            tr_done = {}

            def emit_dma(i):
                tt = rnd * RT + i
                em.wait("g", "t", entry_t)
                wait_stores("g", entry_st)
                em.wait("g", "v", g_lastnorm[i % 2])
                sem = "xb%d" % (i % 2)
                def fx(e, _tt=tt, _bi=i % 2):
                    return e.dma_start(out=xb[:, _bi, :], in_=src[_tt, :, :])
                dma_cnt[i] = (sem, em.dma("g", sem, fx))

            emit_dma(0)
            emit_dma(1)
            em.wait("s", "v", negmsq_v)
            chain_done = {}
            for i in range(RT):
                em.wait("s", dma_cnt[i][0], dma_cnt[i][1])
                s_op(lambda e, _bi=i % 2, _i=i: e.activation(
                    sqdump[:, 0:2048], xb[:, _bi, 0:2048], AF.Square,
                    accum_out=stats[:, _i, 1:2]))
                s_op(lambda e, _bi=i % 2, _i=i: e.activation(
                    sqdump[:, 0:2048], xb[:, _bi, 2048:4096], AF.Square,
                    accum_out=stats[:, _i, 7:8]))
                s_op(lambda e, _i=i: e.activation(
                    stats[:, _i, 4:5], stats[:, _i, 1:2], AF.Identity,
                    bias=stats[:, _i, 3:4], scale=1.0 / D))
                s_op(lambda e: e.drain())
                s_op(lambda e, _i=i: e.activation(
                    stats[:, _i, 4:5], stats[:, _i, 7:8], AF.Identity,
                    bias=stats[:, _i, 4:5], scale=1.0 / D))
                s_op(lambda e: e.drain())
                s_op(lambda e, _i=i: e.activation(stats[:, _i, 5:6],
                                                  stats[:, _i, 4:5], AF.Ln,
                                                  bias=epsln[:, :]))
                s_op(lambda e: e.drain())
                s_op(lambda e, _i=i: e.activation(stats[:, _i, 6:7],
                                                  stats[:, _i, 5:6], AF.Exp,
                                                  scale=-0.5), inc=True)
                chain_done[i] = em.cnt["s"]
                em.wait("v", "s", chain_done[i])
                em.wait("v", dma_cnt[i][0], dma_cnt[i][1])
                if i >= 1:
                    em.wait("v", "t", tr_done[i - 1])
                def fn(e, _bi=i % 2, _i=i):
                    return e.tensor_scalar(ybf_odd[:, :], xb[:, _bi, :],
                                    stats[:, _i, 2:3], stats[:, _i, 6:7],
                                    ALU.subtract, ALU.mult)
                v_op(fn, inc=True)
                g_lastnorm[i % 2] = em.cnt["v"]
                em.wait("t", "v", em.cnt["v"])
                for half in range(2):
                    bs = [4 * half + k for k in range(4)]
                    bank_fill_prep(bs)
                    for k in range(16):
                        kc = 16 * half + k
                        b = bs[k // 4]
                        off = (k % 4) * 128
                        def fm(e, _kc=kc, _b=b, _off=off):
                            return e.matmul(PS[:, _b, _off:_off + 128],
                                     ybf_odd[:, _kc * 128:(_kc + 1) * 128],
                                     ident_sb[:, :], start=True, stop=True)
                        t_op(fm, inc=(k == 15))
                    bank_filled(bs)
                    for b in bs:
                        consume_bank("v", b)
                    kc0 = 16 * half
                    def fe(e, _kc0=kc0, _i=i, _h=half):
                        return e.tensor_copy(
                            yT[:, _kc0:_kc0 + 16, _i * 128:(_i + 1) * 128],
                            PS[:, _h * 4:_h * 4 + 4, :]
                            .rearrange("p a (c d) -> p (a c) d", d=128))
                    v_op(fe, inc=True)
                    bank_consumed(bs, "v")
                tr_done[i] = em.cnt["t"]
                if i + 2 < RT:
                    emit_dma(i + 2)

            if not ffn:
                return
            emit_ffn_tail()

        def ffn_B(rnd, src, dst, wsrc="bf16", save=False, feed_stats=False):
            gT_ready_v = em.cnt["v"]
            entry_t = em.cnt["t"]
            entry_s = em.cnt["s"]
            entry_st = store_totals()
            wb_g = {}
            wb_free_t = {}
            xb_y = {}
            xbf_free_t = {}

            def issue_w(oc):
                buf = oc % 2
                q = "g" if wsrc == "f32" else "y"
                em.wait(q, "t", wb_free_t.get(oc - 2, entry_t))
                em.wait(q, "wf0", fold_state["wf0"])
                em.wait(q, "wf1", fold_state["wf1"])
                em.wait(q, "v", fold_done_v.get(2 * NH - 1, 0))

                if wsrc == "bf16":
                    em.wait(q, "wbs", wbbf_saved[0])
                sem = "wb%d" % buf
                srcw = wbT if wsrc == "f32" else wbbf
                last = 0
                for part in range(4):
                    k0 = part * 8
                    def fw(e, _oc=oc, _buf=buf, _k0=k0, _srcw=srcw):
                        return e.dma_start(
                            out=wb_ch[_buf][:, _k0:_k0 + 8, :],
                            in_=_srcw.rearrange("k p o -> p k o")
                            [:, _k0:_k0 + 8, _oc * 512:(_oc + 1) * 512])
                    last = em.dma(q, sem, fw)
                wb_g[oc] = (sem, last)
                if save:
                    em.wait("a", sem, last)
                    for part in range(2):
                        k0 = part * 16
                        def fsv(e, _oc=oc, _buf=buf, _k0=k0):
                            return e.dma_start(
                                out=wbbf.rearrange("k p o -> p k o")
                                [:, _k0:_k0 + 16, _oc * 512:(_oc + 1) * 512],
                                in_=wb_ch[_buf][:, _k0:_k0 + 16, :])
                        em.dma("a", "wbs", fsv)

            def issue_x(oc):
                buf = oc % 2
                em.wait("g", "t", xbf_free_t.get(oc - 2, entry_t))
                em.wait("g", "s", entry_s)
                em.wait("g", "v", gT_ready_v)
                wait_stores("g", entry_st)
                sem = "xf%d" % buf
                last = 0
                for part in range(2):
                    a0 = part * 4
                    def fx(e, _oc=oc, _buf=buf, _a0=a0):
                        return e.dma_start(
                            out=xbfc[_buf][:, _a0:_a0 + 4, :],
                            in_=src[rnd * 8 + _a0:rnd * 8 + _a0 + 4, :,
                                    _oc * 512:(_oc + 1) * 512]
                            .rearrange("a p b -> p a b"))
                    last = em.dma("g", sem, fx)
                xb_y[oc] = (sem, last)

            issue_w(0)
            issue_x(0)
            issue_w(1)
            issue_x(1)

            for oc in range(NOC):
                buf = oc % 2
                em.wait("t", wb_g[oc][0], wb_g[oc][1])
                em.wait("t", xb_y[oc][0], xb_y[oc][1])
                em.wait("t", "v", gT_ready_v)
                for th in range(2):
                    bs = [4 * th + k for k in range(4)]
                    bank_fill_prep(bs)
                    for k in range(4):
                        def fi(e, _b=bs[k], _buf=buf, _tt=4 * th + k):
                            return e.matmul(PS[:, _b, :], ident2_sb[:, :],
                                     xbfc[_buf][:, _tt, :], start=True,
                                     stop=False)
                        t_op(fi)
                    for kc in range(KC):
                        for k in range(4):
                            tt = 4 * th + k
                            def fm(e, _kc=kc, _b=bs[k], _buf=buf, _tt=tt):
                                return e.matmul(PS[:, _b, :],
                                         yT[:, _kc, _tt * 128:(_tt + 1) * 128],
                                         wb_ch[_buf][:, _kc, :],
                                         start=False, stop=False)
                            t_op(fm)
                    for k in range(4):
                        tt = 4 * th + k
                        def fl(e, _b=bs[k], _tt=tt, _oc=oc):
                            return e.matmul(PS[:, _b, :],
                                     gT[:, _tt * 128:(_tt + 1) * 128],
                                     bfT_sb[:, _oc * 512:(_oc + 1) * 512],
                                     start=False, stop=True)
                        t_op(fl, inc=(k == 3))
                    bank_filled(bs)
                    for k in range(4):
                        tt = 4 * th + k
                        b = bs[k]
                        sl = next_slot()
                        bf = (dst.dtype == BF16)
                        tgt = outb_bf[sl] if bf else outb[:, sl, :]
                        em.wait("v", "as%d" % sl, slot_state["free"][sl])
                        consume_bank("v", b)
                        pslot = (rnd % 2) * RT + tt
                        acc = (spart[:, pslot, oc:oc + 1] if feed_stats else None)
                        def fe(e, _b=b, _tgt=tgt, _oc=oc, _acc=acc):
                            return e.scalar_tensor_tensor(
                                _tgt, PS[:, _b, :], 0.5,
                                bbrep[:, _oc * 512:(_oc + 1) * 512],
                                ALU.mult, ALU.add, accum_out=_acc)
                        v_op(fe, inc=True)
                        bank_consumed([b], "v")
                        em.wait("a", "v", em.cnt["v"])
                        def fst(e, _tgt=tgt, _tt=rnd * 8 + tt, _oc=oc):
                            return e.dma_start(out=dst[_tt, :, _oc * 512:(_oc + 1) * 512],
                                        in_=_tgt)
                        slot_state["free"][sl] = em.dma("a", "as%d" % sl, fst)
                wb_free_t[oc] = em.mark("t")
                xbf_free_t[oc] = wb_free_t[oc]
                if oc + 2 < NOC:
                    issue_w(oc + 2)
                    issue_x(oc + 2)


        def attn_B(rnd, src, dst, feed_stats=False):
            entry_t = em.cnt["t"]
            entry_s = em.cnt["s"]
            entry_v = em.cnt["v"]
            entry_st = store_totals()
            for sl in range(3):
                em.wait("v", "as%d" % sl, slot_state["free"][sl])
            for ocq in range(8):
                bs = [2 * (ocq % 2), 2 * (ocq % 2) + 1]
                bank_fill_prep(bs)
                for kc in range(KC):
                    for tc in range(2):
                        def fq(e, _kc=kc, _ocq=ocq, _tc=tc, _b=bs[tc]):
                            return e.matmul(PS[:, _b, :],
                                     wbig[:, _kc, _ocq * 128:(_ocq + 1) * 128],
                                     yT[:, _kc, _tc * 512:(_tc + 1) * 512],
                                     start=(_kc == 0), stop=(_kc == KC - 1))
                        t_op(fq, inc=(kc == KC - 1 and tc == 1))
                bank_filled(bs)
                for tc in range(2):
                    b = bs[tc]
                    consume_bank("v", b)
                    def f1(e, _b=b, _ocq=ocq):
                        return e.tensor_scalar(outb[:, 0, :], PS[:, _b, :],
                                        bq_sb[:, _ocq:_ocq + 1], 0.0,
                                        ALU.add, ALU.max)
                    v_op(f1)
                    def f2(e, _b=b, _ocq=ocq):
                        return e.tensor_scalar(outb[:, 1, :], PS[:, _b, :],
                                        bq_sb[:, _ocq:_ocq + 1], 0.0,
                                        ALU.add, ALU.min)
                    v_op(f2, inc=True)
                    bank_consumed([b], "v")
                    em.wait("s", "v", em.cnt["v"])
                    s_op(lambda e: e.activation(outb[:, 2, :], outb[:, 1, :],
                                                AF.Exp), inc=True)
                    em.wait("v", "s", em.cnt["s"])
                    def f3(e, _ocq=ocq, _tc=tc):
                        return e.tensor_add(sigq[:, _ocq, _tc * 512:(_tc + 1) * 512],
                                     outb[:, 0, :], outb[:, 2, :])
                    v_op(f3, inc=True)
            em.barrier()

            for h in range(NH):
                bank_fill_prep([0, 1, 2, 3])
                for j in range(RT):
                    def fr(e, _h=h, _j=j):
                        return e.matmul(PS[:, _j // 2,
                                    (_j % 2) * 256:(_j % 2) * 256 + 129],
                                 sigq[:, _h, _j * 128:(_j + 1) * 128],
                                 mz[:, _h, :], start=True, stop=True)
                    t_op(fr, inc=(j == RT - 1))
                bank_filled([0, 1, 2, 3])
                for b in (0, 1, 2, 3):
                    consume_bank("v", b)
                for j in range(RT):
                    def fd(e, _j=j):
                        return e.tensor_scalar(
                            tp2[:, _j, 0:1],
                            PS[:, _j // 2,
                               (_j % 2) * 256 + 128:(_j % 2) * 256 + 129],
                            epsatt[:, :], None, ALU.add)
                    v_op(fd)
                for j in range(RT):
                    v_op(lambda e, _j=j: e.reciprocal(tp2[:, _j, 1:2],
                                                      tp2[:, _j, 0:1]))
                for j in range(RT):
                    def fsc(e, _j=j):
                        return e.tensor_scalar(
                            retrst[:, _j, :],
                            PS[:, _j // 2, (_j % 2) * 256:(_j % 2) * 256 + 128],
                            tp2[:, _j, 1:2], None, ALU.mult)
                    v_op(fsc, inc=(j == RT - 1))
                bank_consumed([0, 1, 2, 3], "v")
                em.wait("t", "v", em.cnt["v"])
                bank_fill_prep([4, 5])
                for j in range(RT):
                    b = 4 + j // 4
                    off = (j % 4) * 128
                    def ftr(e, _j=j, _b=b, _off=off):
                        return e.matmul(PS[:, _b, _off:_off + 128], retrst[:, _j, :],
                                 ident_sb[:, :], start=True, stop=True)
                    t_op(ftr, inc=(j == RT - 1))
                bank_filled([4, 5])
                consume_bank("v", 4)
                consume_bank("v", 5)
                def fev(e, _h=h):
                    return e.tensor_copy(retrT[:, _h, :],
                                  PS[:, 4:6, :].rearrange("p a b -> p (a b)"))
                v_op(fev, inc=True)
                bank_consumed([4, 5], "v")
            em.barrier()

            bo_entry_t = em.cnt["t"]
            wo_y = {}
            wo_free_t = {}
            xb_y = {}
            xbf_free_t = {}

            def issue_wo(oc):
                buf = oc % 2
                em.wait("g", "t", wo_free_t.get(oc - 2, bo_entry_t))
                em.wait("g", "wf0", fold_state["wf0"])
                em.wait("g", "wf1", fold_state["wf1"])
                em.wait("g", "v", fold_state["v"])
                sem = "wo%d" % buf
                def fw(e, _oc=oc, _buf=buf):
                    return e.dma_start(out=wof_ch[_buf][:, :, :],
                                in_=wofd.rearrange("h p o -> p h o")
                                [:, :, _oc * 512:(_oc + 1) * 512])
                wo_y[oc] = (sem, em.dma("g", sem, fw))

            def issue_x(oc):
                buf = oc % 2
                em.wait("g", "t", xbf_free_t.get(oc - 2, bo_entry_t))
                wait_stores("g", entry_st)
                sem = "xf%d" % buf
                last = 0
                for part in range(2):
                    a0 = part * 4
                    def fx(e, _oc=oc, _buf=buf, _a0=a0):
                        return e.dma_start(
                            out=xbfc[_buf][:, _a0:_a0 + 4, :],
                            in_=src[rnd * 8 + _a0:rnd * 8 + _a0 + 4, :,
                                    _oc * 512:(_oc + 1) * 512]
                            .rearrange("a p b -> p a b"))
                    last = em.dma("g", sem, fx)
                xb_y[oc] = (sem, last)

            issue_wo(0)
            issue_x(0)
            issue_wo(1)
            issue_x(1)

            for oc in range(NOC):
                buf = oc % 2
                em.wait("t", wo_y[oc][0], wo_y[oc][1])
                em.wait("t", xb_y[oc][0], xb_y[oc][1])
                for th in range(2):
                    bs = [4 * th + k for k in range(4)]
                    bank_fill_prep(bs)
                    for k in range(4):
                        def fi(e, _b=bs[k], _buf=buf, _tt=4 * th + k):
                            return e.matmul(PS[:, _b, :], ident_sb[:, :],
                                     xbfc[_buf][:, _tt, :], start=True,
                                     stop=False)
                        t_op(fi)
                    for kc in range(NH):
                        for k in range(4):
                            tt = 4 * th + k
                            def fm(e, _kc=kc, _b=bs[k], _buf=buf, _tt=tt):
                                return e.matmul(PS[:, _b, :],
                                         retrT[:, _kc, _tt * 128:(_tt + 1) * 128],
                                         wof_ch[_buf][:, _kc, :],
                                         start=False, stop=(_kc == NH - 1))
                            t_op(fm, inc=(kc == NH - 1 and k == 3))
                    bank_filled(bs)
                    for k in range(4):
                        tt = 4 * th + k
                        b = bs[k]
                        sl = next_slot()
                        bf = (dst.dtype == BF16)
                        tgt = outb_bf[sl] if bf else outb[:, sl, :]
                        em.wait("s", "as%d" % sl, slot_state["free"][sl])
                        consume_bank("s", b)
                        pslot = (rnd % 2) * RT + tt
                        acc = spart[:, pslot, oc:oc + 1] if feed_stats else None
                        def fe(e, _b=b, _tgt=tgt, _acc=acc):
                            return e.activation(_tgt, PS[:, _b, :], AF.Copy,
                                                accum_out=_acc)
                        s_op(fe)
                        bank_consumed([b], "s")
                        em.mark("s")
                        em.wait("a", "s", em.cnt["s"])
                        def fst(e, _tgt=tgt, _tt=rnd * 8 + tt, _oc=oc):
                            return e.dma_start(out=dst[_tt, :, _oc * 512:(_oc + 1) * 512],
                                        in_=_tgt)
                        slot_state["free"][sl] = em.dma("a", "as%d" % sl, fst)
                wo_free_t[oc] = em.mark("t")
                xbf_free_t[oc] = wo_free_t[oc]
                if oc + 2 < NOC:
                    issue_wo(oc + 2)
                    issue_x(oc + 2)

        p1_dst = x1d if phases >= 2 else outd
        wbbf_saved = [0]
        for rnd in range(2):
            stage_A(rnd, xd, ffn=True, fold_hook=(rnd == 0))
            em.barrier()
            ffn_B(rnd, xd, p1_dst, wsrc=("f32" if rnd == 0 else "bf16"),
                  save=(rnd == 0), feed_stats=(phases >= 2))
            if rnd == 0:
                wbbf_saved[0] = em.cnt["wbs"]
            em.barrier()

        if phases >= 2:
            p2_dst = x2d if phases >= 3 else outd
            p1_end_t = em.cnt["t"]
            wq_last = 0
            for rnd in range(2):
                stage_A_fast(rnd, x1d, ffn=False, sq_prod_s=em.cnt["s"])
                if rnd == 0:
                    em.wait("g", "t", p1_end_t)
                    for part in range(8):
                        k0 = part * 4
                        def fwq(e, _k0=k0):
                            return e.dma_start(out=wbig[:, _k0:_k0 + 4, :],
                                        in_=wqT.rearrange("k p o -> p k o")
                                        [:, _k0:_k0 + 4, :])
                        wq_last = em.dma("g", "gsm", fwq)
                em.barrier()
                em.wait("t", "gsm", wq_last)
                attn_B(rnd, x1d, p2_dst, feed_stats=(phases >= 3))
                em.barrier()

        if phases >= 3:
            for rnd in range(2):
                stage_A_fast(rnd, x2d, ffn=True, sq_prod_s=em.cnt["s"])
                em.barrier()
                ffn_B(rnd, x2d, outd)
                em.barrier()

        for k in ("as0", "as1", "as2"):
            em.wait("y", k, em.cnt[k])

        def replay(eng_obj, key):
            eng_obj._sems = sems
            for o in em.q[key]:
                ins = o.fn(eng_obj)
                for (s, n) in o.incs:
                    ins.then_inc(sems[s], n)

        @block.tensor
        def _(e):
            replay(e, "t")

        @block.vector
        def _(e):
            replay(e, "v")

        @block.scalar
        def _(e):
            replay(e, "s")

        @block.gpsimd
        def _(e):
            replay(e, "g")

        @block.sync
        def _(e):
            replay(e, "y")

    return nc


def _prep_inputs(inputs):
    f = np.ascontiguousarray
    x = inputs["L0_attn_hidden_states"]
    Wq = inputs["L0_attn_Wq"]
    bq = inputs["L0_attn_bq"]
    Wo = inputs["L0_attn_Wo"]
    M = inputs["L0_attn_M"]
    z = inputs["L0_attn_z"]
    Wb = inputs["L0_ffn_W_base"]
    bb = inputs["L0_ffn_b_base"]
    Wr = inputs["L0_ffn_W_router"]
    A = inputs["L0_ffn_A"]
    B = inputs["L0_ffn_B"]

    xs = np.asarray(x, np.float32).reshape(16384, D)
    wq_sel = np.asarray(Wq, np.float32).reshape(32, 128, D)[::4].reshape(1024, D)
    bq_sel = np.asarray(bq, np.float32).reshape(32, 128)[::4].reshape(8, 128)
    common = {
        "wbT": f(np.asarray(Wb, np.float32).T).reshape(KC, 128, D),
        "wqT": f(wq_sel.T).reshape(KC, 128, 1024),
        "woT": f(np.asarray(Wo, np.float32).T).reshape(KC, 128, D),
        "aT": f(np.asarray(A, np.float32).reshape(128, D).T).reshape(KC, 128, 128),
        "bfT": f(np.asarray(B, np.float32).transpose(0, 2, 1).reshape(128, D)),
        "wrT": f(np.asarray(Wr, np.float32).T).reshape(KC, 128, 8),
        "bb": f(np.asarray(bb, np.float32).reshape(1, D)),
        "bq": f(bq_sel.T),
        "M": f(np.asarray(M, np.float32)),
        "zT": f(np.asarray(z, np.float32).T),
        "ident": np.eye(128, dtype=ml_dtypes.bfloat16),
        "ident2": (2 * np.eye(128)).astype(ml_dtypes.bfloat16),
        "ones_row": np.ones((1, 128), ml_dtypes.bfloat16),
    }
    shards = []
    for c in range(8):
        m = dict(common)
        m["x0"] = f(xs[c * 2048:(c + 1) * 2048].reshape(NT, 128, D))
        shards.append(m)
    return shards


def kernel(**inputs):
    shards = _prep_inputs(inputs)
    nc = build_program(phases=3)
    res = run_bass_kernel_spmd(nc, shards, core_ids=list(range(8)))
    out = np.concatenate(
        [np.asarray(r["out"]).reshape(2048, D) for r in res.results], axis=0)
    return np.ascontiguousarray(out.reshape(4, 4096, 4096)).astype(np.float32)


# version 11 (final)
# speedup vs baseline: 1.1203x; 1.0396x over previous
import numpy as np
import ml_dtypes
import concourse.bass as bass
import concourse.mybir as mybir
from concourse.bass_utils import run_bass_kernel_spmd

F32 = mybir.dt.float32
BF16 = mybir.dt.bfloat16
AF = mybir.ActivationFunctionType
ALU = mybir.AluOpType

NT = 16
D = 4096
KC = 32
NOC = 8
RT = 8
NH = 8
HD = 128
EPS_LN = 1e-5
EPS_ATT = 1e-6

STREAMS = ("t", "v", "s", "g", "y")
DMA_SEMS = ("xb0", "xb1", "wb0", "wb1", "yb0", "yb1", "xf0", "xf1",
            "wo0", "wo1", "gsm", "ysm", "as0", "as1", "as2", "wbs",
            "wf0", "wf1")
ENGS = ("t", "v", "s") + DMA_SEMS
STREAM_OF = {"t": "t", "v": "v", "s": "s", "g": "g", "y": "y", "a": "s"}


class Op:
    __slots__ = ("fn", "incs", "is_wait")
    def __init__(self, fn, is_wait=False):
        self.fn = fn
        self.incs = []
        self.is_wait = is_wait


class Em:
    def __init__(self):
        self.q = {e: [] for e in STREAMS}
        self.cnt = {e: 0 for e in ENGS}
        self.lastwait = {e: {s: 0 for s in ENGS} for e in STREAMS}

    def op(self, eng, fn, inc=False):
        o = Op(fn)
        self.q[STREAM_OF[eng]].append(o)
        if inc:
            self._attach(eng, o)
        return self.cnt[eng]

    def _attach(self, sem, o):
        n = 16 if sem in DMA_SEMS else 1
        o.incs.append((sem, n))
        self.cnt[sem] += n

    def dma(self, queue, sem, fn):
        o = Op(fn)
        self.q[STREAM_OF[queue]].append(o)
        self._attach(sem, o)
        return self.cnt[sem]

    def wait(self, eng, sem, val=None):
        st = STREAM_OF[eng]
        if val is None:
            val = self.cnt[sem]
        if val <= 0 or val <= self.lastwait[st][sem]:
            return
        self.lastwait[st][sem] = val
        def f(e, _s=sem, _v=val):
            return e.wait_ge(e._sems[_s], _v)
        self.q[st].append(Op(f, is_wait=True))

    def mark(self, eng):
        st = STREAM_OF[eng]
        for o in reversed(self.q[st]):
            if o.is_wait:
                continue
            if not o.incs:
                self._attach(eng, o)
            return self.cnt[eng]
        return self.cnt[eng]

    def barrier(self, engs=("t", "v", "s")):
        counts = {e: self.mark(e) for e in engs}
        for e in engs:
            for other in engs:
                if other != e:
                    self.wait(e, other, counts[other])


class Bank:
    __slots__ = ("fill", "cons")
    def __init__(self):
        self.fill = 0
        self.cons = []


def build_program(phases=3):
    nc = bass.Bass("TRN2", target_bir_lowering=False, detect_race_conditions=False)

    xd = nc.dram_tensor("x0", [NT, 128, D], F32, kind="ExternalInput").ap()
    wbT = nc.dram_tensor("wbT", [KC, 128, D], F32, kind="ExternalInput").ap()
    wqT = nc.dram_tensor("wqT", [KC, 128, NH * HD], F32, kind="ExternalInput").ap()
    woT = nc.dram_tensor("woT", [KC, 128, D], F32, kind="ExternalInput").ap()
    aT = nc.dram_tensor("aT", [KC, 128, 128], F32, kind="ExternalInput").ap()
    bfT = nc.dram_tensor("bfT", [128, D], F32, kind="ExternalInput").ap()
    wrT = nc.dram_tensor("wrT", [KC, 128, NH], F32, kind="ExternalInput").ap()
    bbd = nc.dram_tensor("bb", [1, D], F32, kind="ExternalInput").ap()
    bqd = nc.dram_tensor("bq", [128, NH], F32, kind="ExternalInput").ap()
    Md = nc.dram_tensor("M", [NH, 128, HD], F32, kind="ExternalInput").ap()
    zTd = nc.dram_tensor("zT", [128, NH], F32, kind="ExternalInput").ap()
    identd = nc.dram_tensor("ident", [128, 128], BF16, kind="ExternalInput").ap()
    ident2d = nc.dram_tensor("ident2", [128, 128], BF16, kind="ExternalInput").ap()
    onesd = nc.dram_tensor("ones_row", [1, 128], BF16, kind="ExternalInput").ap()

    outd = nc.dram_tensor("out", [NT, 128, D], F32, kind="ExternalOutput").ap()
    x1d = nc.dram_tensor("x1s", [NT, 128, D], BF16, kind="Internal").ap()
    x2d = nc.dram_tensor("x2s", [NT, 128, D], BF16, kind="Internal").ap()
    wofd = nc.dram_tensor("wofd", [NH, 128, D], BF16, kind="Internal").ap()
    wbbf = nc.dram_tensor("wbbf", [KC, 128, D], BF16, kind="Internal").ap()

    em = Em()
    banks = [Bank() for _ in range(8)]

    from contextlib import ExitStack
    with ExitStack() as _st:
        def SB(name, shape, dt):
            return _st.enter_context(nc.sbuf_tensor(name, shape, dt))
        ident_sb = SB("ident_sb", [128, 128], BF16)
        ident2_sb = SB("ident2_sb", [128, 128], BF16)
        ones_sb = SB("ones_sb", [1, 128], BF16)
        ones16 = SB("ones16", [128, 16], F32)
        epsln = SB("epsln", [128, 1], F32)
        epsatt = SB("epsatt", [128, 1], F32)
        awrT = SB("awrT", [128, KC, 136], BF16)
        bfT_sb = SB("bfT_sb", [128, D], BF16)
        mz = SB("mz", [128, NH, 129], BF16)
        bbrep = SB("bbrep", [128, D], BF16)
        bq_sb = SB("bq_sb", [128, NH], F32)
        yT = SB("yT", [128, KC, 1024], BF16)
        wbig = SB("wbig", [128, KC, 1024], BF16)
        xb = SB("xb", [128, 2, D], BF16)
        ybf = SB("ybf", [128, D], BF16)
        gT = SB("gT", [128, 1024], BF16)
        wexp = SB("wexp", [128, 2, 128], BF16)
        stats = SB("stats", [128, RT, 8], F32)
        ltok = SB("ltok", [128, RT, 8], F32)
        mx8 = SB("mx8", [128, RT, 8], F32)
        tp2 = SB("tp2", [128, RT, 4], F32)
        msk = SB("msk", [128, RT, 8], F32)
        e1b = SB("e1b", [128, RT, 8], F32)
        scr = SB("scr", [128, 8192], BF16)
        outb = SB("outb", [128, 3, 512], F32)
        retrst = SB("retrst", [128, NH, 128], BF16)
        PS = _st.enter_context(nc.psum_tensor("PS", [128, 8, 512], F32))
        _semnames = ("t", "v", "s") + DMA_SEMS
        sems = {nm: _st.enter_context(nc.semaphore("sem_" + nm))
                for nm in _semnames}
        block = _st.enter_context(nc.Block())

        outb_bf = [outb[:, i, :].bitcast(BF16)[:, 0:512] for i in range(3)]
        sigq = scr[:, :].rearrange("p (h t) -> p h t", h=NH)
        xbfc = [scr[:, i * 4096:(i + 1) * 4096].rearrange("p (a b) -> p a b", a=8)
                for i in range(2)]
        dumpA = scr[:, 0:4096]
        ybf_odd = scr[:, 4096:8192]
        sqdump = ybf[:, 0:2048]
        spart = ybf[:, 2048:4096].bitcast(F32).rearrange(
            "p (r t c) -> p (r t) c", r=2, t=RT)
        hT = scr[:, 4096:5120]
        wexpT = scr[:, 5120:6144]
        ltT = scr[0:8, 6144:7168]
        retrT = yT[:, 0:NH, :]
        wof_ch = [xb[:, i, :].rearrange("p (a b) -> p a b", a=8) for i in range(2)]
        wb_ch = [wbig[:, :, i * 512:(i + 1) * 512] for i in range(2)]

        def t_op(fn, inc=False):
            return em.op("t", fn, inc)

        def v_op(fn, inc=False):
            return em.op("v", fn, inc)

        def s_op(fn, inc=False):
            return em.op("s", fn, inc)

        def bank_fill_prep(bs):
            for b in bs:
                for (eng, val) in banks[b].cons:
                    em.wait("t", eng, val)

        def bank_filled(bs):
            c = em.mark("t")
            for b in bs:
                banks[b].fill = c
                banks[b].cons = []

        def consume_bank(eng, b):
            em.wait(eng, "t", banks[b].fill)

        def bank_consumed(bs, eng):
            c = em.mark(eng)
            for b in bs:
                banks[b].cons.append((eng, c))

        slot_state = {"i": 0, "free": [0, 0, 0]}
        slot_free_s = [0, 0, 0]

        def next_slot():
            sl = slot_state["i"] % 3
            slot_state["i"] += 1
            return sl

        em.dma("g", "gsm", lambda e: e.dma_start(out=awrT[:, :, 0:128],
                                          in_=aT.rearrange("k p o -> p k o")))
        em.dma("g", "gsm", lambda e: e.dma_start(out=awrT[:, :, 128:136],
                                          in_=wrT.rearrange("k p o -> p k o")))
        em.dma("g", "gsm", lambda e: e.dma_start(out=bfT_sb[:, :], in_=bfT[:, :]))
        em.dma("g", "gsm", lambda e: e.dma_start(out=mz[:, :, 0:128],
                                          in_=Md.rearrange("h p o -> p h o"),
                                          allow_slow_non_contiguous=True))
        em.dma("g", "gsm", lambda e: e.dma_start(out=mz[:, :, 128:129],
                                          in_=zTd.rearrange("p (h o) -> p h o", o=1),
                                          allow_slow_non_contiguous=True))
        em.dma("y", "ysm", lambda e: e.dma_start(out=ident_sb[:, :], in_=identd[:, :]))
        em.dma("y", "ysm", lambda e: e.dma_start(out=ident2_sb[:, :], in_=ident2d[:, :]))
        em.dma("y", "ysm", lambda e: e.dma_start(out=ones_sb[:, :], in_=onesd[:, :]))
        em.dma("y", "ysm", lambda e: e.dma_start(out=bq_sb[:, :], in_=bqd[:, :]))
        em.dma("g", "gsm", lambda e: e.dma_start(out=bbrep[0:1, :], in_=bbd[:, :]))
        gc0, yc0 = em.cnt["gsm"], em.cnt["ysm"]

        v_op(lambda e: e.memset(ones16[:, :], 1.0))
        v_op(lambda e: e.memset(epsln[:, :], EPS_LN))
        v_op(lambda e: e.memset(epsatt[:, :], EPS_ATT), inc=True)

        em.wait("t", "ysm", yc0)
        em.wait("t", "gsm", gc0)
        for oc in range(NOC):
            b = oc % 2
            bank_fill_prep([b])
            def f(e, _b=b, _oc=oc):
                return e.matmul(PS[:, _b, :], ones_sb[:, :],
                         bbrep[0:1, _oc * 512:(_oc + 1) * 512],
                         start=True, stop=True)
            t_op(f, inc=True)
            bank_filled([b])
            consume_bank("v", b)
            def fv(e, _b=b, _oc=oc):
                return e.tensor_scalar(bbrep[:, _oc * 512:(_oc + 1) * 512], PS[:, _b, :],
                                0.5, None, ALU.mult)
            v_op(fv, inc=True)
            bank_consumed([b], "v")

        fold_state = {"v": 0, "wf0": 0, "wf1": 0}
        fold_done_v = {}

        def wo_stage(k4):
            return wbig[:, 4 * k4:4 * k4 + 4, :].bitcast(F32)

        def _mk_fold_unit(u):
            h, half = u % NH, u // NH
            p = u % 2
            base = 0 if p == 0 else 4
            srcs = [wo_stage(base + g) for g in range(4)]
            dsem = "ysm"
            fsem = "wf%d" % p
            def emit():
                if u >= 2:
                    em.wait("y", "v", fold_done_v[u - 2])
                    em.wait("y", fsem, fold_state[fsem])
                last = 0
                for g4 in range(4):
                    def fdl(e, _h=h, _g=g4, _half=half, _dst=srcs[g4]):
                        return e.dma_start(
                            out=_dst,
                            in_=woT[4 * _h + _g, :,
                                    _half * 2048:(_half + 1) * 2048]
                            .rearrange("p (a b) -> p a b", a=4))
                    last = em.dma("y", dsem, fdl)
                em.wait("v", dsem, last)
                v_op(lambda e, _a=srcs[0], _b=srcs[1], _o=srcs[0]:
                     e.tensor_add(_o, _a, _b))
                v_op(lambda e, _a=srcs[2], _b=srcs[3], _o=srcs[2]:
                     e.tensor_add(_o, _a, _b))
                v_op(lambda e, _a=srcs[0], _b=srcs[2], _o=srcs[0]:
                     e.tensor_add(_o, _a, _b), inc=True)
                fold_done_v[u] = em.cnt["v"]
                em.wait("g", "v", fold_done_v[u])
                def fst(e, _h=h, _half=half, _src=srcs[0]):
                    return e.dma_start(
                        out=wofd[_h, :, _half * 2048:(_half + 1) * 2048]
                        .rearrange("p (a b) -> p a b", a=4), in_=_src)
                em.dma("g", fsem, fst)
                fold_state[fsem] = em.cnt[fsem]
            return emit

        fold_units = [_mk_fold_unit(u) for u in range(2 * NH)]
        fold_units.reverse()

        def store_totals():
            return {k: em.cnt[k] for k in ("as0", "as1", "as2")}

        def wait_stores(eng, tot):
            for k, v in tot.items():
                em.wait(eng, k, v)

        g_lastnorm = {0: 0, 1: 0}

        def emit_ffn_tail():
            bank_fill_prep([0, 1, 2, 3])
            for kc in range(KC):
                for tc in range(2):
                    def fh(e, _kc=kc, _tc=tc):
                        return e.matmul(PS[:, _tc, :], awrT[:, _kc, 0:128],
                                 yT[:, _kc, _tc * 512:(_tc + 1) * 512],
                                 start=(_kc == 0), stop=(_kc == KC - 1))
                    t_op(fh)
                    def fr(e, _kc=kc, _tc=tc):
                        return e.matmul(PS[0:8, 2 + _tc, :], awrT[:, _kc, 128:136],
                                 yT[:, _kc, _tc * 512:(_tc + 1) * 512],
                                 start=(_kc == 0), stop=(_kc == KC - 1))
                    t_op(fr, inc=(kc == KC - 1 and tc == 1))
            bank_filled([0, 1, 2, 3])
            for b in (0, 1, 2, 3):
                consume_bank("v", b)
            v_op(lambda e: e.tensor_copy(
                hT[:, :], PS[:, 0:2, :].rearrange("p a b -> p (a b)")))
            v_op(lambda e: e.tensor_copy(
                ltT[:, :], PS[0:8, 2:4, :].rearrange("p a b -> p (a b)")),
                inc=True)
            bank_consumed([0, 1, 2, 3], "v")

            em.wait("t", "v", em.cnt["v"])
            bank_fill_prep([4])
            for j in range(RT):
                def fl(e, _j=j):
                    return e.matmul(PS[:, 4, _j * 8:(_j + 1) * 8],
                             ltT[:, _j * 128:(_j + 1) * 128],
                             ident_sb[0:8, 0:8], start=True, stop=True)
                t_op(fl, inc=(j == RT - 1))
            bank_filled([4])
            consume_bank("v", 4)
            v_op(lambda e: e.tensor_copy(
                ltok[:, :, :],
                PS[:, 4, 0:64].rearrange("p (a b) -> p a b", a=RT)), inc=True)
            bank_consumed([4], "v")

            for j in range(RT):
                v_op(lambda e, _j=j: e.max(out=mx8[:, _j, :], in_=ltok[:, _j, :]))
            for j in range(RT):
                v_op(lambda e, _j=j: e.tensor_scalar(
                    e1b[:, _j, :], ltok[:, _j, :], mx8[:, _j, 0:1], None,
                    ALU.subtract))
            for j in range(RT):
                v_op(lambda e, _j=j: e.tensor_scalar(
                    msk[:, _j, :], ltok[:, _j, :], mx8[:, _j, 1:2], None,
                    ALU.is_ge))
            for j in range(RT):
                v_op(lambda e, _j=j: e.tensor_scalar(
                    tp2[:, _j, 0:1], mx8[:, _j, 1:2], mx8[:, _j, 0:1], None,
                    ALU.subtract), inc=(j == RT - 1))
            em.wait("s", "v", em.cnt["v"])
            for j in range(RT):
                s_op(lambda e, _j=j: e.activation(e1b[:, _j, :], e1b[:, _j, :],
                                                  AF.Exp))
            for j in range(RT):
                s_op(lambda e, _j=j: e.activation(tp2[:, _j, 1:2],
                                                  tp2[:, _j, 0:1], AF.Exp),
                     inc=(j == RT - 1))
            em.wait("v", "s", em.cnt["s"])
            for j in range(RT):
                v_op(lambda e, _j=j: e.tensor_scalar(
                    tp2[:, _j, 2:3], tp2[:, _j, 1:2], 1.0, None, ALU.add))
            for j in range(RT):
                v_op(lambda e, _j=j: e.reciprocal(tp2[:, _j, 3:4], tp2[:, _j, 2:3]))
            for j in range(RT):
                v_op(lambda e, _j=j: e.tensor_mul(e1b[:, _j, :], e1b[:, _j, :],
                                                  msk[:, _j, :]))
            for j in range(RT):
                v_op(lambda e, _j=j: e.tensor_scalar(
                    e1b[:, _j, :], e1b[:, _j, :], tp2[:, _j, 3:4], 2.0,
                    ALU.mult, ALU.mult))
            for j in range(RT):
                wb_ = j % 2
                for ex in range(8):
                    v_op(lambda e, _j=j, _w=wb_, _ex=ex: e.tensor_scalar(
                        wexp[:, _w, _ex * 16:(_ex + 1) * 16], ones16[:, :],
                        e1b[:, _j, _ex:_ex + 1], None, ALU.mult),
                        inc=(ex == 7))
                em.wait("t", "v", em.cnt["v"])
                b = 5 + (j % 2)
                bank_fill_prep([b])
                def ft(e, _w=wb_, _b=b):
                    return e.matmul(PS[:, _b, 0:128], wexp[:, _w, :], ident_sb[:, :],
                             start=True, stop=True)
                t_op(ft, inc=True)
                bank_filled([b])
                consume_bank("v", b)
                v_op(lambda e, _j=j, _b=b: e.tensor_copy(
                    wexpT[:, _j * 128:(_j + 1) * 128], PS[:, _b, 0:128]),
                    inc=True)
                bank_consumed([b], "v")
            v_op(lambda e: e.tensor_mul(gT[:, :], hT[:, :], wexpT[:, :]),
                 inc=True)

        def stage_A(rnd, src, ffn, fold_hook=False):
            entry_t = em.cnt["t"]
            entry_st = store_totals()
            norm_cnt = {}
            tr_done = {}
            dma_cnt = {}
            sq_done = {}

            def emit_dma(i):
                tt = rnd * RT + i
                em.wait("g", "t", entry_t)
                wait_stores("g", entry_st)
                if i - 2 >= 0:
                    em.wait("g", "s", sq_done[i - 2])
                    em.wait("g", "v", norm_cnt[i - 2])
                sem = "xb%d" % (i % 2)
                def fx(e, _tt=tt, _bi=i % 2):
                    return e.dma_start(out=xb[:, _bi, :], in_=src[_tt, :, :])
                dma_cnt[i] = (sem, em.dma("g", sem, fx))

            emit_dma(0)
            emit_dma(1)
            for i in range(RT):
                em.wait("s", dma_cnt[i][0], dma_cnt[i][1])
                s_op(lambda e, _bi=i % 2, _i=i: e.activation(
                    dumpA[:, :], xb[:, _bi, :], AF.Copy,
                    accum_out=stats[:, _i, 0:1]))
                s_op(lambda e, _bi=i % 2, _i=i: e.activation(
                    dumpA[:, :], xb[:, _bi, :], AF.Square,
                    accum_out=stats[:, _i, 1:2]), inc=True)
                sq_done[i] = em.cnt["s"]
                s_op(lambda e, _i=i: e.activation(stats[:, _i, 2:3],
                                                  stats[:, _i, 0:1],
                                                  AF.Copy, scale=1.0 / D),
                     inc=True)
                em.wait("v", "s", em.cnt["s"])
                v_op(lambda e, _i=i: e.tensor_scalar(
                    stats[:, _i, 3:4], stats[:, _i, 2:3], stats[:, _i, 2:3],
                    -1.0, ALU.mult, ALU.mult), inc=True)
                em.wait("s", "v", em.cnt["v"])
                s_op(lambda e, _i=i: e.activation(stats[:, _i, 4:5],
                                                  stats[:, _i, 1:2], AF.Identity,
                                                  bias=stats[:, _i, 3:4],
                                                  scale=1.0 / D))
                s_op(lambda e: e.drain())
                s_op(lambda e, _i=i: e.activation(stats[:, _i, 5:6],
                                                  stats[:, _i, 4:5], AF.Ln,
                                                  bias=epsln[:, :]))
                s_op(lambda e: e.drain())
                s_op(lambda e, _i=i: e.activation(stats[:, _i, 6:7],
                                                  stats[:, _i, 5:6], AF.Exp,
                                                  scale=-0.5), inc=True)
                chain_done_i = em.cnt["s"]

                em.wait("v", "s", chain_done_i)
                if i >= 1:
                    em.wait("v", "t", tr_done[i - 1])
                yb = ybf_odd
                def fn(e, _bi=i % 2, _i=i, _yb=yb):
                    return e.tensor_scalar(_yb[:, :], xb[:, _bi, :],
                                    stats[:, _i, 2:3], stats[:, _i, 6:7],
                                    ALU.subtract, ALU.mult)
                v_op(fn, inc=True)
                norm_cnt[i] = em.cnt["v"]
                g_lastnorm[i % 2] = em.cnt["v"]

                em.wait("t", "v", norm_cnt[i])
                for half in range(2):
                    bs = [4 * half + k for k in range(4)]
                    bank_fill_prep(bs)
                    for k in range(16):
                        kc = 16 * half + k
                        b = bs[k // 4]
                        off = (k % 4) * 128
                        def fm(e, _kc=kc, _b=b, _off=off, _yb=yb):
                            return e.matmul(PS[:, _b, _off:_off + 128],
                                     _yb[:, _kc * 128:(_kc + 1) * 128],
                                     ident_sb[:, :], start=True, stop=True)
                        t_op(fm, inc=(k == 15))
                    bank_filled(bs)
                    for b in bs:
                        consume_bank("v", b)
                    kc0 = 16 * half
                    def fe(e, _kc0=kc0, _i=i, _h=half):
                        return e.tensor_copy(
                            yT[:, _kc0:_kc0 + 16, _i * 128:(_i + 1) * 128],
                            PS[:, _h * 4:_h * 4 + 4, :]
                            .rearrange("p a (c d) -> p (a c) d", d=128))
                    v_op(fe, inc=True)
                    bank_consumed(bs, "v")
                tr_done[i] = em.cnt["t"]

                if i + 2 < RT:
                    emit_dma(i + 2)
                if fold_hook:
                    em.wait("y", "t", entry_t)
                    for _ in range(2):
                        if fold_units:
                            fold_units.pop()()

            if not ffn:
                return

            emit_ffn_tail()

        def stage_A_fast(rnd, src, ffn, sq_prod_s):
            entry_t = em.cnt["t"]
            entry_st = store_totals()
            for i in range(RT):
                ps_ = (rnd % 2) * RT + i
                v_op(lambda e, _p=ps_, _i=i: e.tensor_scalar(
                    mx8[:, _i, :], spart[:, _p, 0:8], 1.0, None,
                    ALU.mult, ALU.add, accum_out=stats[:, _i, 0:1]),
                    inc=(i == RT - 1))
            em.wait("s", "v", em.cnt["v"])
            for i in range(RT):
                s_op(lambda e, _i=i: e.activation(stats[:, _i, 2:3],
                                                  stats[:, _i, 0:1],
                                                  AF.Copy, scale=1.0 / D),
                     inc=(i == RT - 1))
            em.wait("v", "s", em.cnt["s"])
            for i in range(RT):
                v_op(lambda e, _i=i: e.tensor_scalar(
                    stats[:, _i, 3:4], stats[:, _i, 2:3], stats[:, _i, 2:3],
                    -1.0, ALU.mult, ALU.mult), inc=(i == RT - 1))
            negmsq_v = em.cnt["v"]

            dma_cnt = {}
            tr_done = {}

            def emit_dma(i):
                tt = rnd * RT + i
                em.wait("g", "t", entry_t)
                wait_stores("g", entry_st)
                em.wait("g", "v", g_lastnorm[i % 2])
                sem = "xb%d" % (i % 2)
                def fx(e, _tt=tt, _bi=i % 2):
                    return e.dma_start(out=xb[:, _bi, :], in_=src[_tt, :, :])
                dma_cnt[i] = (sem, em.dma("g", sem, fx))

            emit_dma(0)
            emit_dma(1)
            em.wait("s", "v", negmsq_v)
            chain_done = {}
            for i in range(RT):
                em.wait("s", dma_cnt[i][0], dma_cnt[i][1])
                s_op(lambda e, _bi=i % 2, _i=i: e.activation(
                    sqdump[:, 0:2048], xb[:, _bi, 0:2048], AF.Square,
                    accum_out=stats[:, _i, 1:2]))
                s_op(lambda e, _bi=i % 2, _i=i: e.activation(
                    sqdump[:, 0:2048], xb[:, _bi, 2048:4096], AF.Square,
                    accum_out=stats[:, _i, 7:8]))
                s_op(lambda e, _i=i: e.activation(
                    stats[:, _i, 4:5], stats[:, _i, 1:2], AF.Identity,
                    bias=stats[:, _i, 3:4], scale=1.0 / D))
                s_op(lambda e: e.drain())
                s_op(lambda e, _i=i: e.activation(
                    stats[:, _i, 4:5], stats[:, _i, 7:8], AF.Identity,
                    bias=stats[:, _i, 4:5], scale=1.0 / D))
                s_op(lambda e: e.drain())
                s_op(lambda e, _i=i: e.activation(stats[:, _i, 5:6],
                                                  stats[:, _i, 4:5], AF.Ln,
                                                  bias=epsln[:, :]))
                s_op(lambda e: e.drain())
                s_op(lambda e, _i=i: e.activation(stats[:, _i, 6:7],
                                                  stats[:, _i, 5:6], AF.Exp,
                                                  scale=-0.5), inc=True)
                chain_done[i] = em.cnt["s"]
                em.wait("v", "s", chain_done[i])
                em.wait("v", dma_cnt[i][0], dma_cnt[i][1])
                if i >= 1:
                    em.wait("v", "t", tr_done[i - 1])
                def fn(e, _bi=i % 2, _i=i):
                    return e.tensor_scalar(ybf_odd[:, :], xb[:, _bi, :],
                                    stats[:, _i, 2:3], stats[:, _i, 6:7],
                                    ALU.subtract, ALU.mult)
                v_op(fn, inc=True)
                g_lastnorm[i % 2] = em.cnt["v"]
                em.wait("t", "v", em.cnt["v"])
                for half in range(2):
                    bs = [4 * half + k for k in range(4)]
                    bank_fill_prep(bs)
                    for k in range(16):
                        kc = 16 * half + k
                        b = bs[k // 4]
                        off = (k % 4) * 128
                        def fm(e, _kc=kc, _b=b, _off=off):
                            return e.matmul(PS[:, _b, _off:_off + 128],
                                     ybf_odd[:, _kc * 128:(_kc + 1) * 128],
                                     ident_sb[:, :], start=True, stop=True)
                        t_op(fm, inc=(k == 15))
                    bank_filled(bs)
                    for b in bs:
                        consume_bank("v", b)
                    kc0 = 16 * half
                    def fe(e, _kc0=kc0, _i=i, _h=half):
                        return e.tensor_copy(
                            yT[:, _kc0:_kc0 + 16, _i * 128:(_i + 1) * 128],
                            PS[:, _h * 4:_h * 4 + 4, :]
                            .rearrange("p a (c d) -> p (a c) d", d=128))
                    v_op(fe, inc=True)
                    bank_consumed(bs, "v")
                tr_done[i] = em.cnt["t"]
                if i + 2 < RT:
                    emit_dma(i + 2)

            if not ffn:
                return
            emit_ffn_tail()

        def ffn_B(rnd, src, dst, wsrc="bf16", save=False, feed_stats=False):
            gT_ready_v = em.cnt["v"]
            entry_t = em.cnt["t"]
            entry_s = em.cnt["s"]
            entry_st = store_totals()
            wb_g = {}
            wb_free_t = {}
            xb_y = {}
            xbf_free_t = {}
            xbf_free_v = {}

            def issue_w(oc):
                buf = oc % 2
                q = "g" if wsrc == "f32" else "y"
                em.wait(q, "t", wb_free_t.get(oc - 2, entry_t))
                em.wait(q, "wf0", fold_state["wf0"])
                em.wait(q, "wf1", fold_state["wf1"])
                em.wait(q, "v", fold_done_v.get(2 * NH - 1, 0))

                if wsrc == "bf16":
                    em.wait(q, "wbs", wbbf_saved[0])
                sem = ("wb%d" if wsrc == "f32" else "yb%d") % buf
                srcw = wbT if wsrc == "f32" else wbbf
                last = 0
                for part in range(4):
                    k0 = part * 8
                    def fw(e, _oc=oc, _buf=buf, _k0=k0, _srcw=srcw):
                        return e.dma_start(
                            out=wb_ch[_buf][:, _k0:_k0 + 8, :],
                            in_=_srcw.rearrange("k p o -> p k o")
                            [:, _k0:_k0 + 8, _oc * 512:(_oc + 1) * 512])
                    last = em.dma(q, sem, fw)
                wb_g[oc] = (sem, last)
                if save:
                    em.wait("a", sem, last)
                    for part in range(2):
                        k0 = part * 16
                        def fsv(e, _oc=oc, _buf=buf, _k0=k0):
                            return e.dma_start(
                                out=wbbf.rearrange("k p o -> p k o")
                                [:, _k0:_k0 + 16, _oc * 512:(_oc + 1) * 512],
                                in_=wb_ch[_buf][:, _k0:_k0 + 16, :])
                        em.dma("a", "wbs", fsv)

            def issue_x(oc):
                buf = oc % 2
                em.wait("g", "t", xbf_free_t.get(oc - 2, entry_t))
                em.wait("g", "v", xbf_free_v.get(oc - 2, gT_ready_v))
                em.wait("g", "s", entry_s)
                wait_stores("g", entry_st)
                sem = "xf%d" % buf
                last = 0
                for part in range(2):
                    a0 = part * 4
                    def fx(e, _oc=oc, _buf=buf, _a0=a0):
                        return e.dma_start(
                            out=xbfc[_buf][:, _a0:_a0 + 4, :],
                            in_=src[rnd * 8 + _a0:rnd * 8 + _a0 + 4, :,
                                    _oc * 512:(_oc + 1) * 512]
                            .rearrange("a p b -> p a b"))
                    last = em.dma("g", sem, fx)
                xb_y[oc] = (sem, last)

            issue_w(0)
            issue_x(0)
            issue_w(1)
            issue_x(1)

            for oc in range(NOC):
                buf = oc % 2
                em.wait("t", wb_g[oc][0], wb_g[oc][1])
                em.wait("t", xb_y[oc][0], xb_y[oc][1])
                em.wait("t", "v", gT_ready_v)
                for th in range(2):
                    bs = [4 * th + k for k in range(4)]
                    bank_fill_prep(bs)
                    for kc in range(KC):
                        for k in range(4):
                            tt = 4 * th + k
                            def fm(e, _kc=kc, _b=bs[k], _buf=buf, _tt=tt):
                                return e.matmul(PS[:, _b, :],
                                         yT[:, _kc, _tt * 128:(_tt + 1) * 128],
                                         wb_ch[_buf][:, _kc, :],
                                         start=(_kc == 0), stop=False)
                            t_op(fm)
                    for k in range(4):
                        tt = 4 * th + k
                        def fl(e, _b=bs[k], _tt=tt, _oc=oc):
                            return e.matmul(PS[:, _b, :],
                                     gT[:, _tt * 128:(_tt + 1) * 128],
                                     bfT_sb[:, _oc * 512:(_oc + 1) * 512],
                                     start=False, stop=True)
                        t_op(fl, inc=(k == 3))
                    bank_filled(bs)
                    for k in range(4):
                        tt = 4 * th + k
                        b = bs[k]
                        sl = next_slot()
                        bf = (dst.dtype == BF16)
                        tgt = outb_bf[sl] if bf else outb[:, sl, :]
                        em.wait("v", "as%d" % sl, slot_state["free"][sl])
                        consume_bank("v", b)
                        pslot = (rnd % 2) * RT + tt
                        acc = (spart[:, pslot, oc:oc + 1] if feed_stats else None)
                        def f0(e, _b=b, _oc=oc):
                            return e.scalar_tensor_tensor(
                                PS[:, _b, :], PS[:, _b, :], 0.5,
                                bbrep[:, _oc * 512:(_oc + 1) * 512],
                                ALU.mult, ALU.add)
                        v_op(f0)
                        def fe(e, _b=b, _tgt=tgt, _buf=buf, _tt=tt, _acc=acc):
                            return e.scalar_tensor_tensor(
                                _tgt, PS[:, _b, :], 1.0,
                                xbfc[_buf][:, _tt, :],
                                ALU.mult, ALU.add, accum_out=_acc)
                        v_op(fe, inc=True)
                        bank_consumed([b], "v")
                        em.wait("a", "v", em.cnt["v"])
                        def fst(e, _tgt=tgt, _tt=rnd * 8 + tt, _oc=oc):
                            return e.dma_start(out=dst[_tt, :, _oc * 512:(_oc + 1) * 512],
                                        in_=_tgt)
                        slot_state["free"][sl] = em.dma("a", "as%d" % sl, fst)
                wb_free_t[oc] = em.mark("t")
                xbf_free_t[oc] = wb_free_t[oc]
                xbf_free_v[oc] = em.cnt["v"]
                if oc + 2 < NOC:
                    issue_w(oc + 2)
                    issue_x(oc + 2)


        def attn_B(rnd, src, dst, feed_stats=False):
            entry_t = em.cnt["t"]
            entry_s = em.cnt["s"]
            entry_v = em.cnt["v"]
            entry_st = store_totals()
            for sl in range(3):
                em.wait("v", "as%d" % sl, slot_state["free"][sl])
            for ocq in range(8):
                bs = [2 * (ocq % 2), 2 * (ocq % 2) + 1]
                bank_fill_prep(bs)
                for kc in range(KC):
                    for tc in range(2):
                        def fq(e, _kc=kc, _ocq=ocq, _tc=tc, _b=bs[tc]):
                            return e.matmul(PS[:, _b, :],
                                     wbig[:, _kc, _ocq * 128:(_ocq + 1) * 128],
                                     yT[:, _kc, _tc * 512:(_tc + 1) * 512],
                                     start=(_kc == 0), stop=(_kc == KC - 1))
                        t_op(fq, inc=(kc == KC - 1 and tc == 1))
                bank_filled(bs)
                for tc in range(2):
                    b = bs[tc]
                    consume_bank("v", b)
                    def f1(e, _b=b, _ocq=ocq):
                        return e.tensor_scalar(outb[:, 0, :], PS[:, _b, :],
                                        bq_sb[:, _ocq:_ocq + 1], 0.0,
                                        ALU.add, ALU.max)
                    v_op(f1)
                    def f2(e, _b=b, _ocq=ocq):
                        return e.tensor_scalar(outb[:, 1, :], PS[:, _b, :],
                                        bq_sb[:, _ocq:_ocq + 1], 0.0,
                                        ALU.add, ALU.min)
                    v_op(f2, inc=True)
                    bank_consumed([b], "v")
                    em.wait("s", "v", em.cnt["v"])
                    s_op(lambda e: e.activation(outb[:, 2, :], outb[:, 1, :],
                                                AF.Exp), inc=True)
                    em.wait("v", "s", em.cnt["s"])
                    def f3(e, _ocq=ocq, _tc=tc):
                        return e.tensor_add(sigq[:, _ocq, _tc * 512:(_tc + 1) * 512],
                                     outb[:, 0, :], outb[:, 2, :])
                    v_op(f3, inc=True)
            em.barrier()

            for h in range(NH):
                bank_fill_prep([0, 1, 2, 3])
                for j in range(RT):
                    def fr(e, _h=h, _j=j):
                        return e.matmul(PS[:, _j // 2,
                                    (_j % 2) * 256:(_j % 2) * 256 + 129],
                                 sigq[:, _h, _j * 128:(_j + 1) * 128],
                                 mz[:, _h, :], start=True, stop=True)
                    t_op(fr, inc=(j == RT - 1))
                bank_filled([0, 1, 2, 3])
                for b in (0, 1, 2, 3):
                    consume_bank("v", b)
                for j in range(RT):
                    def fd(e, _j=j):
                        return e.tensor_scalar(
                            tp2[:, _j, 0:1],
                            PS[:, _j // 2,
                               (_j % 2) * 256 + 128:(_j % 2) * 256 + 129],
                            epsatt[:, :], None, ALU.add)
                    v_op(fd)
                for j in range(RT):
                    v_op(lambda e, _j=j: e.reciprocal(tp2[:, _j, 1:2],
                                                      tp2[:, _j, 0:1]))
                for j in range(RT):
                    def fsc(e, _j=j):
                        return e.tensor_scalar(
                            retrst[:, _j, :],
                            PS[:, _j // 2, (_j % 2) * 256:(_j % 2) * 256 + 128],
                            tp2[:, _j, 1:2], None, ALU.mult)
                    v_op(fsc, inc=(j == RT - 1))
                bank_consumed([0, 1, 2, 3], "v")
                em.wait("t", "v", em.cnt["v"])
                bank_fill_prep([4, 5])
                for j in range(RT):
                    b = 4 + j // 4
                    off = (j % 4) * 128
                    def ftr(e, _j=j, _b=b, _off=off):
                        return e.matmul(PS[:, _b, _off:_off + 128], retrst[:, _j, :],
                                 ident_sb[:, :], start=True, stop=True)
                    t_op(ftr, inc=(j == RT - 1))
                bank_filled([4, 5])
                consume_bank("v", 4)
                consume_bank("v", 5)
                def fev(e, _h=h):
                    return e.tensor_copy(retrT[:, _h, :],
                                  PS[:, 4:6, :].rearrange("p a b -> p (a b)"))
                v_op(fev, inc=True)
                bank_consumed([4, 5], "v")
            em.barrier()

            bo_entry_t = em.cnt["t"]
            wo_y = {}
            wo_free_t = {}
            xb_y = {}
            xbf_free_t = {}
            xbf_free_v = {}

            def issue_wo(oc):
                buf = oc % 2
                em.wait("g", "t", wo_free_t.get(oc - 2, bo_entry_t))
                em.wait("g", "wf0", fold_state["wf0"])
                em.wait("g", "wf1", fold_state["wf1"])
                em.wait("g", "v", fold_state["v"])
                sem = "wo%d" % buf
                def fw(e, _oc=oc, _buf=buf):
                    return e.dma_start(out=wof_ch[_buf][:, :, :],
                                in_=wofd.rearrange("h p o -> p h o")
                                [:, :, _oc * 512:(_oc + 1) * 512])
                wo_y[oc] = (sem, em.dma("g", sem, fw))

            def issue_x(oc):
                buf = oc % 2
                em.wait("g", "t", xbf_free_t.get(oc - 2, bo_entry_t))
                em.wait("g", "v", xbf_free_v.get(oc - 2, entry_v))
                wait_stores("g", entry_st)
                sem = "xf%d" % buf
                last = 0
                for part in range(2):
                    a0 = part * 4
                    def fx(e, _oc=oc, _buf=buf, _a0=a0):
                        return e.dma_start(
                            out=xbfc[_buf][:, _a0:_a0 + 4, :],
                            in_=src[rnd * 8 + _a0:rnd * 8 + _a0 + 4, :,
                                    _oc * 512:(_oc + 1) * 512]
                            .rearrange("a p b -> p a b"))
                    last = em.dma("g", sem, fx)
                xb_y[oc] = (sem, last)

            issue_wo(0)
            issue_x(0)
            issue_wo(1)
            issue_x(1)

            for oc in range(NOC):
                buf = oc % 2
                em.wait("t", wo_y[oc][0], wo_y[oc][1])
                em.wait("t", xb_y[oc][0], xb_y[oc][1])
                for th in range(2):
                    bs = [4 * th + k for k in range(4)]
                    bank_fill_prep(bs)
                    for kc in range(NH):
                        for k in range(4):
                            tt = 4 * th + k
                            def fm(e, _kc=kc, _b=bs[k], _buf=buf, _tt=tt):
                                return e.matmul(PS[:, _b, :],
                                         retrT[:, _kc, _tt * 128:(_tt + 1) * 128],
                                         wof_ch[_buf][:, _kc, :],
                                         start=(_kc == 0), stop=(_kc == NH - 1))
                            t_op(fm, inc=(kc == NH - 1 and k == 3))
                    bank_filled(bs)
                    for k in range(4):
                        tt = 4 * th + k
                        b = bs[k]
                        sl = next_slot()
                        bf = (dst.dtype == BF16)
                        tgt = outb_bf[sl] if bf else outb[:, sl, :]
                        em.wait("v", "as%d" % sl, slot_state["free"][sl])
                        consume_bank("v", b)
                        pslot = (rnd % 2) * RT + tt
                        acc = spart[:, pslot, oc:oc + 1] if feed_stats else None
                        def fe(e, _b=b, _tgt=tgt, _buf=buf, _tt=tt, _acc=acc):
                            return e.scalar_tensor_tensor(
                                _tgt, PS[:, _b, :], 1.0,
                                xbfc[_buf][:, _tt, :],
                                ALU.mult, ALU.add, accum_out=_acc)
                        v_op(fe, inc=True)
                        bank_consumed([b], "v")
                        em.wait("a", "v", em.cnt["v"])
                        def fst(e, _tgt=tgt, _tt=rnd * 8 + tt, _oc=oc):
                            return e.dma_start(out=dst[_tt, :, _oc * 512:(_oc + 1) * 512],
                                        in_=_tgt)
                        slot_state["free"][sl] = em.dma("a", "as%d" % sl, fst)
                wo_free_t[oc] = em.mark("t")
                xbf_free_t[oc] = wo_free_t[oc]
                xbf_free_v[oc] = em.cnt["v"]
                if oc + 2 < NOC:
                    issue_wo(oc + 2)
                    issue_x(oc + 2)

        p1_dst = x1d if phases >= 2 else outd
        wbbf_saved = [0]
        for rnd in range(2):
            stage_A(rnd, xd, ffn=True, fold_hook=(rnd == 0))
            em.barrier()
            ffn_B(rnd, xd, p1_dst, wsrc=("f32" if rnd == 0 else "bf16"),
                  save=(rnd == 0), feed_stats=(phases >= 2))
            if rnd == 0:
                wbbf_saved[0] = em.cnt["wbs"]
            em.barrier()

        if phases >= 2:
            p2_dst = x2d if phases >= 3 else outd
            p1_end_t = em.cnt["t"]
            wq_last = 0
            for rnd in range(2):
                stage_A_fast(rnd, x1d, ffn=False, sq_prod_s=em.cnt["s"])
                if rnd == 0:
                    em.wait("g", "t", p1_end_t)
                    for part in range(8):
                        k0 = part * 4
                        def fwq(e, _k0=k0):
                            return e.dma_start(out=wbig[:, _k0:_k0 + 4, :],
                                        in_=wqT.rearrange("k p o -> p k o")
                                        [:, _k0:_k0 + 4, :])
                        wq_last = em.dma("g", "gsm", fwq)
                em.barrier()
                em.wait("t", "gsm", wq_last)
                attn_B(rnd, x1d, p2_dst, feed_stats=(phases >= 3))
                em.barrier()

        if phases >= 3:
            for rnd in range(2):
                stage_A_fast(rnd, x2d, ffn=True, sq_prod_s=em.cnt["s"])
                em.barrier()
                ffn_B(rnd, x2d, outd)
                em.barrier()

        for k in ("as0", "as1", "as2"):
            em.wait("y", k, em.cnt[k])

        def replay(eng_obj, key):
            eng_obj._sems = sems
            for o in em.q[key]:
                ins = o.fn(eng_obj)
                for (s, n) in o.incs:
                    ins.then_inc(sems[s], n)

        @block.tensor
        def _(e):
            replay(e, "t")

        @block.vector
        def _(e):
            replay(e, "v")

        @block.scalar
        def _(e):
            replay(e, "s")

        @block.gpsimd
        def _(e):
            replay(e, "g")

        @block.sync
        def _(e):
            replay(e, "y")

    return nc


def _prep_inputs(inputs):
    f = np.ascontiguousarray
    x = inputs["L0_attn_hidden_states"]
    Wq = inputs["L0_attn_Wq"]
    bq = inputs["L0_attn_bq"]
    Wo = inputs["L0_attn_Wo"]
    M = inputs["L0_attn_M"]
    z = inputs["L0_attn_z"]
    Wb = inputs["L0_ffn_W_base"]
    bb = inputs["L0_ffn_b_base"]
    Wr = inputs["L0_ffn_W_router"]
    A = inputs["L0_ffn_A"]
    B = inputs["L0_ffn_B"]

    xs = np.asarray(x, np.float32).reshape(16384, D)
    wq_sel = np.asarray(Wq, np.float32).reshape(32, 128, D)[::4].reshape(1024, D)
    bq_sel = np.asarray(bq, np.float32).reshape(32, 128)[::4].reshape(8, 128)
    common = {
        "wbT": f(np.asarray(Wb, np.float32).T).reshape(KC, 128, D),
        "wqT": f(wq_sel.T).reshape(KC, 128, 1024),
        "woT": f(np.asarray(Wo, np.float32).T).reshape(KC, 128, D),
        "aT": f(np.asarray(A, np.float32).reshape(128, D).T).reshape(KC, 128, 128),
        "bfT": f(np.asarray(B, np.float32).transpose(0, 2, 1).reshape(128, D)),
        "wrT": f(np.asarray(Wr, np.float32).T).reshape(KC, 128, 8),
        "bb": f(np.asarray(bb, np.float32).reshape(1, D)),
        "bq": f(bq_sel.T),
        "M": f(np.asarray(M, np.float32)),
        "zT": f(np.asarray(z, np.float32).T),
        "ident": np.eye(128, dtype=ml_dtypes.bfloat16),
        "ident2": (2 * np.eye(128)).astype(ml_dtypes.bfloat16),
        "ones_row": np.ones((1, 128), ml_dtypes.bfloat16),
    }
    shards = []
    for c in range(8):
        m = dict(common)
        m["x0"] = f(xs[c * 2048:(c + 1) * 2048].reshape(NT, 128, D))
        shards.append(m)
    return shards


def kernel(**inputs):
    shards = _prep_inputs(inputs)
    nc = build_program(phases=3)
    res = run_bass_kernel_spmd(nc, shards, core_ids=list(range(8)))
    out = np.concatenate(
        [np.asarray(r["out"]).reshape(2048, D) for r in res.results], axis=0)
    return np.ascontiguousarray(out.reshape(4, 4096, 4096)).astype(np.float32)
